# version 15
# baseline (speedup 1.0000x reference)
import numpy as np
import ml_dtypes
from contextlib import ExitStack

import concourse.bass as bass
import concourse.mybir as mybir
from concourse.bass_utils import run_bass_kernel_spmd

F32 = mybir.dt.float32
BF16 = mybir.dt.bfloat16
AF = mybir.ActivationFunctionType
ALU = mybir.AluOpType
AX = mybir.AxisListType

NCORES = 8
E = 2048
HD = 128
B = 2
S = 4096
T = B * S
FS = 256
TS = 1024
SCALE = 1.0 / float(np.sqrt(HD))
LEPS = 1e-5
LB128 = float(-0.5 * np.log(128.0))



class _PInstr:
    __slots__ = ("entry",)

    def __init__(self, entry):
        self.entry = entry

    def then_inc(self, sem, value=1, **kw):
        self.entry[3].append((sem, value))
        return self


class _PEng:

    def __init__(self, ops):
        self._ops = ops

    def wait_ge(self, sem, val):
        self._ops.append(("w", sem, val, None))

    def __getattr__(self, name):
        ops = self._ops

        def call(*a, **kw):
            entry = ["op", name, (a, kw), []]
            ops.append(entry)
            return _PInstr(entry)

        return call


class _DeferredBlock:
    def __init__(self):
        self.ops = {k: [] for k in ("tensor", "scalar", "vector", "gpsimd", "sync")}

    def _mk(self, key):
        def deco(fn):
            fn(_PEng(self.ops[key]))
            return None
        return deco

    @property
    def tensor(self):
        return self._mk("tensor")

    @property
    def scalar(self):
        return self._mk("scalar")

    @property
    def vector(self):
        return self._mk("vector")

    @property
    def gpsimd(self):
        return self._mk("gpsimd")

    @property
    def sync(self):
        return self._mk("sync")

    def flush(self, nc):
        with nc.Block() as blk:
            for key in ("tensor", "scalar", "vector", "gpsimd", "sync"):
                ops = self.ops[key]
                if not ops:
                    continue

                def run(eng, ops=ops):
                    for kind, name, payload, incs in ops:
                        if kind == "w":
                            eng.wait_ge(name, payload)
                        else:
                            a, kw = payload
                            instr = getattr(eng, name)(*a, **kw)
                            for sem, val in incs:
                                instr.then_inc(sem, val)

                getattr(blk, key)(run)


_STAGES = ["ln", "p2b0", "attn0", "p2b1", "attn1", "full"]


def build_nc(taps=(), stage="full"):
    taps = set(taps)
    slvl = _STAGES.index(stage)
    nc = bass.Bass("TRN2", target_bir_lowering=False, debug=False)

    xtok = nc.dram_tensor("xtok", [TS, E], F32, kind="ExternalInput")
    xcol = nc.dram_tensor("xcol", [T, FS], F32, kind="ExternalInput")
    wqt_h = nc.dram_tensor("wqt", [E, FS], F32, kind="ExternalInput")
    wkt_h = nc.dram_tensor("wkt", [E, FS], F32, kind="ExternalInput")
    wvdt_h = nc.dram_tensor("wvdt", [E, 512], F32, kind="ExternalInput")
    wot_h = nc.dram_tensor("wot", [E, FS], F32, kind="ExternalInput")
    awt_h = nc.dram_tensor("awt", [E, 128], F32, kind="ExternalInput")
    gwt_h = nc.dram_tensor("gwt", [E, 16], F32, kind="ExternalInput")
    bq_h = nc.dram_tensor("bq", [128, 2], F32, kind="ExternalInput")
    bk_h = nc.dram_tensor("bk", [128, 2], F32, kind="ExternalInput")
    bvd_h = nc.dram_tensor("bvd", [1, 512], F32, kind="ExternalInput")
    bo_h = nc.dram_tensor("bo", [1, FS], F32, kind="ExternalInput")
    bm_h = nc.dram_tensor("bm", [128, FS], F32, kind="ExternalInput")
    cos2_h = nc.dram_tensor("cos2", [128, S], F32, kind="ExternalInput")
    sinpm_h = nc.dram_tensor("sinpm", [128, S], F32, kind="ExternalInput")
    out_h = nc.dram_tensor("out", [T, FS], F32, kind="ExternalOutput")

    xnt_in = [nc.dram_tensor(f"xnt_in{h}", [E, TS // 2], BF16, kind="Internal") for h in range(2)]
    xnt_out = [nc.dram_tensor(f"xnt_out{h}", [NCORES * E, TS // 2], BF16, kind="Internal",
                              addr_space="Shared") for h in range(2)]
    aot_in = [nc.dram_tensor(f"aot_in{b}", [FS, S], BF16, kind="Internal") for b in range(B)]
    aot_out = [nc.dram_tensor(f"aot_out{b}", [NCORES * FS, S], BF16, kind="Internal",
                              addr_space="Shared") for b in range(B)]
    dsc = nc.dram_tensor("dsc", [T, FS], BF16, kind="Internal")
    tmd = nc.dram_tensor("tmd", [128, T], BF16, kind="Internal")

    tap_h = {}
    for nm, shp in (("xn", [E, TS]), ("qk", [128, 2 * 16384]), ("vb", [128, 2 * 8256]),
                    ("dense", [T, FS]), ("tmask", [128, T]), ("attn", [FS, T]),
                    ("lg", [128, 64]), ("cws", [128, 64]), ("cwt", [8, 1024]),
                    ("cwx", [128, 1024])):
        if nm in taps:
            tap_h[nm] = nc.dram_tensor("tap_" + nm, shp, F32, kind="ExternalOutput")

    with ExitStack() as ctx:
        def sb(name, shape, dt, **kw):
            return ctx.enter_context(nc.sbuf_tensor(name, shape, dt, **kw))

        WQT = sb("WQT", [128, 4096], BF16)
        WKT = sb("WKT", [128, 4096], BF16)
        WVD = sb("WVD", [128, 8192], BF16)
        AMT = sb("AMT", [128, 2048], BF16)
        GW8 = sb("GW8", [128, 256], BF16)
        BM = sb("BM", [128, FS], BF16)
        EXM = sb("EXM", [8, 128], BF16)
        BQ = sb("BQ", [128, 2], F32)
        BK = sb("BK", [128, 2], F32)
        BVD = sb("BVD", [1, 512], BF16)
        BOR = sb("BOR", [1, FS], BF16)
        ONES1 = sb("ONES1", [1, 128], BF16)
        ONES128 = sb("ONES128", [128, 128], BF16)
        ID128 = sb("ID128", [128, 128], F32)
        EPS = sb("EPS", [128, 1], F32)
        LBC = sb("LBC", [128, 1], F32)
        COSS = sb("COSS", [128, 1024], F32)
        LN0 = sb("LN0", [128, 2048], F32)
        LN1 = sb("LN1", [128, 2048], F32)
        XNJ = sb("XNJ", [128, 2048], BF16)
        DS2B = sb("DS2B", [128, 2048], BF16)
        DSTG = sb("DSTG", [128, 2048], BF16)
        BIGA = sb("BIGA", [128, 16384], BF16, is_transpose_dest=True)
        XSB0 = sb("XSB0", [128, 16384], BF16)
        Q = sb("Q", [128, 8192], BF16)
        K = sb("K", [128, 8192], BF16)
        V = sb("V", [128, 2 * 32 * 129], BF16)
        LG = sb("LG", [128, 64], F32)
        T2A = sb("T2A", [128, 64], F32)
        T2B = sb("T2B", [128, 64], F32)
        T2C = sb("T2C", [128, 64], F32)
        CWSd = [sb("CWS0", [128, 64], F32), sb("CWS1", [128, 64], F32)]
        M1 = sb("M1", [128, 8], F32)
        M2 = sb("M2", [128, 8], F32)
        DEN = sb("DEN", [128, 8], F32)
        RE = sb("RE", [128, 8], F32)
        CWT = sb("CWT", [8, 1024], BF16)
        CWXS = sb("CWXS", [128, 1024], BF16)
        RT = sb("RT", [128, 512], F32)
        RA = sb("RA", [128, 1024], F32)
        SHB = sb("SHB", [128, 1024], BF16)
        RMXQ = sb("RMXQ", [128, 1], F32)
        RMXK = sb("RMXK", [128, 1], F32)
        RTM = sb("RTM", [128, 1], F32)
        NEGSH = sb("NEGSH", [128, 1], F32)
        PTB = sb("PTB", [128, 1024], BF16)
        TMB = sb("TMB", [128, 1024], BF16)
        AOS = sb("AOS", [128, 512], BF16)
        AOTS = sb("AOTS", [128, 1024], BF16, is_transpose_dest=True)
        SSQ = sb("SSQ", [128, 1], F32)
        SUMT = sb("SUMT", [128, 1], F32)
        MEAN = sb("MEAN", [128, 1], F32)
        MSQ = sb("MSQ", [128, 1], F32)
        VAR = sb("VAR", [128, 1], F32)
        LNT = sb("LNT", [128, 1], F32)
        RSTD = sb("RSTD", [128, 1], F32)
        RCP = sb("RCP", [128, 4], F32)
        FT = sb("FT", [128, 512], F32)

        PS = [ctx.enter_context(nc.psum_tensor(f"PS{i}", [128, 512], F32)) for i in range(8)]

        sems = {}
        cnt = {}
        for nm in ("w", "w2", "x", "x2", "v", "a", "p", "g", "t", "c", "sb", "sb2", "sbA", "sb2A",
                   "sc", "sd", "tm", "ds", "o", "o3", "o2", "ft", "s2", "e", "m", "av", "ao", "tb",
                   "u00", "u01", "u10", "u11"):
            sems[nm] = ctx.enter_context(nc.semaphore("s_" + nm))
            cnt[nm] = 0

        def inc(nm, instr, k=1):
            instr.then_inc(sems[nm], k)
            cnt[nm] += k
            return cnt[nm]

        def wait(eng, nm, val=None):
            eng.wait_ge(sems[nm], cnt[nm] if val is None else val)

        bank_free = [("v", 0)] * 8

        def wait_banks(t, idxs):
            need = {}
            for i in idxs:
                nm, c = bank_free[i]
                need[nm] = max(need.get(nm, 0), c)
            for nm, c in need.items():
                if c > 0:
                    t.wait_ge(sems[nm], c)

        block = _DeferredBlock()

        @block.gpsimd
        def _(g):
            inc("w", g.dma_start(out=WQT[:, :].rearrange("p (i f) -> p i f", f=FS),
                                 in_=wqt_h[:, :].rearrange("(i p) f -> p i f", p=128)), 16)
            inc("w", g.dma_start(out=WKT[:, :].rearrange("p (i f) -> p i f", f=FS),
                                 in_=wkt_h[:, :].rearrange("(i p) f -> p i f", p=128)), 16)
            inc("w", g.dma_start(out=WVD[:, :].rearrange("p (i f) -> p i f", f=512),
                                 in_=wvdt_h[:, :].rearrange("(i p) f -> p i f", p=128)), 16)
            inc("w", g.dma_start(out=AMT[:, :].rearrange("p (i f) -> p i f", f=128),
                                 in_=awt_h[:, :].rearrange("(i p) f -> p i f", p=128)), 16)
            inc("w", g.dma_start(out=GW8[:, :].rearrange("p (i f) -> p i f", f=16),
                                 in_=gwt_h[:, :].rearrange("(i p) f -> p i f", p=128)), 16)
            inc("w", g.dma_start(out=BM[:, :], in_=bm_h[:, :]), 16)
            inc("w", g.dma_start(out=BVD[:, :], in_=bvd_h[:, :]), 16)
            inc("w", g.dma_start(out=BOR[:, :], in_=bo_h[:, :]), 16)
            g.memset(ONES1[:, :], 1.0)
            g.memset(ONES128[:, :], 1.0)
            g.memset(V[:, :], 1.0)
            inc("g", g.memset(ID128[:, :], 1.0), 1)
            wait(g, "g")
            inc("g", g.affine_select(ID128[:, :], ID128[:, :], [[-1, 128]], ALU.is_equal, 0.0,
                                     base=0, channel_multiplier=1), 1)
            inc("g", g.memset(EXM[:, :], 1.0), 1)
            wait(g, "g")
            inc("g", g.affine_select(EXM[:, :], EXM[:, :], [[1, 128]], ALU.is_ge, 0.0,
                                     base=0, channel_multiplier=-16), 1)
            wait(g, "g")
            inc("g", g.affine_select(EXM[:, :], EXM[:, :], [[-1, 128]], ALU.is_ge, 0.0,
                                     base=15, channel_multiplier=16), 1)
        g_consts = cnt["g"]

        @block.scalar
        def _(s):
            inc("w2", s.dma_start(out=BQ[:, :], in_=bq_h[:, :]), 16)
            inc("w2", s.dma_start(out=BK[:, :], in_=bk_h[:, :]), 16)
        w_all = cnt["w"]
        w2_all = cnt["w2"]

        @block.vector
        def _(v):
            v.memset(EPS[:, :], LEPS)
            v.memset(LBC[:, :], LB128)
            wait(v, "w", w_all)
            wait(v, "w2", w2_all)
            inc("v", v.tensor_scalar_mul(BM[:, :], BM[:, :], 16.0), 1)

        lnbuf = [LN0, LN1]
        ln_tr = [0] * 8
        for j in range(8):
            xs = "x" if j % 2 == 0 else "x2"

            @block.sync
            def _(sy, j=j, xs=xs):
                if j >= 2:
                    wait(sy, "t", ln_tr[j - 2])
                inc(xs, sy.dma_start(out=lnbuf[j % 2][:, :],
                                     in_=xtok[128 * j:128 * (j + 1), :]), 16)
            x_ld = cnt[xs]

            @block.vector
            def _(v, j=j, x_ld=x_ld, xs=xs):
                wait(v, xs, x_ld)
                if j >= 1:
                    wait(v, "t", ln_tr[j - 1])
                buf = lnbuf[j % 2]
                v.scalar_tensor_tensor(XNJ[:, :], buf[:, :], 1.0, buf[:, :],
                                       ALU.mult, ALU.mult, accum_out=SSQ[:, :])
                inc("v", v.tensor_scalar(DS2B[:, :], buf[:, :], 1.0, 0.0,
                                         ALU.mult, ALU.add, accum_out=SUMT[:, :]), 1)
                wait(v, "v")
                inc("v", v.tensor_scalar(MEAN[:, :], SUMT[:, :], 1.0 / E, None, ALU.mult), 1)
                wait(v, "v")
                inc("v", v.tensor_scalar(MSQ[:, :], MEAN[:, :], MEAN[:, :], None, ALU.mult), 1)
                wait(v, "v")
                inc("v", v.tensor_scalar(VAR[:, :], SSQ[:, :], 1.0 / E, MSQ[:, :],
                                         ALU.mult, ALU.subtract), 1)
            v_var = cnt["v"]

            @block.scalar
            def _(s, v_var=v_var):
                wait(s, "v", v_var)
                inc("a", s.activation(LNT[:, :], VAR[:, :], AF.Ln, bias=EPS[:, :]), 1)
                wait(s, "a")
                inc("a", s.activation(RSTD[:, :], LNT[:, :], AF.Exp, scale=-0.5), 1)
            a_rstd = cnt["a"]

            @block.vector
            def _(v, j=j, a_rstd=a_rstd):
                wait(v, "a", a_rstd)
                inc("v", v.tensor_scalar(XNJ[:, :], lnbuf[j % 2][:, :], MEAN[:, :],
                                         RSTD[:, :], ALU.subtract, ALU.mult), 1)
            v_xn = cnt["v"]

            @block.sync
            def _(sy, j=j, v_xn=v_xn):
                wait(sy, "v", v_xn)
                for i in range(8):
                    inc("t", sy.dma_start_transpose(
                        BIGA[:, 1024 * i + 128 * j:1024 * i + 128 * (j + 1)],
                        XNJ[:, 128 * i:128 * (i + 1)]), 16)

            @block.scalar
            def _(s, j=j, v_xn=v_xn):
                wait(s, "v", v_xn)
                for i in range(8, 16):
                    inc("t", s.dma_start_transpose(
                        BIGA[:, 1024 * i + 128 * j:1024 * i + 128 * (j + 1)],
                        XNJ[:, 128 * i:128 * (i + 1)]), 16)
            ln_tr[j] = cnt["t"]

        I32 = mybir.dt.int32
        for h in range(2):
            hs = "sc" if h == 0 else "sd"

            @block.sync
            def _(sy, h=h, hs=hs):
                wait(sy, "t", ln_tr[4 * h + 3])
                inc(hs, sy.dma_start(
                    out=xnt_in[h][:, :].rearrange("(i p) t -> p i t", p=128),
                    in_=BIGA[:, :].rearrange("p (i t) -> p i t", t=1024)[:, :, 512 * h:512 * (h + 1)]), 16)
            sc_x = cnt[hs]

            @block.gpsimd
            def _(g, h=h, sc_x=sc_x, hs=hs):
                wait(g, hs, sc_x)
                inc("c", g.collective_compute(
                    "AllGather", ALU.bypass, replica_groups=[list(range(NCORES))],
                    ins=[xnt_in[h].ap().opt()],
                    outs=[xnt_out[h].ap().opt()]), 1)
        c_ag1 = cnt["c"]

        if "xn" in tap_h:
            @block.gpsimd
            def _(g):
                wait(g, "c", c_ag1)
                for i in range(16):
                    inc("o2", g.dma_start(out=tap_h["xn"][128 * i:128 * (i + 1), :],
                                          in_=xnt_out[128 * i:128 * (i + 1), :]), 16)

        strm = [BIGA, XSB0]
        pe_sb_done = []
        rope_done = [0]
        rope_half = [0, 0]
        rope_a0 = [0]
        rope_a0_half = [0, 0]
        tm_dma = [0]
        ds_dma = [0]
        ag2 = [0, 0]
        g_v1 = [0]
        attn_av_done = [0, 0]
        aots_dma = [("u00", 0), ("u01", 0)]
        aos_free = [0]
        rope_ch_done = [0]

        pending_chain = [None]
        ct_last = {}

        def emit_phase2_half(b):
            rope_ch_done[0] = max(rope_ch_done[0], cnt["g"])
            for sbi in range(4):
                gsb = 4 * b + sbi
                g0 = 1024 * sbi
                cur = strm[gsb % 2]

                sbh = [("sb", "sbA"), ("sb2", "sb2A")][gsb % 2]
                sb_ldh = [0, 0]

                @block.scalar
                def _(sy, gsb=gsb, cur=cur, g0=g0, sbh=sbh):
                    if pe_sb_done:
                        wait(sy, "p", pe_sb_done[-1])
                    for h in range(2):
                        wait(sy, "c", min(h + 1, c_ag1))
                        sb_ldh[h] = inc(sbh[h], sy.dma_start(
                            out=cur[:, :].rearrange("p (i t) -> p i t", t=1024)[:, :, 512 * h:512 * (h + 1)],
                            in_=xnt_out[h][E * gsb:E * (gsb + 1), :].rearrange("(i p) t -> p i t", p=128)), 16)

                mt_mm = [0] * 8
                mt_ev = [0] * 8
                for j in range(8):
                    @block.tensor
                    def _(t, gsb=gsb, cur=cur, j=j, sbh=sbh):
                        if j == 0:
                            wait(t, sbh[0], sb_ldh[0])
                            if gsb == 0:
                                wait(t, "w", w_all)
                                wait(t, "w2", w2_all)
                                wait(t, "g", g_consts)
                        if j == 4:
                            wait(t, sbh[1], sb_ldh[1])
                        vd, pg = PS[4 + (j % 2)], PS[6 + (j % 2)]
                        wait_banks(t, [4 + (j % 2), 6 + (j % 2)])
                        for i in range(16):
                            lhs = cur[:, 1024 * i + 128 * j:1024 * i + 128 * (j + 1)]
                            t.matmul(vd[:, :], lhs, WVD[:, 512 * i:512 * (i + 1)],
                                     start=(i == 0), stop=False)
                            t.matmul(pg[:, 0:8], lhs, GW8[:, 16 * i:16 * i + 8],
                                     start=(i == 0), stop=(i == 15))
                        mt_mm[j] = inc("p", t.matmul(vd[:, :], ONES1[:, :], BVD[:, :],
                                                     start=False, stop=True), 1)

                    @block.vector
                    def _(v, sbi=sbi, b=b, j=j):
                        vd, pg = PS[4 + (j % 2)], PS[6 + (j % 2)]
                        wait(v, "p", mt_mm[j])
                        if j % 4 == 0:
                            wait(v, "ds", ds_dma[-1])
                            if j == 0 and b == 1 and sbi == 0:
                                wait(v, "g", g_v1[0])
                        for a in range(2):
                            ch = (a * 32 + 8 * sbi + j) * 129
                            v.tensor_copy(V[:, ch:ch + 128], vd[:, 128 * a:128 * (a + 1)])
                        v.tensor_copy(DSTG[:, 256 * j:256 * (j + 1)], vd[:, 256:512])
                        mt_ev[j] = inc("v", v.tensor_copy(LG[:, 8 * j:8 * (j + 1)], pg[:, 0:8]), 1)
                        bank_free[4 + (j % 2)] = ("v", mt_ev[j])
                        bank_free[6 + (j % 2)] = ("v", mt_ev[j])

                q_mm = [0] * 4
                k_mm = [0] * 4

                @block.tensor
                def _(t, cur=cur, sbh=sbh):
                    for a in range(2):
                        for cch in range(2):
                            ps = PS[2 * a + cch]
                            wait_banks(t, [2 * a + cch])
                            wait(t, sbh[cch], sb_ldh[cch])
                            for i in range(16):
                                mm = t.matmul(ps[:, :],
                                              WQT[:, 256 * i + 128 * a:256 * i + 128 * (a + 1)],
                                              cur[:, 1024 * i + 512 * cch:1024 * i + 512 * (cch + 1)],
                                              start=(i == 0), stop=(i == 15))
                            q_mm[2 * a + cch] = inc("p", mm, 1)

                ls_gate = [0]

                @block.vector
                def _(v, sbi=sbi):
                    for a in range(2):
                        for cch in range(2):
                            ps = PS[2 * a + cch]
                            wait(v, "p", q_mm[2 * a + cch])
                            cc = inc("v", v.tensor_scalar(
                                Q[:, 4096 * a + 1024 * sbi + 512 * cch:
                                  4096 * a + 1024 * sbi + 512 * (cch + 1)],
                                ps[:, :], BQ[:, a:a + 1], None, ALU.add), 1)
                            bank_free[2 * a + cch] = ("v", cc)
                    lg3 = LG[:, :].rearrange("p (g e) -> p g e", e=8)
                    m1b = M1[:, :].rearrange("p (g o) -> p g o", o=1).broadcast_to([128, 8, 8])
                    a3 = T2A[:, :].rearrange("p (g e) -> p g e", e=8)
                    wait(v, "v")
                    inc("v", v.reduce_max(M1[:, :], lg3, axis=AX.X), 1)
                    wait(v, "v")
                    ls_gate[0] = inc("v", v.tensor_tensor(a3, lg3, m1b, ALU.subtract), 1)

                expl_done = [0]

                @block.scalar
                def _(s):
                    wait(s, "v", ls_gate[0])
                    expl_done[0] = inc("a", s.activation(T2B[:, :], T2A[:, :], AF.Exp), 1)

                cw_ready = [0]
                CWS = CWSd[gsb % 2]

                @block.vector
                def _(v, gsb=gsb, CWS=CWS):
                    a3 = T2A[:, :].rearrange("p (g e) -> p g e", e=8)
                    c3 = T2C[:, :].rearrange("p (g e) -> p g e", e=8)
                    cw3 = CWS[:, :].rearrange("p (g e) -> p g e", e=8)
                    wait(v, "p", ct_last.get(gsb % 2, 0))
                    m2b = M2[:, :].rearrange("p (g o) -> p g o", o=1).broadcast_to([128, 8, 8])
                    reb = RE[:, :].rearrange("p (g o) -> p g o", o=1).broadcast_to([128, 8, 8])
                    wait(v, "v", ls_gate[0])
                    inc("v", v.tensor_scalar(T2C[:, :], T2A[:, :], 0.0, -1e30,
                                             ALU.is_ge, ALU.mult), 1)
                    wait(v, "v")
                    inc("v", v.tensor_add(T2C[:, :], T2C[:, :], T2A[:, :]), 1)
                    wait(v, "v")
                    inc("v", v.reduce_max(M2[:, :], c3, axis=AX.X), 1)
                    wait(v, "v")
                    inc("v", v.tensor_tensor(c3, a3, m2b, ALU.is_ge), 1)
                    wait(v, "a", expl_done[0])
                    wait(v, "v")
                    inc("v", v.tensor_mul(T2A[:, :], T2C[:, :], T2B[:, :]), 1)
                    wait(v, "v")
                    inc("v", v.reduce_sum(DEN[:, :], a3, axis=AX.X), 1)
                    wait(v, "v")
                    inc("v", v.reciprocal(RE[:, :], DEN[:, :]), 1)
                    wait(v, "v")
                    cw_ready[0] = inc("v", v.tensor_tensor(cw3, a3, reb, ALU.mult), 1)

                @block.tensor
                def _(t, cur=cur):
                    for a in range(2):
                        for cch in range(2):
                            ps = PS[2 * a + cch]
                            wait_banks(t, [2 * a + cch])
                            for i in range(16):
                                mm = t.matmul(ps[:, :],
                                              WKT[:, 256 * i + 128 * a:256 * i + 128 * (a + 1)],
                                              cur[:, 1024 * i + 512 * cch:1024 * i + 512 * (cch + 1)],
                                              start=(i == 0), stop=(i == 15))
                            k_mm[2 * a + cch] = inc("p", mm, 1)

                @block.vector
                def _(v, sbi=sbi):
                    for a in range(2):
                        for cch in range(2):
                            ps = PS[2 * a + cch]
                            wait(v, "p", k_mm[2 * a + cch])
                            cc = inc("v", v.tensor_scalar(
                                K[:, 4096 * a + 1024 * sbi + 512 * cch:
                                  4096 * a + 1024 * sbi + 512 * (cch + 1)],
                                ps[:, :], BK[:, a:a + 1], None, ALU.add), 1)
                            bank_free[2 * a + cch] = ("v", cc)

                def make_chain(cur=cur, gsb=gsb, cw_ready=cw_ready, CWS=CWS):
                    ct_mm = [0] * 8
                    emm = [0] * 2
                    tmat_mm = [0] * 2
                    cwt_done = [0]
                    cwx_done = [0]
                    tm_ready = [0]

                    ct_mm = [0] * 8

                    @block.tensor
                    def _(t):
                        wait(t, "v", cw_ready[0])
                        for jj in range(8):
                            ps = PS[4 + jj // 4]
                            wait_banks(t, [4 + jj // 4])
                            ct_mm[jj] = inc("p", t.transpose(
                                ps[0:8, 128 * (jj % 4):128 * (jj % 4) + 128],
                                CWS[:, 8 * jj:8 * jj + 8], ID128[:, :]), 1)
                        ct_last[gsb % 2] = ct_mm[7]

                    cwt_done = [0]

                    @block.vector
                    def _(v):
                        for jj in range(8):
                            wait(v, "p", ct_mm[jj])
                            cc = inc("v", v.tensor_copy(CWT[:, 128 * jj:128 * (jj + 1)],
                                                        PS[4 + jj // 4][0:8,
                                                        128 * (jj % 4):128 * (jj % 4) + 128]), 1)
                            if jj % 4 == 3:
                                bank_free[4 + jj // 4] = ("v", cc)
                        cwt_done[0] = cnt["v"]

                    emm = [0] * 2

                    @block.tensor
                    def _(t):
                        wait(t, "v", cwt_done[0])
                        for cch in range(2):
                            ps = PS[6 + cch]
                            wait_banks(t, [6 + cch])
                            emm[cch] = inc("p", t.matmul(ps[:, :], EXM[:, :],
                                                         CWT[:, 512 * cch:512 * (cch + 1)],
                                                         start=True, stop=True), 1)

                    cwx_done = [0]

                    @block.vector
                    def _(v):
                        for cch in range(2):
                            wait(v, "p", emm[cch])
                            cc = inc("v", v.tensor_copy(CWXS[:, 512 * cch:512 * (cch + 1)],
                                                        PS[6 + cch][:, :]), 1)
                            bank_free[6 + cch] = ("v", cc)
                        cwx_done[0] = cnt["v"]

                    tmat_mm = [0] * 2

                    @block.tensor
                    def _(t, cur=cur):
                        for cch in range(2):
                            ps = PS[cch]
                            wait_banks(t, [cch])
                            for i in range(16):
                                mm = t.matmul(ps[:, :], AMT[:, 128 * i:128 * (i + 1)],
                                              cur[:, 1024 * i + 512 * cch:1024 * i + 512 * (cch + 1)],
                                              start=(i == 0), stop=(i == 15))
                            tmat_mm[cch] = inc("p", mm, 1)
                        pe_sb_done.append(cnt["p"])

                    tm_ready = [0]

                    @block.vector
                    def _(v):
                        wait(v, "v", cwx_done[0])
                        wait(v, "tm", tm_dma[-1])
                        for cch in range(2):
                            wait(v, "p", tmat_mm[cch])
                            cc = inc("v", v.tensor_tensor(TMB[:, 512 * cch:512 * (cch + 1)],
                                                          PS[cch][:, :],
                                                          CWXS[:, 512 * cch:512 * (cch + 1)],
                                                          ALU.mult), 1)
                            bank_free[cch] = ("v", cc)
                        tm_ready[0] = cnt["v"]


                    @block.sync
                    def _(sy, gsb=gsb):
                        wait(sy, "v", tm_ready[0])
                        tm_dma.append(inc("tm", sy.dma_start(
                            out=tmd[:, 1024 * gsb:1024 * (gsb + 1)], in_=TMB[:, :]), 16))

                if pending_chain[0] is not None:
                    pending_chain[0]()
                pending_chain[0] = make_chain

                kev_done = [cnt["v"]]
                gp_prev = [cnt["g"]]
                for a in range(2):
                  for ch in range(2):
                    @block.sync
                    def _(sy, g0=g0, ch=ch):
                        wait(sy, "g", rope_ch_done[0])
                        inc("tb", sy.dma_start(out=COSS[:, 0:512],
                                               in_=cos2_h[:, g0 + 512 * ch:g0 + 512 * (ch + 1)]), 16)
                        inc("tb", sy.dma_start(out=COSS[:, 512:1024],
                                               in_=sinpm_h[:, g0 + 512 * ch:g0 + 512 * (ch + 1)]), 16)
                    tb_ld = cnt["tb"]

                    for side, X in ((0, Q), (1, K)):
                        if True:
                            cols = 4096 * a + 1024 * sbi + 512 * ch
                            a2_done = [0]

                            @block.gpsimd
                            def _(g, X=X, cols=cols, tb_ld=tb_ld,
                                  first=(side == 0), vfirst=(side == 0 and a == 0 and ch == 0),
                                  kev=kev_done[0]):
                                if first:
                                    wait(g, "tb", tb_ld)
                                if vfirst:
                                    wait(g, "v", kev)
                                wait(g, "g")
                                inc("g", g.tensor_tensor(RA[:, 0:512], X[:, cols:cols + 512],
                                                         COSS[:, 0:512], ALU.mult), 1)
                                wait(g, "g")
                                inc("g", g.tensor_tensor(RA[:, 512:1024], X[:, cols:cols + 512],
                                                         COSS[:, 512:1024], ALU.mult), 1)
                                wait(g, "g")
                                g.tensor_copy(RT[0:64, :], RA[64:128, 512:1024])
                                inc("g", g.tensor_copy(RT[64:128, :], RA[0:64, 512:1024]), 1)
                                wait(g, "g")
                                gp_prev[0] = inc("g", g.tensor_tensor(
                                    X[:, cols:cols + 512], RA[:, 0:512],
                                    RT[:, :], ALU.add), 1)
                    rope_ch_done[0] = cnt["g"]
                  if a == 0:
                      rope_a0[0] = cnt["g"]
                rope_done.append(cnt["g"])

                @block.sync
                def _(sy, gsb=gsb):
                    wait(sy, "v", mt_ev[7])
                    ds_dma.append(inc("ds", sy.dma_start(
                        out=dsc[1024 * gsb:1024 * (gsb + 1), :].rearrange("(j p) o -> p j o", p=128),
                        in_=DSTG[:, :].rearrange("p (j o) -> p j o", o=FS)), 16))

            if pending_chain[0] is not None:
                pending_chain[0]()
                pending_chain[0] = None
            rope_half[b] = rope_done[-1]
            rope_a0_half[b] = rope_a0[0]

        mask_ready = [0]

        def emit_attention_half(b, mid_hook=None):
            aot_last = [0]
            if b == 0:
                @block.gpsimd
                def _(g):
                    g.wait_ge(sems["v"], cnt["v"])
                    inc("g", g.memset(LN0[:, :], 1.0), 1)
                    wait(g, "g")
                    for dl in range(4):
                        mm = g.affine_select(LN0[:, 512 * dl:512 * (dl + 1)],
                                             LN0[:, 512 * dl:512 * (dl + 1)],
                                             [[1, 512]], ALU.is_ge, 0.0,
                                             base=-128 * dl, channel_multiplier=-1)
                    mask_ready[0] = inc("g", mm, 1)
            for a in range(2):
                sq_pe = [0] * 8
                red_dve = [0] * 8
                for idx in range(8):
                    side = idx // 4
                    cc = idx % 4
                    X = (Q, K)[side]
                    RMX = (RMXQ, RMXK)[side]
                    sq_dve = [0]

                    @block.vector
                    def _(v, X=X, cc=cc, idx=idx, a=a, b=b):
                        if idx == 0:
                            wait(v, "g", rope_a0_half[b] if a == 0 else rope_half[b])
                            wait(v, "av", cnt["av"])
                        if idx >= 1:
                            wait(v, "p", sq_pe[idx - 1])
                        cols = 4096 * a + 1024 * cc
                        sq_dve[0] = inc("v", v.scalar_tensor_tensor(
                            SHB[:, :], X[:, cols:cols + 1024], 1.0,
                            X[:, cols:cols + 1024], ALU.mult, ALU.mult), 1)

                    @block.tensor
                    def _(t, idx=idx):
                        wait(t, "v", sq_dve[0])
                        if idx == 0:
                            wait_banks(t, [4, 5])
                        else:
                            wait(t, "v", red_dve[idx - 1])
                        t.matmul(PS[4][:, :], ONES128[:, :], SHB[:, 0:512],
                                 start=True, stop=True)
                        sq_pe[idx] = inc("p", t.matmul(PS[5][:, :], ONES128[:, :],
                                                       SHB[:, 512:1024],
                                                       start=True, stop=True), 1)

                    @block.vector
                    def _(v, idx=idx, RMX=RMX):
                        wait(v, "p", sq_pe[idx])
                        if idx % 4 == 0:
                            inc("v", v.reduce_max(RMX[:, :], PS[4][:, :], axis=AX.X), 1)
                        else:
                            inc("v", v.reduce_max(RTM[:, :], PS[4][:, :], axis=AX.X), 1)
                            wait(v, "v")
                            inc("v", v.tensor_max(RMX[:, :], RMX[:, :], RTM[:, :]), 1)
                        wait(v, "v")
                        inc("v", v.reduce_max(RTM[:, :], PS[5][:, :], axis=AX.X), 1)
                        wait(v, "v")
                        red_dve[idx] = inc("v", v.tensor_max(RMX[:, :], RMX[:, :], RTM[:, :]), 1)
                bank_free[4] = ("v", red_dve[7])
                bank_free[5] = ("v", red_dve[7])

                shp_v = [0]

                @block.vector
                def _(v):
                    wait(v, "v", red_dve[7])
                    shp_v[0] = inc("v", v.tensor_tensor(RTM[:, :], RMXQ[:, :], RMXK[:, :],
                                                        ALU.mult), 1)

                shp_a = [0]

                @block.scalar
                def _(s):
                    wait(s, "v", shp_v[0])
                    inc("a", s.activation(LNT[:, :], RTM[:, :], AF.Ln), 1)
                    wait(s, "a")
                    shp_a[0] = inc("a", s.activation(NEGSH[:, :], LNT[:, :], AF.Exp,
                                                     scale=0.5, bias=LBC[:, :]), 1)

                shp_neg = [0]

                @block.vector
                def _(v):
                    wait(v, "a", shp_a[0])
                    shp_neg[0] = inc("v", v.tensor_scalar_mul(NEGSH[:, :], NEGSH[:, :], -1.0), 1)

                e_cnt = {}
                m_cnt = {}
                av_last_qb = [0]
                PSS = [PS[0], PS[1], PS[6], PS[7]]
                slot_e = [0, 0, 0, 0]
                slot_av = [0, 0, 0, 0]

                def pbslot(ping):
                    buf = PTB if ping < 2 else SHB
                    off = 512 * (ping % 2)
                    return buf[:, off:off + 512]

                for qb in range(8):
                    nk = 4 * (qb + 1)
                    ao_gate = cnt["ao"]

                    def emit_av(t, kav, qb=qb, nk=nk, a=a):
                        diag = kav >= 4 * qb
                        if diag:
                            t.wait_ge(sems["m"], m_cnt[(qb, kav)])
                        else:
                            t.wait_ge(sems["e"], e_cnt[(qb, kav)])
                        pb = pbslot(kav % 4)
                        for qs in range(4):
                            psO = PS[2 + qs]
                            mm = t.matmul(psO[:, 0:129],
                                          pb[:, 128 * qs:128 * (qs + 1)],
                                          V[:, (a * 32 + kav) * 129:(a * 32 + kav) * 129 + 129],
                                          start=(kav == 0),
                                          stop=(kav == nk - 1))
                        slot_av[kav % 4] = cnt["av"] + 1
                        return inc("av", mm, 1)

                    for kc in range(nk):
                        @block.tensor
                        def _(t, kc=kc, qb=qb, nk=nk, a=a, b=b):
                            ping = kc % 4
                            if slot_e[ping]:
                                t.wait_ge(sems["e"], slot_e[ping])
                            elif qb == 0 and kc == 0:
                                wait_banks(t, [0, 1, 2, 3, 4, 5, 6, 7])
                                t.wait_ge(sems["g"],
                                          rope_a0_half[b] if a == 0 else rope_half[b])
                            if kc == 0:
                                t.wait_ge(sems["ao"], ao_gate)
                            mm = t.matmul(PSS[ping][:, :],
                                          K[:, 4096 * a + 128 * kc:4096 * a + 128 * (kc + 1)],
                                          Q[:, 4096 * a + 512 * qb:4096 * a + 512 * (qb + 1)],
                                          start=True, stop=True)
                            inc("s2", mm, 1)
                            if kc >= 1:
                                emit_av(t, kc - 1)
                        s2_now = cnt["s2"]

                        @block.scalar
                        def _(s, kc=kc, qb=qb, s2_now=s2_now):
                            ping = kc % 4
                            if qb == 0 and kc == 0:
                                wait(s, "v", shp_neg[0])
                            wait(s, "s2", s2_now)
                            if slot_av[ping]:
                                s.wait_ge(sems["av"], slot_av[ping])
                            e_cnt[(qb, kc)] = inc("e", s.activation(
                                pbslot(ping), PSS[ping][:, :], AF.Exp,
                                bias=NEGSH[:, :], scale=SCALE), 1)
                            slot_e[ping] = e_cnt[(qb, kc)]

                        if kc >= 4 * qb:
                            @block.vector
                            def _(v, kc=kc, qb=qb):
                                wait(v, "e", e_cnt[(qb, kc)])
                                if qb == 0 and kc == 0:
                                    wait(v, "g", mask_ready[0])
                                dl = kc - 4 * qb
                                m_cnt[(qb, kc)] = inc("m", v.tensor_tensor(
                                    pbslot(kc % 4), pbslot(kc % 4),
                                    LN0[:, 512 * dl:512 * (dl + 1)], ALU.mult), 1)

                    @block.tensor
                    def _(t, qb=qb, nk=nk):
                        av_last_qb[0] = emit_av(t, nk - 1)

                    ao_now = [0]

                    @block.vector
                    def _(v, qb=qb, nk=nk):
                        wait(v, "av", av_last_qb[0])
                        wait(v, "t", aos_free[0])
                        wait(v, "ao", cnt["ao"])
                        for qs in range(4):
                            inc("v", v.reciprocal(RCP[:, qs:qs + 1], PS[2 + qs][:, 128:129]), 1)
                        wait(v, "v")
                        for qs in range(4):
                            last = v.tensor_scalar(AOS[:, 128 * qs:128 * (qs + 1)],
                                                   PS[2 + qs][:, 0:128], RCP[:, qs:qs + 1],
                                                   None, ALU.mult)
                            if qs < 3:
                                inc("v", last, 1)
                        ao_now[0] = inc("ao", last, 1)

                    @block.sync
                    def _(sy, qb=qb, a=a, b=b):
                        wait(sy, "ao", ao_now[0])
                        pp = qb % 2
                        sy.wait_ge(sems[aots_dma[pp][0]], aots_dma[pp][1])
                        for qs in range(4):
                            aos_free[0] = inc("t", sy.dma_start_transpose(
                                AOTS[:, 512 * pp + 128 * qs:512 * pp + 128 * (qs + 1)],
                                AOS[:, 128 * qs:128 * (qs + 1)]), 16)
                        wait(sy, "t", aos_free[0])
                        us = f"u{b}{pp}"
                        aots_dma[pp] = (us, inc(us, sy.dma_start(
                            out=aot_in[b][128 * a:128 * (a + 1), 512 * qb:512 * (qb + 1)],
                            in_=AOTS[:, 512 * pp:512 * pp + 512]), 16))

                for bi in (0, 1, 6, 7):
                    bank_free[bi] = ("e", cnt["e"])
                for bi in (2, 3, 4, 5):
                    bank_free[bi] = ("ao", cnt["ao"])
                if a == 0 and mid_hook is not None:
                    mid_hook()
                    bank_free[0] = ("v", fin_ev[(3, 6)])
                    bank_free[1] = ("v", fin_ev[(3, 7)])

            attn_av_done[b] = cnt["av"]

            @block.gpsimd
            def _(g, b=b):
                wait(g, f"u{b}0", cnt[f"u{b}0"])
                wait(g, f"u{b}1", cnt[f"u{b}1"])
                ag2[b] = inc("c", g.collective_compute(
                    "AllGather", ALU.bypass, replica_groups=[list(range(NCORES))],
                    ins=[aot_in[b].ap().opt()],
                    outs=[aot_out[b].ap().opt()]), 1)
                if b == 0:
                    g.wait_ge(sems["av"], attn_av_done[0])
                    g_v1[0] = inc("g", g.memset(V[:, :], 1.0), 1)

        av_prev = {}

        def emit_qkv_taps(b):
            if not ({"qk", "vb"} & set(tap_h)):
                return

            @block.gpsimd
            def _(g, b=b):
                g.wait_ge(sems["g"], rope_half[b])
                if "qk" in tap_h:
                    inc("o2", g.dma_start(out=tap_h["qk"][:, 16384 * b:16384 * b + 8192],
                                         in_=Q[:, :]), 16)
                    inc("o2", g.dma_start(out=tap_h["qk"][:, 16384 * b + 8192:16384 * (b + 1)],
                                         in_=K[:, :]), 16)
                if "vb" in tap_h:
                    inc("o2", g.dma_start(out=tap_h["vb"][:, 8256 * b:8256 * (b + 1)],
                                         in_=V[:, :]), 16)

        def emit_attn_tap(b):
            if "attn" not in tap_h:
                return

            @block.gpsimd
            def _(g, b=b):
                wait(g, "c", max(ag2h[b]))
                for a in range(2):
                    inc("o2", g.dma_start(out=tap_h["attn"][128 * a:128 * (a + 1), S * b:S * (b + 1)],
                                          in_=aot_inH[b][a][:, :]), 16)

        fin_pe_j = {}
        fin_pe = {}
        fin_ev = {}
        out_done = {}
        ft_done = {}
        asb = [BIGA, XSB0]
        xcb = [LN0, LN1]
        dsb = [XNJ, DS2B]
        tmbb = [TMB, PTB]

        def emit_final_tb(tb, mid):
            bb = tb // 4
            c4 = 1024 * (tb % 4)
            sbs = "sb" if tb % 2 == 0 else "sb2"
            xs = "x" if tb % 2 == 0 else "x2"
            xcb_t = LN1 if mid else xcb[tb % 2]
            tmb_t = TMB if mid else tmbb[tb % 2]

            @block.sync
            def _(sy, tb=tb, bb=bb, c4=c4, sbs=sbs):
                wait(sy, "c", ag2[bb])
                if tb == 0:
                    wait(sy, "p", pe_sb_done[-1])
                if tb - 2 in fin_pe:
                    wait(sy, "p", fin_pe[tb - 2])
                inc(sbs, sy.dma_start(
                    out=asb[tb % 2][:, :].rearrange("p (i t) -> p i t", t=1024),
                    in_=aot_out[bb][:, c4:c4 + 1024].rearrange("(i p) t -> p i t", p=128)), 16)
            f_ld = cnt[sbs]

            @block.scalar
            def _(s, tb=tb, xs=xs, mid=mid, xcb_t=xcb_t, tmb_t=tmb_t):
                if mid:
                    if tb == 0:
                        wait(s, "v", cnt["v"])
                        wait(s, "tm", tm_dma[-1])
                        wait(s, "ds", ds_dma[-1])
                    else:
                        wait(s, "v", fin_ev[(tb - 1, 7)])
                        wait(s, "p", fin_pe[tb - 1])
                else:
                    if tb in (4, 5):
                        wait(s, "m", cnt["m"])
                        s.wait_ge(sems["av"], attn_av_done[1])
                        wait(s, "e", cnt["e"])
                        wait(s, "tm", tm_dma[-1])
                        wait(s, "ds", ds_dma[-1])
                    if (tb - 2, 7) in fin_ev:
                        wait(s, "v", fin_ev[(tb - 2, 7)])
                    if tb - 2 in fin_pe:
                        wait(s, "p", fin_pe[tb - 2])
                inc(xs, s.dma_start(out=xcb_t[:, :].rearrange("p (j o) -> p j o", o=FS),
                                    in_=xcol[1024 * tb:1024 * (tb + 1), :]
                                    .rearrange("(j p) o -> p j o", p=128)), 16)
                inc(xs, s.dma_start(out=dsb[tb % 2][:, :].rearrange("p (j o) -> p j o", o=FS),
                                    in_=dsc[1024 * tb:1024 * (tb + 1), :]
                                    .rearrange("(j p) o -> p j o", p=128)), 16)
                inc(xs, s.dma_start(out=tmb_t[:, :],
                                    in_=tmd[:, 1024 * tb:1024 * (tb + 1)]), 16)
            f_x = cnt[xs]

            for j in range(8):
                @block.tensor
                def _(t, tb=tb, f_ld=f_ld, f_x=f_x, sbs=sbs, xs=xs, j=j, tmb_t=tmb_t):
                    if j == 0:
                        wait(t, sbs, f_ld)
                        wait(t, xs, f_x)
                        if tb == 0:
                            wait_banks(t, [0, 1])
                            wait(t, "w", wot_ld)
                    ps = PS[j % 2]
                    if (tb, j - 2) in fin_ev:
                        wait(t, "v", fin_ev[(tb, j - 2)])
                    elif (tb - 1, j + 6) in fin_ev:
                        wait(t, "v", fin_ev[(tb - 1, j + 6)])
                    cur = asb[tb % 2]
                    for i in range(16):
                        t.matmul(ps[:, 0:256],
                                 cur[:, 1024 * i + 128 * j:1024 * i + 128 * (j + 1)],
                                 WQT[:, 256 * i:256 * (i + 1)], start=(i == 0), stop=False)
                    t.matmul(ps[:, 0:256], tmb_t[:, 128 * j:128 * (j + 1)], BM[:, :],
                             start=False, stop=False)
                    fin_pe_j[(tb, j)] = inc("p", t.matmul(ps[:, 0:256], ONES1[:, :], BOR[:, :],
                                                          start=False, stop=True), 1)

                @block.vector
                def _(v, tb=tb, j=j, xcb_t=xcb_t):
                    ps = PS[j % 2]
                    wait(v, "p", fin_pe_j[(tb, j)])
                    ft = FT[:, 256 * (j % 2):256 * (j % 2) + 256]
                    if (tb, j - 2) in ft_done:
                        wait(v, "ft", ft_done[(tb, j - 2)])
                    elif (tb - 1, j + 6) in ft_done:
                        wait(v, "ft", ft_done[(tb - 1, j + 6)])
                    inc("v", v.tensor_tensor(ft, ps[:, 0:256],
                                             dsb[tb % 2][:, 256 * j:256 * (j + 1)], ALU.add), 1)
                    wait(v, "v")
                    fin_ev[(tb, j)] = inc("v", v.tensor_tensor(
                        ft, ft, xcb_t[:, 256 * j:256 * (j + 1)], ALU.add), 1)

                osem = "o" if j % 2 == 0 else "o3"

                @block.sync
                def _(sy, tb=tb, j=j, osem=osem):
                    wait(sy, "v", fin_ev[(tb, j)])
                    out_done[(tb, j)] = (osem, inc(osem, sy.dma_start(
                        out=out_h[1024 * tb + 128 * j:1024 * tb + 128 * (j + 1), :],
                        in_=FT[:, 256 * (j % 2):256 * (j % 2) + 256]), 16))

                @block.gpsimd
                def _(g, tb=tb, j=j):
                    g.wait_ge(sems[out_done[(tb, j)][0]], out_done[(tb, j)][1])
                    ft_done[(tb, j)] = inc("ft", g.nop(), 1)
            fin_pe[tb] = cnt["p"]


        if slvl >= 1:
            emit_phase2_half(0)
            emit_qkv_taps(0)
        if slvl >= 2:
            emit_attention_half(0)
            emit_attn_tap(0)
        if slvl >= 3:
            emit_phase2_half(1)
            emit_qkv_taps(1)

        if slvl >= 3:
            @block.gpsimd
            def _(g):
                g.wait_ge(sems["p"], pe_sb_done[-1])
                inc("w", g.dma_start(out=WQT[:, :].rearrange("p (i f) -> p i f", f=FS),
                                     in_=wot_h[:, :].rearrange("(i p) f -> p i f", p=128)), 16)
            wot_ld = cnt["w"]
        if slvl >= 4:
            mid = None
            if slvl >= 5:
                def mid():
                    for tb in range(4):
                        emit_final_tb(tb, True)
            emit_attention_half(1, mid_hook=mid)
            emit_attn_tap(1)

        if {"lg", "cws", "cwt", "cwx"} & set(tap_h):
            @block.gpsimd
            def _(g):
                g.wait_ge(sems["v"], cnt["v"])
                for nm, buf in (("lg", LG), ("cws", CWS), ("cwt", CWT), ("cwx", CWXS)):
                    if nm in tap_h:
                        inc("o2", g.dma_start(out=tap_h[nm][:, :], in_=buf[:, :]), 16)

        if {"dense", "tmask"} & set(tap_h):
            @block.gpsimd
            def _(g):
                wait(g, "tm", cnt["tm"])
                wait(g, "ds", cnt["ds"])
                if "dense" in tap_h:
                    inc("o2", g.dma_start(out=tap_h["dense"][:, :], in_=dsc[:, :]), 16)
                if "tmask" in tap_h:
                    inc("o2", g.dma_start(out=tap_h["tmask"][:, :], in_=tmd[:, :]), 16)

        if slvl >= 5:
            for tb in range(4, 8):
                emit_final_tb(tb, False)

        @block.sync
        def _(sy):
            wait(sy, "o", cnt["o"])
            wait(sy, "o3", cnt["o3"])

        if cnt["o2"]:
            @block.gpsimd
            def _(g):
                wait(g, "o2", cnt["o2"])

        block.flush(nc)

    return nc


_PERM = np.concatenate([np.arange(0, 128, 2), np.arange(1, 128, 2)])


def _stage(core, ins):
    c = core
    Fc = slice(256 * c, 256 * (c + 1))
    Tc = slice(1024 * c, 1024 * (c + 1))
    xf = np.asarray(ins["L0_attn_x"], dtype=np.float32).reshape(T, E)

    def permrows(w):
        w2 = w.reshape(2, 128, E)[:, _PERM, :]
        return w2.reshape(256, E)

    wq = permrows(np.asarray(ins["L0_attn_Wq"], np.float32)[Fc])
    wk = permrows(np.asarray(ins["L0_attn_Wk"], np.float32)[Fc])
    wv = np.asarray(ins["L0_attn_Wv"], np.float32)[Fc]
    wo = np.asarray(ins["L0_attn_Wo"], np.float32)[Fc]
    wd = np.asarray(ins["L0_ffn_weight"], np.float32)[Fc]
    bq = np.asarray(ins["L0_attn_bq"], np.float32)[Fc].reshape(2, 128)[:, _PERM].T
    bk = np.asarray(ins["L0_attn_bk"], np.float32)[Fc].reshape(2, 128)[:, _PERM].T
    bv = np.asarray(ins["L0_attn_bv"], np.float32)[Fc]
    bo = np.asarray(ins["L0_attn_bo"], np.float32)[Fc].reshape(1, FS)
    aw = np.asarray(ins["L0_ffn_A_w"], np.float32).reshape(128, E)
    gw = np.asarray(ins["L0_ffn_gate_w"], np.float32)
    bmw = np.asarray(ins["L0_ffn_B_w"], np.float32)[:, Fc, :]
    bm = bmw.transpose(0, 2, 1).reshape(128, FS)

    inv = 10000.0 ** (-np.arange(64, dtype=np.float64) / 64.0)
    ang = inv[:, None] * np.arange(S, dtype=np.float64)[None, :]
    cosv = np.cos(ang).astype(np.float32)
    sinv = np.sin(ang).astype(np.float32)

    return {
        "xtok": np.ascontiguousarray(xf[Tc]),
        "xcol": np.ascontiguousarray(xf[:, Fc]),
        "wqt": np.ascontiguousarray(wq.T),
        "wkt": np.ascontiguousarray(wk.T),
        "wvdt": np.ascontiguousarray(np.concatenate([wv.T, wd.T], axis=1)),
        "wot": np.ascontiguousarray(wo.T),
        "awt": np.ascontiguousarray(aw.T),
        "gwt": np.ascontiguousarray(
            np.concatenate([gw.T, np.zeros((E, 8), np.float32)], axis=1)),
        "bq": np.ascontiguousarray(bq),
        "bk": np.ascontiguousarray(bk),
        "bvd": np.concatenate([bv, np.zeros(256, np.float32)]).reshape(1, 512),
        "bo": np.ascontiguousarray(bo),
        "bm": np.ascontiguousarray(bm),
        "cos2": np.ascontiguousarray(np.concatenate([cosv, cosv], axis=0)),
        "sinpm": np.ascontiguousarray(np.concatenate([sinv, -sinv], axis=0)),
    }


def kernel(**inputs):
    nc = build_nc()
    in_maps = [_stage(c, inputs) for c in range(NCORES)]
    res = run_bass_kernel_spmd(nc, in_maps, core_ids=list(range(NCORES)))
    out = np.concatenate([np.asarray(res.results[c]["out"]) for c in range(NCORES)], axis=1)
    return out.reshape(B, S, E).astype(np.float32)


# version 16
# speedup vs baseline: 1.0074x; 1.0074x over previous
import numpy as np
import ml_dtypes
from contextlib import ExitStack

import concourse.bass as bass
import concourse.mybir as mybir
from concourse.bass_utils import run_bass_kernel_spmd

F32 = mybir.dt.float32
BF16 = mybir.dt.bfloat16
AF = mybir.ActivationFunctionType
ALU = mybir.AluOpType
AX = mybir.AxisListType

NCORES = 8
E = 2048
HD = 128
B = 2
S = 4096
T = B * S
FS = 256
TS = 1024
SCALE = 1.0 / float(np.sqrt(HD))
LEPS = 1e-5
LB128 = float(-0.5 * np.log(128.0))



class _PInstr:
    __slots__ = ("entry",)

    def __init__(self, entry):
        self.entry = entry

    def then_inc(self, sem, value=1, **kw):
        self.entry[3].append((sem, value))
        return self


class _PEng:

    def __init__(self, ops):
        self._ops = ops

    def wait_ge(self, sem, val):
        self._ops.append(("w", sem, val, None))

    def __getattr__(self, name):
        ops = self._ops

        def call(*a, **kw):
            entry = ["op", name, (a, kw), []]
            ops.append(entry)
            return _PInstr(entry)

        return call


class _DeferredBlock:
    def __init__(self):
        self.ops = {k: [] for k in ("tensor", "scalar", "vector", "gpsimd", "sync")}

    def _mk(self, key):
        def deco(fn):
            fn(_PEng(self.ops[key]))
            return None
        return deco

    @property
    def tensor(self):
        return self._mk("tensor")

    @property
    def scalar(self):
        return self._mk("scalar")

    @property
    def vector(self):
        return self._mk("vector")

    @property
    def gpsimd(self):
        return self._mk("gpsimd")

    @property
    def sync(self):
        return self._mk("sync")

    def flush(self, nc):
        with nc.Block() as blk:
            for key in ("tensor", "scalar", "vector", "gpsimd", "sync"):
                ops = self.ops[key]
                if not ops:
                    continue

                def run(eng, ops=ops):
                    for kind, name, payload, incs in ops:
                        if kind == "w":
                            eng.wait_ge(name, payload)
                        else:
                            a, kw = payload
                            instr = getattr(eng, name)(*a, **kw)
                            for sem, val in incs:
                                instr.then_inc(sem, val)

                getattr(blk, key)(run)


_STAGES = ["ln", "p2b0", "attn0", "p2b1", "attn1", "full"]


def build_nc(taps=(), stage="full"):
    taps = set(taps)
    slvl = _STAGES.index(stage)
    nc = bass.Bass("TRN2", target_bir_lowering=False, debug=False)

    xtok = nc.dram_tensor("xtok", [TS, E], F32, kind="ExternalInput")
    xcol = nc.dram_tensor("xcol", [T, FS], F32, kind="ExternalInput")
    wqt_h = nc.dram_tensor("wqt", [E, FS], F32, kind="ExternalInput")
    wkt_h = nc.dram_tensor("wkt", [E, FS], F32, kind="ExternalInput")
    wvdt_h = nc.dram_tensor("wvdt", [E, 512], F32, kind="ExternalInput")
    wot_h = nc.dram_tensor("wot", [E, FS], F32, kind="ExternalInput")
    awt_h = nc.dram_tensor("awt", [E, 128], F32, kind="ExternalInput")
    gwt_h = nc.dram_tensor("gwt", [E, 16], F32, kind="ExternalInput")
    bq_h = nc.dram_tensor("bq", [128, 2], F32, kind="ExternalInput")
    bk_h = nc.dram_tensor("bk", [128, 2], F32, kind="ExternalInput")
    bvd_h = nc.dram_tensor("bvd", [1, 512], F32, kind="ExternalInput")
    bo_h = nc.dram_tensor("bo", [1, FS], F32, kind="ExternalInput")
    bm_h = nc.dram_tensor("bm", [128, FS], F32, kind="ExternalInput")
    cos2_h = nc.dram_tensor("cos2", [128, S], F32, kind="ExternalInput")
    sinpm_h = nc.dram_tensor("sinpm", [128, S], F32, kind="ExternalInput")
    out_h = nc.dram_tensor("out", [T, FS], F32, kind="ExternalOutput")

    xnt_in = [nc.dram_tensor(f"xnt_in{h}", [E, TS // 2], BF16, kind="Internal") for h in range(2)]
    xnt_out = [nc.dram_tensor(f"xnt_out{h}", [NCORES * E, TS // 2], BF16, kind="Internal",
                              addr_space="Shared") for h in range(2)]
    aot_in = [nc.dram_tensor(f"aot_in{b}", [FS, S], BF16, kind="Internal") for b in range(B)]
    aot_out = [nc.dram_tensor(f"aot_out{b}", [NCORES * FS, S], BF16, kind="Internal",
                              addr_space="Shared") for b in range(B)]
    dsc = nc.dram_tensor("dsc", [T, FS], BF16, kind="Internal")
    tmd = nc.dram_tensor("tmd", [128, T], BF16, kind="Internal")

    tap_h = {}
    for nm, shp in (("xn", [E, TS]), ("qk", [128, 2 * 16384]), ("vb", [128, 2 * 8256]),
                    ("dense", [T, FS]), ("tmask", [128, T]), ("attn", [FS, T]),
                    ("lg", [128, 64]), ("cws", [128, 64]), ("cwt", [8, 1024]),
                    ("cwx", [128, 1024])):
        if nm in taps:
            tap_h[nm] = nc.dram_tensor("tap_" + nm, shp, F32, kind="ExternalOutput")

    with ExitStack() as ctx:
        def sb(name, shape, dt, **kw):
            return ctx.enter_context(nc.sbuf_tensor(name, shape, dt, **kw))

        WQT = sb("WQT", [128, 4096], BF16)
        WKT = sb("WKT", [128, 4096], BF16)
        WVD = sb("WVD", [128, 8192], BF16)
        AMT = sb("AMT", [128, 2048], BF16)
        GW8 = sb("GW8", [128, 256], BF16)
        BM = sb("BM", [128, FS], BF16)
        EXM = sb("EXM", [8, 128], BF16)
        BQ = sb("BQ", [128, 2], F32)
        BK = sb("BK", [128, 2], F32)
        BVD = sb("BVD", [1, 512], BF16)
        BOR = sb("BOR", [1, FS], BF16)
        ONES1 = sb("ONES1", [1, 128], BF16)
        ONES128 = sb("ONES128", [128, 128], BF16)
        ID128 = sb("ID128", [128, 128], F32)
        EPS = sb("EPS", [128, 1], F32)
        LBC = sb("LBC", [128, 1], F32)
        COSS = sb("COSS", [128, 1024], F32)
        LN0 = sb("LN0", [128, 2048], F32)
        LN1 = sb("LN1", [128, 2048], F32)
        XNJ = sb("XNJ", [128, 2048], BF16)
        DS2B = sb("DS2B", [128, 2048], BF16)
        DSTG = sb("DSTG", [128, 2048], BF16)
        BIGA = sb("BIGA", [128, 16384], BF16, is_transpose_dest=True)
        XSB0 = sb("XSB0", [128, 16384], BF16)
        Q = sb("Q", [128, 8192], BF16)
        K = sb("K", [128, 8192], BF16)
        V = sb("V", [128, 2 * 32 * 129], BF16)
        LG = sb("LG", [128, 64], F32)
        T2A = sb("T2A", [128, 64], F32)
        T2B = sb("T2B", [128, 64], F32)
        T2C = sb("T2C", [128, 64], F32)
        CWSd = [sb("CWS0", [128, 64], F32), sb("CWS1", [128, 64], F32)]
        M1 = sb("M1", [128, 8], F32)
        M2 = sb("M2", [128, 8], F32)
        DEN = sb("DEN", [128, 8], F32)
        RE = sb("RE", [128, 8], F32)
        CWT = sb("CWT", [8, 1024], BF16)
        CWXS = sb("CWXS", [128, 1024], BF16)
        RT = sb("RT", [128, 512], F32)
        RA = sb("RA", [128, 1024], F32)
        SHB = sb("SHB", [128, 1024], BF16)
        RMXQ = sb("RMXQ", [128, 1], F32)
        RMXK = sb("RMXK", [128, 1], F32)
        RTM = sb("RTM", [128, 1], F32)
        NEGSH = sb("NEGSH", [128, 1], F32)
        PTB = sb("PTB", [128, 1024], BF16)
        TMB = sb("TMB", [128, 1024], BF16)
        AOS = sb("AOS", [128, 512], BF16)
        AOTS = sb("AOTS", [128, 1024], BF16, is_transpose_dest=True)
        SSQ = sb("SSQ", [128, 1], F32)
        SUMT = sb("SUMT", [128, 1], F32)
        MEAN = sb("MEAN", [128, 1], F32)
        MSQ = sb("MSQ", [128, 1], F32)
        VAR = sb("VAR", [128, 1], F32)
        LNT = sb("LNT", [128, 1], F32)
        RSTD = sb("RSTD", [128, 1], F32)
        RCP = sb("RCP", [128, 4], F32)
        FT = sb("FT", [128, 512], F32)

        PS = [ctx.enter_context(nc.psum_tensor(f"PS{i}", [128, 512], F32)) for i in range(8)]

        sems = {}
        cnt = {}
        for nm in ("w", "w2", "x", "x2", "v", "a", "p", "g", "t", "c", "sb", "sb2", "sbA", "sb2A",
                   "sc", "sd", "tm", "ds", "o", "o3", "o2", "ft", "s2", "e", "m", "av", "ao", "tb",
                   "u00", "u01", "u10", "u11"):
            sems[nm] = ctx.enter_context(nc.semaphore("s_" + nm))
            cnt[nm] = 0

        def inc(nm, instr, k=1):
            instr.then_inc(sems[nm], k)
            cnt[nm] += k
            return cnt[nm]

        def wait(eng, nm, val=None):
            eng.wait_ge(sems[nm], cnt[nm] if val is None else val)

        bank_free = [("v", 0)] * 8

        def wait_banks(t, idxs):
            need = {}
            for i in idxs:
                nm, c = bank_free[i]
                need[nm] = max(need.get(nm, 0), c)
            for nm, c in need.items():
                if c > 0:
                    t.wait_ge(sems[nm], c)

        block = _DeferredBlock()

        @block.gpsimd
        def _(g):
            inc("w", g.dma_start(out=WQT[:, :].rearrange("p (i f) -> p i f", f=FS),
                                 in_=wqt_h[:, :].rearrange("(i p) f -> p i f", p=128)), 16)
            inc("w", g.dma_start(out=WKT[:, :].rearrange("p (i f) -> p i f", f=FS),
                                 in_=wkt_h[:, :].rearrange("(i p) f -> p i f", p=128)), 16)
            inc("w", g.dma_start(out=WVD[:, :].rearrange("p (i f) -> p i f", f=512),
                                 in_=wvdt_h[:, :].rearrange("(i p) f -> p i f", p=128)), 16)
            inc("w", g.dma_start(out=AMT[:, :].rearrange("p (i f) -> p i f", f=128),
                                 in_=awt_h[:, :].rearrange("(i p) f -> p i f", p=128)), 16)
            inc("w", g.dma_start(out=GW8[:, :].rearrange("p (i f) -> p i f", f=16),
                                 in_=gwt_h[:, :].rearrange("(i p) f -> p i f", p=128)), 16)
            inc("w", g.dma_start(out=BM[:, :], in_=bm_h[:, :]), 16)
            inc("w", g.dma_start(out=BVD[:, :], in_=bvd_h[:, :]), 16)
            inc("w", g.dma_start(out=BOR[:, :], in_=bo_h[:, :]), 16)
            g.memset(ONES1[:, :], 1.0)
            g.memset(ONES128[:, :], 1.0)
            g.memset(V[:, :], 1.0)
            inc("g", g.memset(ID128[:, :], 1.0), 1)
            wait(g, "g")
            inc("g", g.affine_select(ID128[:, :], ID128[:, :], [[-1, 128]], ALU.is_equal, 0.0,
                                     base=0, channel_multiplier=1), 1)
            inc("g", g.memset(EXM[:, :], 1.0), 1)
            wait(g, "g")
            inc("g", g.affine_select(EXM[:, :], EXM[:, :], [[1, 128]], ALU.is_ge, 0.0,
                                     base=0, channel_multiplier=-16), 1)
            wait(g, "g")
            inc("g", g.affine_select(EXM[:, :], EXM[:, :], [[-1, 128]], ALU.is_ge, 0.0,
                                     base=15, channel_multiplier=16), 1)
        g_consts = cnt["g"]

        @block.scalar
        def _(s):
            inc("w2", s.dma_start(out=BQ[:, :], in_=bq_h[:, :]), 16)
            inc("w2", s.dma_start(out=BK[:, :], in_=bk_h[:, :]), 16)
        w_all = cnt["w"]
        w2_all = cnt["w2"]

        @block.vector
        def _(v):
            v.memset(EPS[:, :], LEPS)
            v.memset(LBC[:, :], LB128)
            wait(v, "w", w_all)
            wait(v, "w2", w2_all)
            inc("v", v.tensor_scalar_mul(BM[:, :], BM[:, :], 16.0), 1)

        lnbuf = [LN0, LN1]
        ln_tr = [0] * 8
        for j in range(8):
            xs = "x" if j % 2 == 0 else "x2"

            @block.sync
            def _(sy, j=j, xs=xs):
                if j >= 2:
                    wait(sy, "t", ln_tr[j - 2])
                inc(xs, sy.dma_start(out=lnbuf[j % 2][:, :],
                                     in_=xtok[128 * j:128 * (j + 1), :]), 16)
            x_ld = cnt[xs]

            @block.vector
            def _(v, j=j, x_ld=x_ld, xs=xs):
                wait(v, xs, x_ld)
                if j >= 1:
                    wait(v, "t", ln_tr[j - 1])
                buf = lnbuf[j % 2]
                v.scalar_tensor_tensor(XNJ[:, :], buf[:, :], 1.0, buf[:, :],
                                       ALU.mult, ALU.mult, accum_out=SSQ[:, :])
                inc("v", v.tensor_scalar(DS2B[:, :], buf[:, :], 1.0, 0.0,
                                         ALU.mult, ALU.add, accum_out=SUMT[:, :]), 1)
                wait(v, "v")
                inc("v", v.tensor_scalar(MEAN[:, :], SUMT[:, :], 1.0 / E, None, ALU.mult), 1)
                wait(v, "v")
                inc("v", v.tensor_scalar(MSQ[:, :], MEAN[:, :], MEAN[:, :], None, ALU.mult), 1)
                wait(v, "v")
                inc("v", v.tensor_scalar(VAR[:, :], SSQ[:, :], 1.0 / E, MSQ[:, :],
                                         ALU.mult, ALU.subtract), 1)
            v_var = cnt["v"]

            @block.scalar
            def _(s, v_var=v_var):
                wait(s, "v", v_var)
                inc("a", s.activation(LNT[:, :], VAR[:, :], AF.Ln, bias=EPS[:, :]), 1)
                wait(s, "a")
                inc("a", s.activation(RSTD[:, :], LNT[:, :], AF.Exp, scale=-0.5), 1)
            a_rstd = cnt["a"]

            @block.vector
            def _(v, j=j, a_rstd=a_rstd):
                wait(v, "a", a_rstd)
                inc("v", v.tensor_scalar(XNJ[:, :], lnbuf[j % 2][:, :], MEAN[:, :],
                                         RSTD[:, :], ALU.subtract, ALU.mult), 1)
            v_xn = cnt["v"]

            @block.sync
            def _(sy, j=j, v_xn=v_xn):
                wait(sy, "v", v_xn)
                for i in range(8):
                    inc("t", sy.dma_start_transpose(
                        BIGA[:, 1024 * i + 128 * j:1024 * i + 128 * (j + 1)],
                        XNJ[:, 128 * i:128 * (i + 1)]), 16)

            @block.scalar
            def _(s, j=j, v_xn=v_xn):
                wait(s, "v", v_xn)
                for i in range(8, 16):
                    inc("t", s.dma_start_transpose(
                        BIGA[:, 1024 * i + 128 * j:1024 * i + 128 * (j + 1)],
                        XNJ[:, 128 * i:128 * (i + 1)]), 16)
            ln_tr[j] = cnt["t"]

        I32 = mybir.dt.int32
        for h in range(2):
            hs = "sc" if h == 0 else "sd"

            @block.sync
            def _(sy, h=h, hs=hs):
                wait(sy, "t", ln_tr[4 * h + 3])
                inc(hs, sy.dma_start(
                    out=xnt_in[h][:, :].rearrange("(i p) t -> p i t", p=128),
                    in_=BIGA[:, :].rearrange("p (i t) -> p i t", t=1024)[:, :, 512 * h:512 * (h + 1)]), 16)
            sc_x = cnt[hs]

            @block.gpsimd
            def _(g, h=h, sc_x=sc_x, hs=hs):
                wait(g, hs, sc_x)
                inc("c", g.collective_compute(
                    "AllGather", ALU.bypass, replica_groups=[list(range(NCORES))],
                    ins=[xnt_in[h].ap().opt()],
                    outs=[xnt_out[h].ap().opt()]), 1)
        c_ag1 = cnt["c"]

        if "xn" in tap_h:
            @block.gpsimd
            def _(g):
                wait(g, "c", c_ag1)
                for i in range(16):
                    inc("o2", g.dma_start(out=tap_h["xn"][128 * i:128 * (i + 1), :],
                                          in_=xnt_out[128 * i:128 * (i + 1), :]), 16)

        strm = [BIGA, XSB0]
        pe_sb_done = []
        rope_done = [0]
        rope_half = [0, 0]
        rope_a0 = [0]
        rope_a0_half = [0, 0]
        ropemk = {}
        tm_dma = [0]
        ds_dma = [0]
        ag2 = [0, 0]
        g_v1 = [0]
        attn_av_done = [0, 0]
        aots_dma = [("u00", 0), ("u01", 0)]
        aos_free = [0]
        rope_ch_done = [0]

        pending_chain = [None]
        ct_last = {}

        def emit_phase2_half(b):
            rope_ch_done[0] = max(rope_ch_done[0], cnt["g"])
            for sbi in range(4):
                gsb = 4 * b + sbi
                g0 = 1024 * sbi
                cur = strm[gsb % 2]

                sbh = [("sb", "sbA"), ("sb2", "sb2A")][gsb % 2]
                sb_ldh = [0, 0]

                @block.scalar
                def _(sy, gsb=gsb, cur=cur, g0=g0, sbh=sbh):
                    if pe_sb_done:
                        wait(sy, "p", pe_sb_done[-1])
                    for h in range(2):
                        wait(sy, "c", min(h + 1, c_ag1))
                        sb_ldh[h] = inc(sbh[h], sy.dma_start(
                            out=cur[:, :].rearrange("p (i t) -> p i t", t=1024)[:, :, 512 * h:512 * (h + 1)],
                            in_=xnt_out[h][E * gsb:E * (gsb + 1), :].rearrange("(i p) t -> p i t", p=128)), 16)

                mt_mm = [0] * 8
                mt_ev = [0] * 8
                for j in range(8):
                    @block.tensor
                    def _(t, gsb=gsb, cur=cur, j=j, sbh=sbh):
                        if j == 0:
                            wait(t, sbh[0], sb_ldh[0])
                            if gsb == 0:
                                wait(t, "w", w_all)
                                wait(t, "w2", w2_all)
                                wait(t, "g", g_consts)
                        if j == 4:
                            wait(t, sbh[1], sb_ldh[1])
                        vd, pg = PS[4 + (j % 2)], PS[6 + (j % 2)]
                        wait_banks(t, [4 + (j % 2), 6 + (j % 2)])
                        for i in range(16):
                            lhs = cur[:, 1024 * i + 128 * j:1024 * i + 128 * (j + 1)]
                            t.matmul(vd[:, :], lhs, WVD[:, 512 * i:512 * (i + 1)],
                                     start=(i == 0), stop=False)
                            t.matmul(pg[:, 0:8], lhs, GW8[:, 16 * i:16 * i + 8],
                                     start=(i == 0), stop=(i == 15))
                        mt_mm[j] = inc("p", t.matmul(vd[:, :], ONES1[:, :], BVD[:, :],
                                                     start=False, stop=True), 1)

                    @block.vector
                    def _(v, sbi=sbi, b=b, j=j):
                        vd, pg = PS[4 + (j % 2)], PS[6 + (j % 2)]
                        wait(v, "p", mt_mm[j])
                        if j % 4 == 0:
                            wait(v, "ds", ds_dma[-1])
                            if j == 0 and b == 1 and sbi == 0:
                                wait(v, "g", g_v1[0])
                        for a in range(2):
                            ch = (a * 32 + 8 * sbi + j) * 129
                            v.tensor_copy(V[:, ch:ch + 128], vd[:, 128 * a:128 * (a + 1)])
                        v.tensor_copy(DSTG[:, 256 * j:256 * (j + 1)], vd[:, 256:512])
                        mt_ev[j] = inc("v", v.tensor_copy(LG[:, 8 * j:8 * (j + 1)], pg[:, 0:8]), 1)
                        bank_free[4 + (j % 2)] = ("v", mt_ev[j])
                        bank_free[6 + (j % 2)] = ("v", mt_ev[j])

                q_mm = [0] * 4
                k_mm = [0] * 4

                @block.tensor
                def _(t, cur=cur, sbh=sbh):
                    for a in range(2):
                        for cch in range(2):
                            ps = PS[2 * a + cch]
                            wait_banks(t, [2 * a + cch])
                            wait(t, sbh[cch], sb_ldh[cch])
                            for i in range(16):
                                mm = t.matmul(ps[:, :],
                                              WQT[:, 256 * i + 128 * a:256 * i + 128 * (a + 1)],
                                              cur[:, 1024 * i + 512 * cch:1024 * i + 512 * (cch + 1)],
                                              start=(i == 0), stop=(i == 15))
                            q_mm[2 * a + cch] = inc("p", mm, 1)

                ls_gate = [0]

                @block.vector
                def _(v, sbi=sbi):
                    for a in range(2):
                        for cch in range(2):
                            ps = PS[2 * a + cch]
                            wait(v, "p", q_mm[2 * a + cch])
                            cc = inc("v", v.tensor_scalar(
                                Q[:, 4096 * a + 1024 * sbi + 512 * cch:
                                  4096 * a + 1024 * sbi + 512 * (cch + 1)],
                                ps[:, :], BQ[:, a:a + 1], None, ALU.add), 1)
                            bank_free[2 * a + cch] = ("v", cc)
                    lg3 = LG[:, :].rearrange("p (g e) -> p g e", e=8)
                    m1b = M1[:, :].rearrange("p (g o) -> p g o", o=1).broadcast_to([128, 8, 8])
                    a3 = T2A[:, :].rearrange("p (g e) -> p g e", e=8)
                    wait(v, "v")
                    inc("v", v.reduce_max(M1[:, :], lg3, axis=AX.X), 1)
                    wait(v, "v")
                    ls_gate[0] = inc("v", v.tensor_tensor(a3, lg3, m1b, ALU.subtract), 1)

                expl_done = [0]

                @block.scalar
                def _(s):
                    wait(s, "v", ls_gate[0])
                    expl_done[0] = inc("a", s.activation(T2B[:, :], T2A[:, :], AF.Exp), 1)

                cw_ready = [0]
                CWS = CWSd[gsb % 2]

                @block.vector
                def _(v, gsb=gsb, CWS=CWS):
                    a3 = T2A[:, :].rearrange("p (g e) -> p g e", e=8)
                    c3 = T2C[:, :].rearrange("p (g e) -> p g e", e=8)
                    cw3 = CWS[:, :].rearrange("p (g e) -> p g e", e=8)
                    wait(v, "p", ct_last.get(gsb % 2, 0))
                    m2b = M2[:, :].rearrange("p (g o) -> p g o", o=1).broadcast_to([128, 8, 8])
                    reb = RE[:, :].rearrange("p (g o) -> p g o", o=1).broadcast_to([128, 8, 8])
                    wait(v, "v", ls_gate[0])
                    inc("v", v.tensor_scalar(T2C[:, :], T2A[:, :], 0.0, -1e30,
                                             ALU.is_ge, ALU.mult), 1)
                    wait(v, "v")
                    inc("v", v.tensor_add(T2C[:, :], T2C[:, :], T2A[:, :]), 1)
                    wait(v, "v")
                    inc("v", v.reduce_max(M2[:, :], c3, axis=AX.X), 1)
                    wait(v, "v")
                    inc("v", v.tensor_tensor(c3, a3, m2b, ALU.is_ge), 1)
                    wait(v, "a", expl_done[0])
                    wait(v, "v")
                    inc("v", v.tensor_mul(T2A[:, :], T2C[:, :], T2B[:, :]), 1)
                    wait(v, "v")
                    inc("v", v.reduce_sum(DEN[:, :], a3, axis=AX.X), 1)
                    wait(v, "v")
                    inc("v", v.reciprocal(RE[:, :], DEN[:, :]), 1)
                    wait(v, "v")
                    cw_ready[0] = inc("v", v.tensor_tensor(cw3, a3, reb, ALU.mult), 1)

                @block.tensor
                def _(t, cur=cur):
                    for a in range(2):
                        for cch in range(2):
                            ps = PS[2 * a + cch]
                            wait_banks(t, [2 * a + cch])
                            for i in range(16):
                                mm = t.matmul(ps[:, :],
                                              WKT[:, 256 * i + 128 * a:256 * i + 128 * (a + 1)],
                                              cur[:, 1024 * i + 512 * cch:1024 * i + 512 * (cch + 1)],
                                              start=(i == 0), stop=(i == 15))
                            k_mm[2 * a + cch] = inc("p", mm, 1)

                @block.vector
                def _(v, sbi=sbi):
                    for a in range(2):
                        for cch in range(2):
                            ps = PS[2 * a + cch]
                            wait(v, "p", k_mm[2 * a + cch])
                            cc = inc("v", v.tensor_scalar(
                                K[:, 4096 * a + 1024 * sbi + 512 * cch:
                                  4096 * a + 1024 * sbi + 512 * (cch + 1)],
                                ps[:, :], BK[:, a:a + 1], None, ALU.add), 1)
                            bank_free[2 * a + cch] = ("v", cc)

                def make_chain(cur=cur, gsb=gsb, cw_ready=cw_ready, CWS=CWS):
                    ct_mm = [0] * 8
                    emm = [0] * 2
                    tmat_mm = [0] * 2
                    cwt_done = [0]
                    cwx_done = [0]
                    tm_ready = [0]

                    ct_mm = [0] * 8

                    @block.tensor
                    def _(t):
                        wait(t, "v", cw_ready[0])
                        for jj in range(8):
                            ps = PS[4 + jj // 4]
                            wait_banks(t, [4 + jj // 4])
                            ct_mm[jj] = inc("p", t.transpose(
                                ps[0:8, 128 * (jj % 4):128 * (jj % 4) + 128],
                                CWS[:, 8 * jj:8 * jj + 8], ID128[:, :]), 1)
                        ct_last[gsb % 2] = ct_mm[7]

                    cwt_done = [0]

                    @block.vector
                    def _(v):
                        for jj in range(8):
                            wait(v, "p", ct_mm[jj])
                            cc = inc("v", v.tensor_copy(CWT[:, 128 * jj:128 * (jj + 1)],
                                                        PS[4 + jj // 4][0:8,
                                                        128 * (jj % 4):128 * (jj % 4) + 128]), 1)
                            if jj % 4 == 3:
                                bank_free[4 + jj // 4] = ("v", cc)
                        cwt_done[0] = cnt["v"]

                    emm = [0] * 2

                    @block.tensor
                    def _(t):
                        wait(t, "v", cwt_done[0])
                        for cch in range(2):
                            ps = PS[6 + cch]
                            wait_banks(t, [6 + cch])
                            emm[cch] = inc("p", t.matmul(ps[:, :], EXM[:, :],
                                                         CWT[:, 512 * cch:512 * (cch + 1)],
                                                         start=True, stop=True), 1)

                    cwx_done = [0]

                    @block.vector
                    def _(v):
                        for cch in range(2):
                            wait(v, "p", emm[cch])
                            cc = inc("v", v.tensor_copy(CWXS[:, 512 * cch:512 * (cch + 1)],
                                                        PS[6 + cch][:, :]), 1)
                            bank_free[6 + cch] = ("v", cc)
                        cwx_done[0] = cnt["v"]

                    tmat_mm = [0] * 2

                    @block.tensor
                    def _(t, cur=cur):
                        for cch in range(2):
                            ps = PS[cch]
                            wait_banks(t, [cch])
                            for i in range(16):
                                mm = t.matmul(ps[:, :], AMT[:, 128 * i:128 * (i + 1)],
                                              cur[:, 1024 * i + 512 * cch:1024 * i + 512 * (cch + 1)],
                                              start=(i == 0), stop=(i == 15))
                            tmat_mm[cch] = inc("p", mm, 1)
                        pe_sb_done.append(cnt["p"])

                    tm_ready = [0]

                    @block.vector
                    def _(v):
                        wait(v, "v", cwx_done[0])
                        wait(v, "tm", tm_dma[-1])
                        for cch in range(2):
                            wait(v, "p", tmat_mm[cch])
                            cc = inc("v", v.tensor_tensor(TMB[:, 512 * cch:512 * (cch + 1)],
                                                          PS[cch][:, :],
                                                          CWXS[:, 512 * cch:512 * (cch + 1)],
                                                          ALU.mult), 1)
                            bank_free[cch] = ("v", cc)
                        tm_ready[0] = cnt["v"]


                    @block.sync
                    def _(sy, gsb=gsb):
                        wait(sy, "v", tm_ready[0])
                        tm_dma.append(inc("tm", sy.dma_start(
                            out=tmd[:, 1024 * gsb:1024 * (gsb + 1)], in_=TMB[:, :]), 16))

                if pending_chain[0] is not None:
                    pending_chain[0]()
                pending_chain[0] = make_chain

                kev_done = [cnt["v"]]
                gp_prev = [cnt["g"]]
                for a in range(2):
                  for ch in range(2):
                    @block.sync
                    def _(sy, g0=g0, ch=ch):
                        wait(sy, "g", rope_ch_done[0])
                        inc("tb", sy.dma_start(out=COSS[:, 0:512],
                                               in_=cos2_h[:, g0 + 512 * ch:g0 + 512 * (ch + 1)]), 16)
                        inc("tb", sy.dma_start(out=COSS[:, 512:1024],
                                               in_=sinpm_h[:, g0 + 512 * ch:g0 + 512 * (ch + 1)]), 16)
                    tb_ld = cnt["tb"]

                    for side, X in ((0, Q), (1, K)):
                        if True:
                            cols = 4096 * a + 1024 * sbi + 512 * ch
                            a2_done = [0]

                            @block.gpsimd
                            def _(g, X=X, cols=cols, tb_ld=tb_ld,
                                  first=(side == 0), vfirst=(side == 0 and a == 0 and ch == 0),
                                  kev=kev_done[0]):
                                if first:
                                    wait(g, "tb", tb_ld)
                                if vfirst:
                                    wait(g, "v", kev)
                                wait(g, "g")
                                inc("g", g.tensor_tensor(RA[:, 0:512], X[:, cols:cols + 512],
                                                         COSS[:, 0:512], ALU.mult), 1)
                                wait(g, "g")
                                inc("g", g.tensor_tensor(RA[:, 512:1024], X[:, cols:cols + 512],
                                                         COSS[:, 512:1024], ALU.mult), 1)
                                wait(g, "g")
                                g.tensor_copy(RT[0:64, :], RA[64:128, 512:1024])
                                inc("g", g.tensor_copy(RT[64:128, :], RA[0:64, 512:1024]), 1)
                                wait(g, "g")
                                gp_prev[0] = inc("g", g.tensor_tensor(
                                    X[:, cols:cols + 512], RA[:, 0:512],
                                    RT[:, :], ALU.add), 1)
                    rope_ch_done[0] = cnt["g"]
                    ropemk[(b, a, 2 * sbi + ch)] = cnt["g"]
                  if a == 0:
                      rope_a0[0] = cnt["g"]
                rope_done.append(cnt["g"])

                @block.sync
                def _(sy, gsb=gsb):
                    wait(sy, "v", mt_ev[7])
                    ds_dma.append(inc("ds", sy.dma_start(
                        out=dsc[1024 * gsb:1024 * (gsb + 1), :].rearrange("(j p) o -> p j o", p=128),
                        in_=DSTG[:, :].rearrange("p (j o) -> p j o", o=FS)), 16))

            if pending_chain[0] is not None:
                pending_chain[0]()
                pending_chain[0] = None
            rope_half[b] = rope_done[-1]
            rope_a0_half[b] = rope_a0[0]

        mask_ready = [0]

        def emit_attention_half(b, mid_hook=None):
            aot_last = [0]
            if b == 0:
                @block.gpsimd
                def _(g):
                    g.wait_ge(sems["v"], cnt["v"])
                    inc("g", g.memset(LN0[:, :], 1.0), 1)
                    wait(g, "g")
                    for dl in range(4):
                        mm = g.affine_select(LN0[:, 512 * dl:512 * (dl + 1)],
                                             LN0[:, 512 * dl:512 * (dl + 1)],
                                             [[1, 512]], ALU.is_ge, 0.0,
                                             base=-128 * dl, channel_multiplier=-1)
                    mask_ready[0] = inc("g", mm, 1)
            for a in range(2):
                sq_pe = [0] * 8
                red_dve = [0] * 8
                for idx in range(8):
                    side = idx // 4
                    cc = idx % 4
                    X = (Q, K)[side]
                    RMX = (RMXQ, RMXK)[side]
                    sq_dve = [0]

                    @block.vector
                    def _(v, X=X, cc=cc, idx=idx, a=a, b=b):
                        if idx == 0:
                            wait(v, "av", cnt["av"])
                        v.wait_ge(sems["g"], ropemk[(b, a, 2 * cc + 1)])
                        if idx >= 1:
                            wait(v, "p", sq_pe[idx - 1])
                        cols = 4096 * a + 1024 * cc
                        sq_dve[0] = inc("v", v.scalar_tensor_tensor(
                            SHB[:, :], X[:, cols:cols + 1024], 1.0,
                            X[:, cols:cols + 1024], ALU.mult, ALU.mult), 1)

                    @block.tensor
                    def _(t, idx=idx):
                        wait(t, "v", sq_dve[0])
                        if idx == 0:
                            wait_banks(t, [4, 5])
                        else:
                            wait(t, "v", red_dve[idx - 1])
                        t.matmul(PS[4][:, :], ONES128[:, :], SHB[:, 0:512],
                                 start=True, stop=True)
                        sq_pe[idx] = inc("p", t.matmul(PS[5][:, :], ONES128[:, :],
                                                       SHB[:, 512:1024],
                                                       start=True, stop=True), 1)

                    @block.vector
                    def _(v, idx=idx, RMX=RMX):
                        wait(v, "p", sq_pe[idx])
                        if idx % 4 == 0:
                            inc("v", v.reduce_max(RMX[:, :], PS[4][:, :], axis=AX.X), 1)
                        else:
                            inc("v", v.reduce_max(RTM[:, :], PS[4][:, :], axis=AX.X), 1)
                            wait(v, "v")
                            inc("v", v.tensor_max(RMX[:, :], RMX[:, :], RTM[:, :]), 1)
                        wait(v, "v")
                        inc("v", v.reduce_max(RTM[:, :], PS[5][:, :], axis=AX.X), 1)
                        wait(v, "v")
                        red_dve[idx] = inc("v", v.tensor_max(RMX[:, :], RMX[:, :], RTM[:, :]), 1)
                bank_free[4] = ("v", red_dve[7])
                bank_free[5] = ("v", red_dve[7])

                shp_v = [0]

                @block.vector
                def _(v):
                    wait(v, "v", red_dve[7])
                    shp_v[0] = inc("v", v.tensor_tensor(RTM[:, :], RMXQ[:, :], RMXK[:, :],
                                                        ALU.mult), 1)

                shp_a = [0]

                @block.scalar
                def _(s):
                    wait(s, "v", shp_v[0])
                    inc("a", s.activation(LNT[:, :], RTM[:, :], AF.Ln), 1)
                    wait(s, "a")
                    shp_a[0] = inc("a", s.activation(NEGSH[:, :], LNT[:, :], AF.Exp,
                                                     scale=0.5, bias=LBC[:, :]), 1)

                shp_neg = [0]

                @block.vector
                def _(v):
                    wait(v, "a", shp_a[0])
                    shp_neg[0] = inc("v", v.tensor_scalar_mul(NEGSH[:, :], NEGSH[:, :], -1.0), 1)

                e_cnt = {}
                m_cnt = {}
                av_last_qb = [0]
                PSS = [PS[0], PS[1], PS[6], PS[7]]
                slot_e = [0, 0, 0, 0]
                slot_av = [0, 0, 0, 0]

                def pbslot(ping):
                    buf = PTB if ping < 2 else SHB
                    off = 512 * (ping % 2)
                    return buf[:, off:off + 512]

                for qb in range(8):
                    nk = 4 * (qb + 1)
                    ao_gate = cnt["ao"]

                    def emit_av(t, kav, qb=qb, nk=nk, a=a):
                        diag = kav >= 4 * qb
                        if diag:
                            t.wait_ge(sems["m"], m_cnt[(qb, kav)])
                        else:
                            t.wait_ge(sems["e"], e_cnt[(qb, kav)])
                        pb = pbslot(kav % 4)
                        for qs in range(4):
                            psO = PS[2 + qs]
                            mm = t.matmul(psO[:, 0:129],
                                          pb[:, 128 * qs:128 * (qs + 1)],
                                          V[:, (a * 32 + kav) * 129:(a * 32 + kav) * 129 + 129],
                                          start=(kav == 0),
                                          stop=(kav == nk - 1))
                        slot_av[kav % 4] = cnt["av"] + 1
                        return inc("av", mm, 1)

                    for kc in range(nk):
                        @block.tensor
                        def _(t, kc=kc, qb=qb, nk=nk, a=a, b=b):
                            ping = kc % 4
                            if slot_e[ping]:
                                t.wait_ge(sems["e"], slot_e[ping])
                            elif qb == 0 and kc == 0:
                                wait_banks(t, [0, 1, 2, 3, 4, 5, 6, 7])
                            if kc == 0:
                                t.wait_ge(sems["g"], ropemk[(b, a, qb)])
                                t.wait_ge(sems["ao"], ao_gate)
                            mm = t.matmul(PSS[ping][:, :],
                                          K[:, 4096 * a + 128 * kc:4096 * a + 128 * (kc + 1)],
                                          Q[:, 4096 * a + 512 * qb:4096 * a + 512 * (qb + 1)],
                                          start=True, stop=True)
                            inc("s2", mm, 1)
                            if kc >= 1:
                                emit_av(t, kc - 1)
                        s2_now = cnt["s2"]

                        @block.scalar
                        def _(s, kc=kc, qb=qb, s2_now=s2_now):
                            ping = kc % 4
                            if qb == 0 and kc == 0:
                                wait(s, "v", shp_neg[0])
                            wait(s, "s2", s2_now)
                            if slot_av[ping]:
                                s.wait_ge(sems["av"], slot_av[ping])
                            e_cnt[(qb, kc)] = inc("e", s.activation(
                                pbslot(ping), PSS[ping][:, :], AF.Exp,
                                bias=NEGSH[:, :], scale=SCALE), 1)
                            slot_e[ping] = e_cnt[(qb, kc)]

                        if kc >= 4 * qb:
                            @block.vector
                            def _(v, kc=kc, qb=qb):
                                wait(v, "e", e_cnt[(qb, kc)])
                                if qb == 0 and kc == 0:
                                    wait(v, "g", mask_ready[0])
                                dl = kc - 4 * qb
                                m_cnt[(qb, kc)] = inc("m", v.tensor_tensor(
                                    pbslot(kc % 4), pbslot(kc % 4),
                                    LN0[:, 512 * dl:512 * (dl + 1)], ALU.mult), 1)

                    @block.tensor
                    def _(t, qb=qb, nk=nk):
                        av_last_qb[0] = emit_av(t, nk - 1)

                    ao_now = [0]

                    @block.vector
                    def _(v, qb=qb, nk=nk):
                        wait(v, "av", av_last_qb[0])
                        wait(v, "t", aos_free[0])
                        wait(v, "ao", cnt["ao"])
                        for qs in range(4):
                            inc("v", v.reciprocal(RCP[:, qs:qs + 1], PS[2 + qs][:, 128:129]), 1)
                        wait(v, "v")
                        for qs in range(4):
                            last = v.tensor_scalar(AOS[:, 128 * qs:128 * (qs + 1)],
                                                   PS[2 + qs][:, 0:128], RCP[:, qs:qs + 1],
                                                   None, ALU.mult)
                            if qs < 3:
                                inc("v", last, 1)
                        ao_now[0] = inc("ao", last, 1)

                    @block.sync
                    def _(sy, qb=qb, a=a, b=b):
                        wait(sy, "ao", ao_now[0])
                        pp = qb % 2
                        sy.wait_ge(sems[aots_dma[pp][0]], aots_dma[pp][1])
                        for qs in range(4):
                            aos_free[0] = inc("t", sy.dma_start_transpose(
                                AOTS[:, 512 * pp + 128 * qs:512 * pp + 128 * (qs + 1)],
                                AOS[:, 128 * qs:128 * (qs + 1)]), 16)
                        wait(sy, "t", aos_free[0])
                        us = f"u{b}{pp}"
                        aots_dma[pp] = (us, inc(us, sy.dma_start(
                            out=aot_in[b][128 * a:128 * (a + 1), 512 * qb:512 * (qb + 1)],
                            in_=AOTS[:, 512 * pp:512 * pp + 512]), 16))

                for bi in (0, 1, 6, 7):
                    bank_free[bi] = ("e", cnt["e"])
                for bi in (2, 3, 4, 5):
                    bank_free[bi] = ("ao", cnt["ao"])
                if a == 0 and mid_hook is not None:
                    mid_hook()
                    bank_free[0] = ("v", fin_ev[(3, 6)])
                    bank_free[1] = ("v", fin_ev[(3, 7)])

            attn_av_done[b] = cnt["av"]

            @block.gpsimd
            def _(g, b=b):
                wait(g, f"u{b}0", cnt[f"u{b}0"])
                wait(g, f"u{b}1", cnt[f"u{b}1"])
                ag2[b] = inc("c", g.collective_compute(
                    "AllGather", ALU.bypass, replica_groups=[list(range(NCORES))],
                    ins=[aot_in[b].ap().opt()],
                    outs=[aot_out[b].ap().opt()]), 1)
                if b == 0:
                    g.wait_ge(sems["av"], attn_av_done[0])
                    g_v1[0] = inc("g", g.memset(V[:, :], 1.0), 1)

        av_prev = {}

        def emit_qkv_taps(b):
            if not ({"qk", "vb"} & set(tap_h)):
                return

            @block.gpsimd
            def _(g, b=b):
                g.wait_ge(sems["g"], rope_half[b])
                if "qk" in tap_h:
                    inc("o2", g.dma_start(out=tap_h["qk"][:, 16384 * b:16384 * b + 8192],
                                         in_=Q[:, :]), 16)
                    inc("o2", g.dma_start(out=tap_h["qk"][:, 16384 * b + 8192:16384 * (b + 1)],
                                         in_=K[:, :]), 16)
                if "vb" in tap_h:
                    inc("o2", g.dma_start(out=tap_h["vb"][:, 8256 * b:8256 * (b + 1)],
                                         in_=V[:, :]), 16)

        def emit_attn_tap(b):
            if "attn" not in tap_h:
                return

            @block.gpsimd
            def _(g, b=b):
                wait(g, "c", ag2[b])
                inc("o2", g.dma_start(out=tap_h["attn"][:, S * b:S * (b + 1)],
                                      in_=aot_in[b][:, :]), 16)

        fin_pe_j = {}
        fin_pe = {}
        fin_ev = {}
        out_done = {}
        ft_done = {}
        asb = [BIGA, XSB0]
        xcb = [LN0, LN1]
        dsb = [XNJ, DS2B]
        tmbb = [TMB, PTB]

        def emit_final_tb(tb, mid):
            bb = tb // 4
            c4 = 1024 * (tb % 4)
            sbs = "sb" if tb % 2 == 0 else "sb2"
            xs = "x" if tb % 2 == 0 else "x2"
            xcb_t = LN1 if mid else xcb[tb % 2]
            tmb_t = TMB if mid else tmbb[tb % 2]

            @block.sync
            def _(sy, tb=tb, bb=bb, c4=c4, sbs=sbs):
                wait(sy, "c", ag2[bb])
                if tb == 0:
                    wait(sy, "p", pe_sb_done[-1])
                if tb - 2 in fin_pe:
                    wait(sy, "p", fin_pe[tb - 2])
                inc(sbs, sy.dma_start(
                    out=asb[tb % 2][:, :].rearrange("p (i t) -> p i t", t=1024),
                    in_=aot_out[bb][:, c4:c4 + 1024].rearrange("(i p) t -> p i t", p=128)), 16)
            f_ld = cnt[sbs]

            @block.scalar
            def _(s, tb=tb, xs=xs, mid=mid, xcb_t=xcb_t, tmb_t=tmb_t):
                if mid:
                    if tb == 0:
                        wait(s, "v", cnt["v"])
                        wait(s, "tm", tm_dma[-1])
                        wait(s, "ds", ds_dma[-1])
                    else:
                        wait(s, "v", fin_ev[(tb - 1, 7)])
                        wait(s, "p", fin_pe[tb - 1])
                else:
                    if tb in (4, 5):
                        wait(s, "m", cnt["m"])
                        s.wait_ge(sems["av"], attn_av_done[1])
                        wait(s, "e", cnt["e"])
                        wait(s, "tm", tm_dma[-1])
                        wait(s, "ds", ds_dma[-1])
                    if (tb - 2, 7) in fin_ev:
                        wait(s, "v", fin_ev[(tb - 2, 7)])
                    if tb - 2 in fin_pe:
                        wait(s, "p", fin_pe[tb - 2])
                inc(xs, s.dma_start(out=xcb_t[:, :].rearrange("p (j o) -> p j o", o=FS),
                                    in_=xcol[1024 * tb:1024 * (tb + 1), :]
                                    .rearrange("(j p) o -> p j o", p=128)), 16)
                inc(xs, s.dma_start(out=dsb[tb % 2][:, :].rearrange("p (j o) -> p j o", o=FS),
                                    in_=dsc[1024 * tb:1024 * (tb + 1), :]
                                    .rearrange("(j p) o -> p j o", p=128)), 16)
                inc(xs, s.dma_start(out=tmb_t[:, :],
                                    in_=tmd[:, 1024 * tb:1024 * (tb + 1)]), 16)
            f_x = cnt[xs]

            for j in range(8):
                @block.tensor
                def _(t, tb=tb, f_ld=f_ld, f_x=f_x, sbs=sbs, xs=xs, j=j, tmb_t=tmb_t):
                    if j == 0:
                        wait(t, sbs, f_ld)
                        wait(t, xs, f_x)
                        if tb == 0:
                            wait_banks(t, [0, 1])
                            wait(t, "w", wot_ld)
                    ps = PS[j % 2]
                    if (tb, j - 2) in fin_ev:
                        wait(t, "v", fin_ev[(tb, j - 2)])
                    elif (tb - 1, j + 6) in fin_ev:
                        wait(t, "v", fin_ev[(tb - 1, j + 6)])
                    cur = asb[tb % 2]
                    for i in range(16):
                        t.matmul(ps[:, 0:256],
                                 cur[:, 1024 * i + 128 * j:1024 * i + 128 * (j + 1)],
                                 WQT[:, 256 * i:256 * (i + 1)], start=(i == 0), stop=False)
                    t.matmul(ps[:, 0:256], tmb_t[:, 128 * j:128 * (j + 1)], BM[:, :],
                             start=False, stop=False)
                    fin_pe_j[(tb, j)] = inc("p", t.matmul(ps[:, 0:256], ONES1[:, :], BOR[:, :],
                                                          start=False, stop=True), 1)

                @block.vector
                def _(v, tb=tb, j=j, xcb_t=xcb_t):
                    ps = PS[j % 2]
                    wait(v, "p", fin_pe_j[(tb, j)])
                    ft = FT[:, 256 * (j % 2):256 * (j % 2) + 256]
                    if (tb, j - 2) in ft_done:
                        wait(v, "ft", ft_done[(tb, j - 2)])
                    elif (tb - 1, j + 6) in ft_done:
                        wait(v, "ft", ft_done[(tb - 1, j + 6)])
                    inc("v", v.tensor_tensor(ft, ps[:, 0:256],
                                             dsb[tb % 2][:, 256 * j:256 * (j + 1)], ALU.add), 1)
                    wait(v, "v")
                    fin_ev[(tb, j)] = inc("v", v.tensor_tensor(
                        ft, ft, xcb_t[:, 256 * j:256 * (j + 1)], ALU.add), 1)

                osem = "o" if j % 2 == 0 else "o3"

                @block.sync
                def _(sy, tb=tb, j=j, osem=osem):
                    wait(sy, "v", fin_ev[(tb, j)])
                    out_done[(tb, j)] = (osem, inc(osem, sy.dma_start(
                        out=out_h[1024 * tb + 128 * j:1024 * tb + 128 * (j + 1), :],
                        in_=FT[:, 256 * (j % 2):256 * (j % 2) + 256]), 16))

                @block.gpsimd
                def _(g, tb=tb, j=j):
                    g.wait_ge(sems[out_done[(tb, j)][0]], out_done[(tb, j)][1])
                    ft_done[(tb, j)] = inc("ft", g.nop(), 1)
            fin_pe[tb] = cnt["p"]


        if slvl >= 1:
            emit_phase2_half(0)
            emit_qkv_taps(0)
        if slvl >= 2:
            emit_attention_half(0)
            emit_attn_tap(0)
        if slvl >= 3:
            emit_phase2_half(1)
            emit_qkv_taps(1)

        if slvl >= 3:
            @block.gpsimd
            def _(g):
                g.wait_ge(sems["p"], pe_sb_done[-1])
                inc("w", g.dma_start(out=WQT[:, :].rearrange("p (i f) -> p i f", f=FS),
                                     in_=wot_h[:, :].rearrange("(i p) f -> p i f", p=128)), 16)
            wot_ld = cnt["w"]
        if slvl >= 4:
            mid = None
            if slvl >= 5:
                def mid():
                    for tb in range(4):
                        emit_final_tb(tb, True)
            emit_attention_half(1, mid_hook=mid)
            emit_attn_tap(1)

        if {"lg", "cws", "cwt", "cwx"} & set(tap_h):
            @block.gpsimd
            def _(g):
                g.wait_ge(sems["v"], cnt["v"])
                for nm, buf in (("lg", LG), ("cws", CWS), ("cwt", CWT), ("cwx", CWXS)):
                    if nm in tap_h:
                        inc("o2", g.dma_start(out=tap_h[nm][:, :], in_=buf[:, :]), 16)

        if {"dense", "tmask"} & set(tap_h):
            @block.gpsimd
            def _(g):
                wait(g, "tm", cnt["tm"])
                wait(g, "ds", cnt["ds"])
                if "dense" in tap_h:
                    inc("o2", g.dma_start(out=tap_h["dense"][:, :], in_=dsc[:, :]), 16)
                if "tmask" in tap_h:
                    inc("o2", g.dma_start(out=tap_h["tmask"][:, :], in_=tmd[:, :]), 16)

        if slvl >= 5:
            for tb in range(4, 8):
                emit_final_tb(tb, False)

        @block.sync
        def _(sy):
            wait(sy, "o", cnt["o"])
            wait(sy, "o3", cnt["o3"])

        if cnt["o2"]:
            @block.gpsimd
            def _(g):
                wait(g, "o2", cnt["o2"])

        block.flush(nc)

    return nc


_PERM = np.concatenate([np.arange(0, 128, 2), np.arange(1, 128, 2)])


def _stage(core, ins):
    c = core
    Fc = slice(256 * c, 256 * (c + 1))
    Tc = slice(1024 * c, 1024 * (c + 1))
    xf = np.asarray(ins["L0_attn_x"], dtype=np.float32).reshape(T, E)

    def permrows(w):
        w2 = w.reshape(2, 128, E)[:, _PERM, :]
        return w2.reshape(256, E)

    wq = permrows(np.asarray(ins["L0_attn_Wq"], np.float32)[Fc])
    wk = permrows(np.asarray(ins["L0_attn_Wk"], np.float32)[Fc])
    wv = np.asarray(ins["L0_attn_Wv"], np.float32)[Fc]
    wo = np.asarray(ins["L0_attn_Wo"], np.float32)[Fc]
    wd = np.asarray(ins["L0_ffn_weight"], np.float32)[Fc]
    bq = np.asarray(ins["L0_attn_bq"], np.float32)[Fc].reshape(2, 128)[:, _PERM].T
    bk = np.asarray(ins["L0_attn_bk"], np.float32)[Fc].reshape(2, 128)[:, _PERM].T
    bv = np.asarray(ins["L0_attn_bv"], np.float32)[Fc]
    bo = np.asarray(ins["L0_attn_bo"], np.float32)[Fc].reshape(1, FS)
    aw = np.asarray(ins["L0_ffn_A_w"], np.float32).reshape(128, E)
    gw = np.asarray(ins["L0_ffn_gate_w"], np.float32)
    bmw = np.asarray(ins["L0_ffn_B_w"], np.float32)[:, Fc, :]
    bm = bmw.transpose(0, 2, 1).reshape(128, FS)

    inv = 10000.0 ** (-np.arange(64, dtype=np.float64) / 64.0)
    ang = inv[:, None] * np.arange(S, dtype=np.float64)[None, :]
    cosv = np.cos(ang).astype(np.float32)
    sinv = np.sin(ang).astype(np.float32)

    return {
        "xtok": np.ascontiguousarray(xf[Tc]),
        "xcol": np.ascontiguousarray(xf[:, Fc]),
        "wqt": np.ascontiguousarray(wq.T),
        "wkt": np.ascontiguousarray(wk.T),
        "wvdt": np.ascontiguousarray(np.concatenate([wv.T, wd.T], axis=1)),
        "wot": np.ascontiguousarray(wo.T),
        "awt": np.ascontiguousarray(aw.T),
        "gwt": np.ascontiguousarray(
            np.concatenate([gw.T, np.zeros((E, 8), np.float32)], axis=1)),
        "bq": np.ascontiguousarray(bq),
        "bk": np.ascontiguousarray(bk),
        "bvd": np.concatenate([bv, np.zeros(256, np.float32)]).reshape(1, 512),
        "bo": np.ascontiguousarray(bo),
        "bm": np.ascontiguousarray(bm),
        "cos2": np.ascontiguousarray(np.concatenate([cosv, cosv], axis=0)),
        "sinpm": np.ascontiguousarray(np.concatenate([sinv, -sinv], axis=0)),
    }


def kernel(**inputs):
    nc = build_nc()
    in_maps = [_stage(c, inputs) for c in range(NCORES)]
    res = run_bass_kernel_spmd(nc, in_maps, core_ids=list(range(NCORES)))
    out = np.concatenate([np.asarray(res.results[c]["out"]) for c in range(NCORES)], axis=1)
    return out.reshape(B, S, E).astype(np.float32)


# version 17
# speedup vs baseline: 1.1387x; 1.1304x over previous
import numpy as np
import ml_dtypes
from contextlib import ExitStack

import concourse.bass as bass
import concourse.mybir as mybir
from concourse.bass_utils import run_bass_kernel_spmd

F32 = mybir.dt.float32
BF16 = mybir.dt.bfloat16
AF = mybir.ActivationFunctionType
ALU = mybir.AluOpType
AX = mybir.AxisListType

NCORES = 8
E = 2048
HD = 128
B = 2
S = 4096
T = B * S
FS = 256
TS = 1024
SCALE = 1.0 / float(np.sqrt(HD))
LEPS = 1e-5
LB128 = float(-0.5 * np.log(128.0))



class _PInstr:
    __slots__ = ("entry",)

    def __init__(self, entry):
        self.entry = entry

    def then_inc(self, sem, value=1, **kw):
        self.entry[3].append((sem, value))
        return self


class _PEng:

    def __init__(self, ops):
        self._ops = ops

    def wait_ge(self, sem, val):
        self._ops.append(("w", sem, val, None))

    def __getattr__(self, name):
        ops = self._ops

        def call(*a, **kw):
            entry = ["op", name, (a, kw), []]
            ops.append(entry)
            return _PInstr(entry)

        return call


class _DeferredBlock:
    def __init__(self):
        self.ops = {k: [] for k in ("tensor", "scalar", "vector", "gpsimd", "sync")}

    def _mk(self, key):
        def deco(fn):
            fn(_PEng(self.ops[key]))
            return None
        return deco

    @property
    def tensor(self):
        return self._mk("tensor")

    @property
    def scalar(self):
        return self._mk("scalar")

    @property
    def vector(self):
        return self._mk("vector")

    @property
    def gpsimd(self):
        return self._mk("gpsimd")

    @property
    def sync(self):
        return self._mk("sync")

    def flush(self, nc):
        with nc.Block() as blk:
            for key in ("tensor", "scalar", "vector", "gpsimd", "sync"):
                ops = self.ops[key]
                if not ops:
                    continue

                def run(eng, ops=ops):
                    for kind, name, payload, incs in ops:
                        if kind == "w":
                            eng.wait_ge(name, payload)
                        else:
                            a, kw = payload
                            instr = getattr(eng, name)(*a, **kw)
                            for sem, val in incs:
                                instr.then_inc(sem, val)

                getattr(blk, key)(run)


_STAGES = ["ln", "p2b0", "attn0", "p2b1", "attn1", "full"]


def build_nc(taps=(), stage="full"):
    taps = set(taps)
    slvl = _STAGES.index(stage)
    nc = bass.Bass("TRN2", target_bir_lowering=False, debug=False)

    xtok = nc.dram_tensor("xtok", [TS, E], F32, kind="ExternalInput")
    xcol = nc.dram_tensor("xcol", [T, FS], F32, kind="ExternalInput")
    wqt_h = nc.dram_tensor("wqt", [E, FS], F32, kind="ExternalInput")
    wkt_h = nc.dram_tensor("wkt", [E, FS], F32, kind="ExternalInput")
    wvdt_h = nc.dram_tensor("wvdt", [E, 512], F32, kind="ExternalInput")
    wot_h = nc.dram_tensor("wot", [E, FS], F32, kind="ExternalInput")
    awt_h = nc.dram_tensor("awt", [E, 128], F32, kind="ExternalInput")
    gwt_h = nc.dram_tensor("gwt", [E, 16], F32, kind="ExternalInput")
    bq_h = nc.dram_tensor("bq", [128, 2], F32, kind="ExternalInput")
    bk_h = nc.dram_tensor("bk", [128, 2], F32, kind="ExternalInput")
    bvd_h = nc.dram_tensor("bvd", [1, 512], F32, kind="ExternalInput")
    bo_h = nc.dram_tensor("bo", [1, FS], F32, kind="ExternalInput")
    bm_h = nc.dram_tensor("bm", [128, FS], F32, kind="ExternalInput")
    cos2_h = nc.dram_tensor("cos2", [128, S], F32, kind="ExternalInput")
    sinpm_h = nc.dram_tensor("sinpm", [128, S], F32, kind="ExternalInput")
    out_h = nc.dram_tensor("out", [T, FS], F32, kind="ExternalOutput")

    xnt_in = [nc.dram_tensor(f"xnt_in{h}", [E, TS // 2], BF16, kind="Internal") for h in range(2)]
    xnt_out = [nc.dram_tensor(f"xnt_out{h}", [NCORES * E, TS // 2], BF16, kind="Internal",
                              addr_space="Shared") for h in range(2)]
    aot_in = [nc.dram_tensor(f"aot_in{b}", [FS, S], BF16, kind="Internal") for b in range(B)]
    aot_out = [nc.dram_tensor(f"aot_out{b}", [NCORES * FS, S], BF16, kind="Internal",
                              addr_space="Shared") for b in range(B)]
    dsc = nc.dram_tensor("dsc", [T, FS], BF16, kind="Internal")
    tmd = nc.dram_tensor("tmd", [128, T], BF16, kind="Internal")

    tap_h = {}
    for nm, shp in (("xn", [E, TS]), ("qk", [128, 2 * 16384]), ("vb", [128, 2 * 8256]),
                    ("dense", [T, FS]), ("tmask", [128, T]), ("attn", [FS, T]),
                    ("lg", [128, 64]), ("cws", [128, 64]), ("cwt", [8, 1024]),
                    ("cwx", [128, 1024])):
        if nm in taps:
            tap_h[nm] = nc.dram_tensor("tap_" + nm, shp, F32, kind="ExternalOutput")

    with ExitStack() as ctx:
        def sb(name, shape, dt, **kw):
            return ctx.enter_context(nc.sbuf_tensor(name, shape, dt, **kw))

        WQT = sb("WQT", [128, 4096], BF16)
        WKT = sb("WKT", [128, 4096], BF16)
        WVD = sb("WVD", [128, 8192], BF16)
        AMT = sb("AMT", [128, 2048], BF16)
        GW8 = sb("GW8", [128, 256], BF16)
        BM = sb("BM", [128, FS], BF16)
        EXM = sb("EXM", [8, 128], BF16)
        BQ = sb("BQ", [128, 2], F32)
        BK = sb("BK", [128, 2], F32)
        BVD = sb("BVD", [1, 512], BF16)
        BOR = sb("BOR", [1, FS], BF16)
        ONES1 = sb("ONES1", [1, 128], BF16)
        ONES128 = sb("ONES128", [128, 128], BF16)
        ID128 = sb("ID128", [128, 128], F32)
        EPS = sb("EPS", [128, 1], F32)
        LBC = sb("LBC", [128, 1], F32)
        COSS = sb("COSS", [128, 2048], F32)
        LN0 = sb("LN0", [128, 2048], F32)
        LN1 = sb("LN1", [128, 2048], F32)
        XNJ = sb("XNJ", [128, 2048], BF16)
        DS2B = sb("DS2B", [128, 2048], BF16)
        DSTG = sb("DSTG", [128, 2048], BF16)
        BIGA = sb("BIGA", [128, 16384], BF16, is_transpose_dest=True)
        XSB0 = sb("XSB0", [128, 16384], BF16)
        Q = sb("Q", [128, 8192], BF16)
        K = sb("K", [128, 8192], BF16)
        V = sb("V", [128, 2 * 32 * 129], BF16)
        LG = sb("LG", [128, 64], F32)
        T2A = sb("T2A", [128, 64], F32)
        T2B = sb("T2B", [128, 64], F32)
        T2C = sb("T2C", [128, 64], F32)
        CWSd = [sb("CWS0", [128, 64], F32), sb("CWS1", [128, 64], F32)]
        M1 = sb("M1", [128, 8], F32)
        M2 = sb("M2", [128, 8], F32)
        DEN = sb("DEN", [128, 8], F32)
        RE = sb("RE", [128, 8], F32)
        CWT = sb("CWT", [8, 1024], BF16)
        CWXS = sb("CWXS", [128, 1024], BF16)
        _xnj_addr = nc.lookup_mloc(XNJ).addr
        RT = nc.alloc_sbuf_tensor_at("RT", [128, 512], F32, offset=_xnj_addr)
        RA = nc.alloc_sbuf_tensor_at("RA", [128, 512], F32, offset=_xnj_addr + 2048)
        SHB = sb("SHB", [128, 1024], BF16)
        RMXQ = sb("RMXQ", [128, 1], F32)
        RMXK = sb("RMXK", [128, 1], F32)
        RTM = sb("RTM", [128, 1], F32)
        NEGSH = sb("NEGSH", [128, 1], F32)
        PTB = sb("PTB", [128, 1024], BF16)
        TMB = sb("TMB", [128, 1024], BF16)
        AOS = sb("AOS", [128, 512], BF16)
        AOTS = sb("AOTS", [128, 1024], BF16, is_transpose_dest=True)
        SSQ = sb("SSQ", [128, 1], F32)
        SUMT = sb("SUMT", [128, 1], F32)
        MEAN = sb("MEAN", [128, 1], F32)
        MSQ = sb("MSQ", [128, 1], F32)
        VAR = sb("VAR", [128, 1], F32)
        LNT = sb("LNT", [128, 1], F32)
        RSTD = sb("RSTD", [128, 1], F32)
        RCP = sb("RCP", [128, 4], F32)
        FT = sb("FT", [128, 512], F32)

        PS = [ctx.enter_context(nc.psum_tensor(f"PS{i}", [128, 512], F32)) for i in range(8)]

        sems = {}
        cnt = {}
        for nm in ("w", "w2", "x", "x2", "v", "a", "p", "g", "t", "c", "sb", "sb2", "sbA", "sb2A",
                   "sc", "sd", "tm", "ds", "o", "o3", "o2", "ft", "s2", "e", "m", "av", "ao", "tb",
                   "u00", "u01", "u10", "u11"):
            sems[nm] = ctx.enter_context(nc.semaphore("s_" + nm))
            cnt[nm] = 0

        def inc(nm, instr, k=1):
            instr.then_inc(sems[nm], k)
            cnt[nm] += k
            return cnt[nm]

        def wait(eng, nm, val=None):
            eng.wait_ge(sems[nm], cnt[nm] if val is None else val)

        bank_free = [("v", 0)] * 8

        def wait_banks(t, idxs):
            need = {}
            for i in idxs:
                nm, c = bank_free[i]
                need[nm] = max(need.get(nm, 0), c)
            for nm, c in need.items():
                if c > 0:
                    t.wait_ge(sems[nm], c)

        block = _DeferredBlock()

        @block.gpsimd
        def _(g):
            inc("w", g.dma_start(out=WQT[:, :].rearrange("p (i f) -> p i f", f=FS),
                                 in_=wqt_h[:, :].rearrange("(i p) f -> p i f", p=128)), 16)
            inc("w", g.dma_start(out=WKT[:, :].rearrange("p (i f) -> p i f", f=FS),
                                 in_=wkt_h[:, :].rearrange("(i p) f -> p i f", p=128)), 16)
            inc("w", g.dma_start(out=WVD[:, :].rearrange("p (i f) -> p i f", f=512),
                                 in_=wvdt_h[:, :].rearrange("(i p) f -> p i f", p=128)), 16)
            inc("w", g.dma_start(out=AMT[:, :].rearrange("p (i f) -> p i f", f=128),
                                 in_=awt_h[:, :].rearrange("(i p) f -> p i f", p=128)), 16)
            inc("w", g.dma_start(out=GW8[:, :].rearrange("p (i f) -> p i f", f=16),
                                 in_=gwt_h[:, :].rearrange("(i p) f -> p i f", p=128)), 16)
            inc("w", g.dma_start(out=BM[:, :], in_=bm_h[:, :]), 16)
            inc("w", g.dma_start(out=BVD[:, :], in_=bvd_h[:, :]), 16)
            inc("w", g.dma_start(out=BOR[:, :], in_=bo_h[:, :]), 16)
            g.memset(ONES1[:, :], 1.0)
            g.memset(ONES128[:, :], 1.0)
            g.memset(V[:, :], 1.0)
            inc("g", g.memset(ID128[:, :], 1.0), 1)
            wait(g, "g")
            inc("g", g.affine_select(ID128[:, :], ID128[:, :], [[-1, 128]], ALU.is_equal, 0.0,
                                     base=0, channel_multiplier=1), 1)
            inc("g", g.memset(EXM[:, :], 1.0), 1)
            wait(g, "g")
            inc("g", g.affine_select(EXM[:, :], EXM[:, :], [[1, 128]], ALU.is_ge, 0.0,
                                     base=0, channel_multiplier=-16), 1)
            wait(g, "g")
            inc("g", g.affine_select(EXM[:, :], EXM[:, :], [[-1, 128]], ALU.is_ge, 0.0,
                                     base=15, channel_multiplier=16), 1)
        g_consts = cnt["g"]

        @block.scalar
        def _(s):
            inc("w2", s.dma_start(out=BQ[:, :], in_=bq_h[:, :]), 16)
            inc("w2", s.dma_start(out=BK[:, :], in_=bk_h[:, :]), 16)
        w_all = cnt["w"]
        w2_all = cnt["w2"]

        @block.vector
        def _(v):
            v.memset(EPS[:, :], LEPS)
            v.memset(LBC[:, :], LB128)
            wait(v, "w", w_all)
            wait(v, "w2", w2_all)
            inc("v", v.tensor_scalar_mul(BM[:, :], BM[:, :], 16.0), 1)

        lnbuf = [LN0, LN1]
        ln_tr = [0] * 8
        for j in range(8):
            xs = "x" if j % 2 == 0 else "x2"

            @block.sync
            def _(sy, j=j, xs=xs):
                if j >= 2:
                    wait(sy, "t", ln_tr[j - 2])
                inc(xs, sy.dma_start(out=lnbuf[j % 2][:, :],
                                     in_=xtok[128 * j:128 * (j + 1), :]), 16)
            x_ld = cnt[xs]

            @block.vector
            def _(v, j=j, x_ld=x_ld, xs=xs):
                wait(v, xs, x_ld)
                if j >= 1:
                    wait(v, "t", ln_tr[j - 1])
                buf = lnbuf[j % 2]
                v.scalar_tensor_tensor(XNJ[:, :], buf[:, :], 1.0, buf[:, :],
                                       ALU.mult, ALU.mult, accum_out=SSQ[:, :])
                inc("v", v.tensor_scalar(DS2B[:, :], buf[:, :], 1.0, 0.0,
                                         ALU.mult, ALU.add, accum_out=SUMT[:, :]), 1)
                wait(v, "v")
                inc("v", v.tensor_scalar(MEAN[:, :], SUMT[:, :], 1.0 / E, None, ALU.mult), 1)
                wait(v, "v")
                inc("v", v.tensor_scalar(MSQ[:, :], MEAN[:, :], MEAN[:, :], None, ALU.mult), 1)
                wait(v, "v")
                inc("v", v.tensor_scalar(VAR[:, :], SSQ[:, :], 1.0 / E, MSQ[:, :],
                                         ALU.mult, ALU.subtract), 1)
            v_var = cnt["v"]

            @block.scalar
            def _(s, v_var=v_var):
                wait(s, "v", v_var)
                inc("a", s.activation(LNT[:, :], VAR[:, :], AF.Ln, bias=EPS[:, :]), 1)
                wait(s, "a")
                inc("a", s.activation(RSTD[:, :], LNT[:, :], AF.Exp, scale=-0.5), 1)
            a_rstd = cnt["a"]

            @block.vector
            def _(v, j=j, a_rstd=a_rstd):
                wait(v, "a", a_rstd)
                inc("v", v.tensor_scalar(XNJ[:, :], lnbuf[j % 2][:, :], MEAN[:, :],
                                         RSTD[:, :], ALU.subtract, ALU.mult), 1)
            v_xn = cnt["v"]

            @block.sync
            def _(sy, j=j, v_xn=v_xn):
                wait(sy, "v", v_xn)
                for i in range(8):
                    inc("t", sy.dma_start_transpose(
                        BIGA[:, 1024 * i + 128 * j:1024 * i + 128 * (j + 1)],
                        XNJ[:, 128 * i:128 * (i + 1)]), 16)

            @block.scalar
            def _(s, j=j, v_xn=v_xn):
                wait(s, "v", v_xn)
                for i in range(8, 16):
                    inc("t", s.dma_start_transpose(
                        BIGA[:, 1024 * i + 128 * j:1024 * i + 128 * (j + 1)],
                        XNJ[:, 128 * i:128 * (i + 1)]), 16)
            ln_tr[j] = cnt["t"]

        I32 = mybir.dt.int32
        for h in range(2):
            hs = "sc" if h == 0 else "sd"

            @block.sync
            def _(sy, h=h, hs=hs):
                wait(sy, "t", ln_tr[4 * h + 3])
                inc(hs, sy.dma_start(
                    out=xnt_in[h][:, :].rearrange("(i p) t -> p i t", p=128),
                    in_=BIGA[:, :].rearrange("p (i t) -> p i t", t=1024)[:, :, 512 * h:512 * (h + 1)]), 16)
            sc_x = cnt[hs]

            @block.gpsimd
            def _(g, h=h, sc_x=sc_x, hs=hs):
                wait(g, hs, sc_x)
                inc("c", g.collective_compute(
                    "AllGather", ALU.bypass, replica_groups=[list(range(NCORES))],
                    ins=[xnt_in[h].ap().opt()],
                    outs=[xnt_out[h].ap().opt()]), 1)
        c_ag1 = cnt["c"]

        if "xn" in tap_h:
            @block.gpsimd
            def _(g):
                wait(g, "c", c_ag1)
                for i in range(16):
                    inc("o2", g.dma_start(out=tap_h["xn"][128 * i:128 * (i + 1), :],
                                          in_=xnt_out[128 * i:128 * (i + 1), :]), 16)

        strm = [BIGA, XSB0]
        pe_sb_done = []
        rope_done = [0]
        rope_half = [0, 0]
        ropemk = {}
        tm_dma = [0]
        ds_dma = [0]
        ag2 = [0, 0]
        g_v1 = [0]
        attn_av_done = [0, 0]
        aots_dma = [("u00", 0), ("u01", 0)]
        aos_free = [0]
        rope_ch_done = [0]

        pending_chain = [None]
        ct_last = {}

        def emit_phase2_half(b):
            for sbi in range(4):
                gsb = 4 * b + sbi
                g0 = 1024 * sbi
                cur = strm[gsb % 2]
                tb_ld = [0]

                def emit_rope(v, X, a, cch, first, sbi=sbi, tb_ld=tb_ld):
                    cols = 4096 * a + 1024 * sbi + 512 * cch
                    wait(v, "v")
                    if first:
                        wait(v, "tb", tb_ld[0])
                    v.tensor_copy(RT[0:64, :], X[64:128, cols:cols + 512])
                    inc("v", v.tensor_copy(RT[64:128, :], X[0:64, cols:cols + 512]), 1)
                    inc("v", v.tensor_tensor(RA[:, :], X[:, cols:cols + 512],
                                             COSS[:, 512 * cch:512 * (cch + 1)],
                                             ALU.mult), 1)
                    wait(v, "v")
                    inc("v", v.tensor_tensor(RT[:, :], RT[:, :],
                                             COSS[:, 1024 + 512 * cch:1024 + 512 * (cch + 1)],
                                             ALU.mult), 1)
                    wait(v, "v")
                    inc("v", v.tensor_tensor(X[:, cols:cols + 512], RA[:, :],
                                             RT[:, :], ALU.add), 1)

                sbh = [("sb", "sbA"), ("sb2", "sb2A")][gsb % 2]
                sb_ldh = [0, 0]

                @block.scalar
                def _(sy, gsb=gsb, cur=cur, g0=g0, sbh=sbh):
                    if pe_sb_done:
                        wait(sy, "p", pe_sb_done[-1])
                    for h in range(2):
                        wait(sy, "c", min(h + 1, c_ag1))
                        sb_ldh[h] = inc(sbh[h], sy.dma_start(
                            out=cur[:, :].rearrange("p (i t) -> p i t", t=1024)[:, :, 512 * h:512 * (h + 1)],
                            in_=xnt_out[h][E * gsb:E * (gsb + 1), :].rearrange("(i p) t -> p i t", p=128)), 16)

                mt_mm = [0] * 8
                mt_ev = [0] * 8
                for j in range(8):
                    @block.tensor
                    def _(t, gsb=gsb, cur=cur, j=j, sbh=sbh):
                        if j == 0:
                            wait(t, sbh[0], sb_ldh[0])
                            if gsb == 0:
                                wait(t, "w", w_all)
                                wait(t, "w2", w2_all)
                                wait(t, "g", g_consts)
                        if j == 4:
                            wait(t, sbh[1], sb_ldh[1])
                        vd, pg = PS[4 + (j % 2)], PS[6 + (j % 2)]
                        wait_banks(t, [4 + (j % 2), 6 + (j % 2)])
                        for i in range(16):
                            lhs = cur[:, 1024 * i + 128 * j:1024 * i + 128 * (j + 1)]
                            t.matmul(vd[:, :], lhs, WVD[:, 512 * i:512 * (i + 1)],
                                     start=(i == 0), stop=False)
                            t.matmul(pg[:, 0:8], lhs, GW8[:, 16 * i:16 * i + 8],
                                     start=(i == 0), stop=(i == 15))
                        mt_mm[j] = inc("p", t.matmul(vd[:, :], ONES1[:, :], BVD[:, :],
                                                     start=False, stop=True), 1)

                    @block.vector
                    def _(v, sbi=sbi, b=b, j=j):
                        vd, pg = PS[4 + (j % 2)], PS[6 + (j % 2)]
                        wait(v, "p", mt_mm[j])
                        if j % 4 == 0:
                            wait(v, "ds", ds_dma[-1])
                            if j == 0 and b == 1 and sbi == 0:
                                wait(v, "g", g_v1[0])
                        for a in range(2):
                            ch = (a * 32 + 8 * sbi + j) * 129
                            v.tensor_copy(V[:, ch:ch + 128], vd[:, 128 * a:128 * (a + 1)])
                        v.tensor_copy(DSTG[:, 256 * j:256 * (j + 1)], vd[:, 256:512])
                        mt_ev[j] = inc("v", v.tensor_copy(LG[:, 8 * j:8 * (j + 1)], pg[:, 0:8]), 1)
                        bank_free[4 + (j % 2)] = ("v", mt_ev[j])
                        bank_free[6 + (j % 2)] = ("v", mt_ev[j])

                @block.sync
                def _(sy, g0=g0, tb_ld=tb_ld):
                    wait(sy, "v", rope_ch_done[0])
                    inc("tb", sy.dma_start(out=COSS[:, 0:1024],
                                           in_=cos2_h[:, g0:g0 + 1024]), 16)
                    tb_ld[0] = inc("tb", sy.dma_start(out=COSS[:, 1024:2048],
                                                      in_=sinpm_h[:, g0:g0 + 1024]), 16)

                q_mm = [0] * 4
                k_mm = [0] * 4

                @block.tensor
                def _(t, cur=cur, sbh=sbh):
                    for a in range(2):
                        for cch in range(2):
                            ps = PS[2 * a + cch]
                            wait_banks(t, [2 * a + cch])
                            wait(t, sbh[cch], sb_ldh[cch])
                            for i in range(16):
                                mm = t.matmul(ps[:, :],
                                              WQT[:, 256 * i + 128 * a:256 * i + 128 * (a + 1)],
                                              cur[:, 1024 * i + 512 * cch:1024 * i + 512 * (cch + 1)],
                                              start=(i == 0), stop=(i == 15))
                            q_mm[2 * a + cch] = inc("p", mm, 1)

                ls_gate = [0]

                @block.vector
                def _(v, sbi=sbi, b=b):
                    for a in range(2):
                        for cch in range(2):
                            ps = PS[2 * a + cch]
                            wait(v, "p", q_mm[2 * a + cch])
                            cc = inc("v", v.tensor_scalar(
                                Q[:, 4096 * a + 1024 * sbi + 512 * cch:
                                  4096 * a + 1024 * sbi + 512 * (cch + 1)],
                                ps[:, :], BQ[:, a:a + 1], None, ALU.add), 1)
                            bank_free[2 * a + cch] = ("v", cc)
                            emit_rope(v, Q, a, cch, first=(a == 0 and cch == 0))
                    lg3 = LG[:, :].rearrange("p (g e) -> p g e", e=8)
                    m1b = M1[:, :].rearrange("p (g o) -> p g o", o=1).broadcast_to([128, 8, 8])
                    a3 = T2A[:, :].rearrange("p (g e) -> p g e", e=8)
                    wait(v, "v")
                    inc("v", v.reduce_max(M1[:, :], lg3, axis=AX.X), 1)
                    wait(v, "v")
                    ls_gate[0] = inc("v", v.tensor_tensor(a3, lg3, m1b, ALU.subtract), 1)

                expl_done = [0]

                @block.scalar
                def _(s):
                    wait(s, "v", ls_gate[0])
                    expl_done[0] = inc("a", s.activation(T2B[:, :], T2A[:, :], AF.Exp), 1)

                cw_ready = [0]
                CWS = CWSd[gsb % 2]

                @block.vector
                def _(v, gsb=gsb, CWS=CWS):
                    a3 = T2A[:, :].rearrange("p (g e) -> p g e", e=8)
                    c3 = T2C[:, :].rearrange("p (g e) -> p g e", e=8)
                    cw3 = CWS[:, :].rearrange("p (g e) -> p g e", e=8)
                    wait(v, "p", ct_last.get(gsb % 2, 0))
                    m2b = M2[:, :].rearrange("p (g o) -> p g o", o=1).broadcast_to([128, 8, 8])
                    reb = RE[:, :].rearrange("p (g o) -> p g o", o=1).broadcast_to([128, 8, 8])
                    wait(v, "v", ls_gate[0])
                    inc("v", v.tensor_scalar(T2C[:, :], T2A[:, :], 0.0, -1e30,
                                             ALU.is_ge, ALU.mult), 1)
                    wait(v, "v")
                    inc("v", v.tensor_add(T2C[:, :], T2C[:, :], T2A[:, :]), 1)
                    wait(v, "v")
                    inc("v", v.reduce_max(M2[:, :], c3, axis=AX.X), 1)
                    wait(v, "v")
                    inc("v", v.tensor_tensor(c3, a3, m2b, ALU.is_ge), 1)
                    wait(v, "a", expl_done[0])
                    wait(v, "v")
                    inc("v", v.tensor_mul(T2A[:, :], T2C[:, :], T2B[:, :]), 1)
                    wait(v, "v")
                    inc("v", v.reduce_sum(DEN[:, :], a3, axis=AX.X), 1)
                    wait(v, "v")
                    inc("v", v.reciprocal(RE[:, :], DEN[:, :]), 1)
                    wait(v, "v")
                    cw_ready[0] = inc("v", v.tensor_tensor(cw3, a3, reb, ALU.mult), 1)

                @block.tensor
                def _(t, cur=cur):
                    for a in range(2):
                        for cch in range(2):
                            ps = PS[2 * a + cch]
                            wait_banks(t, [2 * a + cch])
                            for i in range(16):
                                mm = t.matmul(ps[:, :],
                                              WKT[:, 256 * i + 128 * a:256 * i + 128 * (a + 1)],
                                              cur[:, 1024 * i + 512 * cch:1024 * i + 512 * (cch + 1)],
                                              start=(i == 0), stop=(i == 15))
                            k_mm[2 * a + cch] = inc("p", mm, 1)

                @block.vector
                def _(v, sbi=sbi, b=b):
                    for a in range(2):
                        for cch in range(2):
                            ps = PS[2 * a + cch]
                            wait(v, "p", k_mm[2 * a + cch])
                            cc = inc("v", v.tensor_scalar(
                                K[:, 4096 * a + 1024 * sbi + 512 * cch:
                                  4096 * a + 1024 * sbi + 512 * (cch + 1)],
                                ps[:, :], BK[:, a:a + 1], None, ALU.add), 1)
                            bank_free[2 * a + cch] = ("v", cc)
                            emit_rope(v, K, a, cch, first=False)
                            ropemk[(b, a, 2 * sbi + cch)] = cnt["v"]
                    rope_ch_done[0] = cnt["v"]

                def make_chain(cur=cur, gsb=gsb, cw_ready=cw_ready, CWS=CWS):
                    ct_mm = [0] * 8
                    emm = [0] * 2
                    tmat_mm = [0] * 2
                    cwt_done = [0]
                    cwx_done = [0]
                    tm_ready = [0]

                    ct_mm = [0] * 8

                    @block.tensor
                    def _(t):
                        wait(t, "v", cw_ready[0])
                        for jj in range(8):
                            ps = PS[4 + jj // 4]
                            wait_banks(t, [4 + jj // 4])
                            ct_mm[jj] = inc("p", t.transpose(
                                ps[0:8, 128 * (jj % 4):128 * (jj % 4) + 128],
                                CWS[:, 8 * jj:8 * jj + 8], ID128[:, :]), 1)
                        ct_last[gsb % 2] = ct_mm[7]

                    cwt_done = [0]

                    @block.vector
                    def _(v):
                        for jj in range(8):
                            wait(v, "p", ct_mm[jj])
                            cc = inc("v", v.tensor_copy(CWT[:, 128 * jj:128 * (jj + 1)],
                                                        PS[4 + jj // 4][0:8,
                                                        128 * (jj % 4):128 * (jj % 4) + 128]), 1)
                            if jj % 4 == 3:
                                bank_free[4 + jj // 4] = ("v", cc)
                        cwt_done[0] = cnt["v"]

                    emm = [0] * 2

                    @block.tensor
                    def _(t):
                        wait(t, "v", cwt_done[0])
                        for cch in range(2):
                            ps = PS[6 + cch]
                            wait_banks(t, [6 + cch])
                            emm[cch] = inc("p", t.matmul(ps[:, :], EXM[:, :],
                                                         CWT[:, 512 * cch:512 * (cch + 1)],
                                                         start=True, stop=True), 1)

                    cwx_done = [0]

                    @block.vector
                    def _(v):
                        for cch in range(2):
                            wait(v, "p", emm[cch])
                            cc = inc("v", v.tensor_copy(CWXS[:, 512 * cch:512 * (cch + 1)],
                                                        PS[6 + cch][:, :]), 1)
                            bank_free[6 + cch] = ("v", cc)
                        cwx_done[0] = cnt["v"]

                    tmat_mm = [0] * 2

                    @block.tensor
                    def _(t, cur=cur):
                        for cch in range(2):
                            ps = PS[cch]
                            wait_banks(t, [cch])
                            for i in range(16):
                                mm = t.matmul(ps[:, :], AMT[:, 128 * i:128 * (i + 1)],
                                              cur[:, 1024 * i + 512 * cch:1024 * i + 512 * (cch + 1)],
                                              start=(i == 0), stop=(i == 15))
                            tmat_mm[cch] = inc("p", mm, 1)
                        pe_sb_done.append(cnt["p"])

                    tm_ready = [0]

                    @block.vector
                    def _(v):
                        wait(v, "v", cwx_done[0])
                        wait(v, "tm", tm_dma[-1])
                        for cch in range(2):
                            wait(v, "p", tmat_mm[cch])
                            cc = inc("v", v.tensor_tensor(TMB[:, 512 * cch:512 * (cch + 1)],
                                                          PS[cch][:, :],
                                                          CWXS[:, 512 * cch:512 * (cch + 1)],
                                                          ALU.mult), 1)
                            bank_free[cch] = ("v", cc)
                        tm_ready[0] = cnt["v"]


                    @block.sync
                    def _(sy, gsb=gsb):
                        wait(sy, "v", tm_ready[0])
                        tm_dma.append(inc("tm", sy.dma_start(
                            out=tmd[:, 1024 * gsb:1024 * (gsb + 1)], in_=TMB[:, :]), 16))

                if pending_chain[0] is not None:
                    pending_chain[0]()
                pending_chain[0] = make_chain

                rope_done.append(cnt["v"])

                @block.sync
                def _(sy, gsb=gsb):
                    wait(sy, "v", mt_ev[7])
                    ds_dma.append(inc("ds", sy.dma_start(
                        out=dsc[1024 * gsb:1024 * (gsb + 1), :].rearrange("(j p) o -> p j o", p=128),
                        in_=DSTG[:, :].rearrange("p (j o) -> p j o", o=FS)), 16))

            if pending_chain[0] is not None:
                pending_chain[0]()
                pending_chain[0] = None
            rope_half[b] = rope_done[-1]

        mask_ready = [0]

        def emit_attention_half(b, mid_hook=None):
            aot_last = [0]
            if b == 0:
                @block.gpsimd
                def _(g):
                    g.wait_ge(sems["v"], cnt["v"])
                    inc("g", g.memset(LN0[:, :], 1.0), 1)
                    wait(g, "g")
                    for dl in range(4):
                        mm = g.affine_select(LN0[:, 512 * dl:512 * (dl + 1)],
                                             LN0[:, 512 * dl:512 * (dl + 1)],
                                             [[1, 512]], ALU.is_ge, 0.0,
                                             base=-128 * dl, channel_multiplier=-1)
                    mask_ready[0] = inc("g", mm, 1)
            for a in range(2):
                sq_pe = [0] * 8
                red_dve = [0] * 8
                for idx in range(8):
                    side = idx // 4
                    cc = idx % 4
                    X = (Q, K)[side]
                    RMX = (RMXQ, RMXK)[side]
                    sq_dve = [0]

                    @block.vector
                    def _(v, X=X, cc=cc, idx=idx, a=a, b=b):
                        if idx == 0:
                            wait(v, "av", cnt["av"])
                        if idx >= 1:
                            wait(v, "p", sq_pe[idx - 1])
                        cols = 4096 * a + 1024 * cc
                        sq_dve[0] = inc("v", v.scalar_tensor_tensor(
                            SHB[:, :], X[:, cols:cols + 1024], 1.0,
                            X[:, cols:cols + 1024], ALU.mult, ALU.mult), 1)

                    @block.tensor
                    def _(t, idx=idx):
                        wait(t, "v", sq_dve[0])
                        if idx == 0:
                            wait_banks(t, [4, 5])
                        else:
                            wait(t, "v", red_dve[idx - 1])
                        t.matmul(PS[4][:, :], ONES128[:, :], SHB[:, 0:512],
                                 start=True, stop=True)
                        sq_pe[idx] = inc("p", t.matmul(PS[5][:, :], ONES128[:, :],
                                                       SHB[:, 512:1024],
                                                       start=True, stop=True), 1)

                    @block.vector
                    def _(v, idx=idx, RMX=RMX):
                        wait(v, "p", sq_pe[idx])
                        if idx % 4 == 0:
                            inc("v", v.reduce_max(RMX[:, :], PS[4][:, :], axis=AX.X), 1)
                        else:
                            inc("v", v.reduce_max(RTM[:, :], PS[4][:, :], axis=AX.X), 1)
                            wait(v, "v")
                            inc("v", v.tensor_max(RMX[:, :], RMX[:, :], RTM[:, :]), 1)
                        wait(v, "v")
                        inc("v", v.reduce_max(RTM[:, :], PS[5][:, :], axis=AX.X), 1)
                        wait(v, "v")
                        red_dve[idx] = inc("v", v.tensor_max(RMX[:, :], RMX[:, :], RTM[:, :]), 1)
                bank_free[4] = ("v", red_dve[7])
                bank_free[5] = ("v", red_dve[7])

                shp_v = [0]

                @block.vector
                def _(v):
                    wait(v, "v", red_dve[7])
                    shp_v[0] = inc("v", v.tensor_tensor(RTM[:, :], RMXQ[:, :], RMXK[:, :],
                                                        ALU.mult), 1)

                shp_a = [0]

                @block.scalar
                def _(s):
                    wait(s, "v", shp_v[0])
                    inc("a", s.activation(LNT[:, :], RTM[:, :], AF.Ln), 1)
                    wait(s, "a")
                    shp_a[0] = inc("a", s.activation(NEGSH[:, :], LNT[:, :], AF.Exp,
                                                     scale=0.5, bias=LBC[:, :]), 1)

                shp_neg = [0]

                @block.vector
                def _(v):
                    wait(v, "a", shp_a[0])
                    shp_neg[0] = inc("v", v.tensor_scalar_mul(NEGSH[:, :], NEGSH[:, :], -1.0), 1)

                e_cnt = {}
                m_cnt = {}
                av_last_qb = [0]
                PSS = [PS[0], PS[1], PS[6], PS[7]]
                slot_e = [0, 0, 0, 0]
                slot_av = [0, 0, 0, 0]

                def pbslot(ping):
                    buf = PTB if ping < 2 else SHB
                    off = 512 * (ping % 2)
                    return buf[:, off:off + 512]

                for qb in range(8):
                    nk = 4 * (qb + 1)
                    ao_gate = cnt["ao"]

                    def emit_av(t, kav, qb=qb, nk=nk, a=a):
                        diag = kav >= 4 * qb
                        if diag:
                            t.wait_ge(sems["m"], m_cnt[(qb, kav)])
                        else:
                            t.wait_ge(sems["e"], e_cnt[(qb, kav)])
                        pb = pbslot(kav % 4)
                        for qs in range(4):
                            psO = PS[2 + qs]
                            mm = t.matmul(psO[:, 0:129],
                                          pb[:, 128 * qs:128 * (qs + 1)],
                                          V[:, (a * 32 + kav) * 129:(a * 32 + kav) * 129 + 129],
                                          start=(kav == 0),
                                          stop=(kav == nk - 1))
                        slot_av[kav % 4] = cnt["av"] + 1
                        return inc("av", mm, 1)

                    for kc in range(nk):
                        @block.tensor
                        def _(t, kc=kc, qb=qb, nk=nk, a=a, b=b):
                            ping = kc % 4
                            if slot_e[ping]:
                                t.wait_ge(sems["e"], slot_e[ping])
                            elif qb == 0 and kc == 0:
                                wait_banks(t, [0, 1, 2, 3, 4, 5, 6, 7])
                            if kc == 0:
                                t.wait_ge(sems["v"], ropemk[(b, a, qb)])
                                t.wait_ge(sems["ao"], ao_gate)
                            mm = t.matmul(PSS[ping][:, :],
                                          K[:, 4096 * a + 128 * kc:4096 * a + 128 * (kc + 1)],
                                          Q[:, 4096 * a + 512 * qb:4096 * a + 512 * (qb + 1)],
                                          start=True, stop=True)
                            inc("s2", mm, 1)
                            if kc >= 1:
                                emit_av(t, kc - 1)
                        s2_now = cnt["s2"]

                        @block.scalar
                        def _(s, kc=kc, qb=qb, s2_now=s2_now):
                            ping = kc % 4
                            if qb == 0 and kc == 0:
                                wait(s, "v", shp_neg[0])
                            wait(s, "s2", s2_now)
                            if slot_av[ping]:
                                s.wait_ge(sems["av"], slot_av[ping])
                            e_cnt[(qb, kc)] = inc("e", s.activation(
                                pbslot(ping), PSS[ping][:, :], AF.Exp,
                                bias=NEGSH[:, :], scale=SCALE), 1)
                            slot_e[ping] = e_cnt[(qb, kc)]

                        if kc >= 4 * qb:
                            @block.vector
                            def _(v, kc=kc, qb=qb):
                                wait(v, "e", e_cnt[(qb, kc)])
                                if qb == 0 and kc == 0:
                                    wait(v, "g", mask_ready[0])
                                dl = kc - 4 * qb
                                m_cnt[(qb, kc)] = inc("m", v.tensor_tensor(
                                    pbslot(kc % 4), pbslot(kc % 4),
                                    LN0[:, 512 * dl:512 * (dl + 1)], ALU.mult), 1)

                    @block.tensor
                    def _(t, qb=qb, nk=nk):
                        av_last_qb[0] = emit_av(t, nk - 1)

                    ao_now = [0]

                    @block.vector
                    def _(v, qb=qb, nk=nk):
                        wait(v, "av", av_last_qb[0])
                        wait(v, "t", aos_free[0])
                        wait(v, "ao", cnt["ao"])
                        for qs in range(4):
                            inc("v", v.reciprocal(RCP[:, qs:qs + 1], PS[2 + qs][:, 128:129]), 1)
                        wait(v, "v")
                        for qs in range(4):
                            last = v.tensor_scalar(AOS[:, 128 * qs:128 * (qs + 1)],
                                                   PS[2 + qs][:, 0:128], RCP[:, qs:qs + 1],
                                                   None, ALU.mult)
                            if qs < 3:
                                inc("v", last, 1)
                        ao_now[0] = inc("ao", last, 1)

                    @block.sync
                    def _(sy, qb=qb, a=a, b=b):
                        wait(sy, "ao", ao_now[0])
                        pp = qb % 2
                        sy.wait_ge(sems[aots_dma[pp][0]], aots_dma[pp][1])
                        for qs in range(4):
                            aos_free[0] = inc("t", sy.dma_start_transpose(
                                AOTS[:, 512 * pp + 128 * qs:512 * pp + 128 * (qs + 1)],
                                AOS[:, 128 * qs:128 * (qs + 1)]), 16)
                        wait(sy, "t", aos_free[0])
                        us = f"u{b}{pp}"
                        aots_dma[pp] = (us, inc(us, sy.dma_start(
                            out=aot_in[b][128 * a:128 * (a + 1), 512 * qb:512 * (qb + 1)],
                            in_=AOTS[:, 512 * pp:512 * pp + 512]), 16))

                for bi in (0, 1, 6, 7):
                    bank_free[bi] = ("e", cnt["e"])
                for bi in (2, 3, 4, 5):
                    bank_free[bi] = ("ao", cnt["ao"])
                if a == 0 and mid_hook is not None:
                    mid_hook()
                    bank_free[0] = ("v", fin_ev[(3, 6)])
                    bank_free[1] = ("v", fin_ev[(3, 7)])

            attn_av_done[b] = cnt["av"]

            @block.gpsimd
            def _(g, b=b):
                wait(g, f"u{b}0", cnt[f"u{b}0"])
                wait(g, f"u{b}1", cnt[f"u{b}1"])
                ag2[b] = inc("c", g.collective_compute(
                    "AllGather", ALU.bypass, replica_groups=[list(range(NCORES))],
                    ins=[aot_in[b].ap().opt()],
                    outs=[aot_out[b].ap().opt()]), 1)
                if b == 0:
                    g.wait_ge(sems["av"], attn_av_done[0])
                    g_v1[0] = inc("g", g.memset(V[:, :], 1.0), 1)

        av_prev = {}

        def emit_qkv_taps(b):
            if not ({"qk", "vb"} & set(tap_h)):
                return

            @block.gpsimd
            def _(g, b=b):
                g.wait_ge(sems["v"], rope_half[b])
                if "qk" in tap_h:
                    inc("o2", g.dma_start(out=tap_h["qk"][:, 16384 * b:16384 * b + 8192],
                                         in_=Q[:, :]), 16)
                    inc("o2", g.dma_start(out=tap_h["qk"][:, 16384 * b + 8192:16384 * (b + 1)],
                                         in_=K[:, :]), 16)
                if "vb" in tap_h:
                    inc("o2", g.dma_start(out=tap_h["vb"][:, 8256 * b:8256 * (b + 1)],
                                         in_=V[:, :]), 16)

        def emit_attn_tap(b):
            if "attn" not in tap_h:
                return

            @block.gpsimd
            def _(g, b=b):
                wait(g, "c", ag2[b])
                inc("o2", g.dma_start(out=tap_h["attn"][:, S * b:S * (b + 1)],
                                      in_=aot_in[b][:, :]), 16)

        fin_pe_j = {}
        fin_pe = {}
        fin_ev = {}
        out_done = {}
        ft_done = {}
        asb = [BIGA, XSB0]
        xcb = [LN0, LN1]
        dsb = [XNJ, DS2B]
        tmbb = [TMB, PTB]

        def emit_final_tb(tb, mid):
            bb = tb // 4
            c4 = 1024 * (tb % 4)
            sbs = "sb" if tb % 2 == 0 else "sb2"
            xs = "x" if tb % 2 == 0 else "x2"
            xcb_t = LN1 if mid else xcb[tb % 2]
            tmb_t = TMB if mid else tmbb[tb % 2]

            @block.sync
            def _(sy, tb=tb, bb=bb, c4=c4, sbs=sbs):
                wait(sy, "c", ag2[bb])
                if tb == 0:
                    wait(sy, "p", pe_sb_done[-1])
                if tb - 2 in fin_pe:
                    wait(sy, "p", fin_pe[tb - 2])
                inc(sbs, sy.dma_start(
                    out=asb[tb % 2][:, :].rearrange("p (i t) -> p i t", t=1024),
                    in_=aot_out[bb][:, c4:c4 + 1024].rearrange("(i p) t -> p i t", p=128)), 16)
            f_ld = cnt[sbs]

            @block.scalar
            def _(s, tb=tb, xs=xs, mid=mid, xcb_t=xcb_t, tmb_t=tmb_t):
                if mid:
                    if tb == 0:
                        wait(s, "v", cnt["v"])
                        wait(s, "tm", tm_dma[-1])
                        wait(s, "ds", ds_dma[-1])
                    else:
                        wait(s, "v", fin_ev[(tb - 1, 7)])
                        wait(s, "p", fin_pe[tb - 1])
                else:
                    if tb in (4, 5):
                        wait(s, "m", cnt["m"])
                        s.wait_ge(sems["av"], attn_av_done[1])
                        wait(s, "e", cnt["e"])
                        wait(s, "tm", tm_dma[-1])
                        wait(s, "ds", ds_dma[-1])
                    if (tb - 2, 7) in fin_ev:
                        wait(s, "v", fin_ev[(tb - 2, 7)])
                    if tb - 2 in fin_pe:
                        wait(s, "p", fin_pe[tb - 2])
                inc(xs, s.dma_start(out=xcb_t[:, :].rearrange("p (j o) -> p j o", o=FS),
                                    in_=xcol[1024 * tb:1024 * (tb + 1), :]
                                    .rearrange("(j p) o -> p j o", p=128)), 16)
                inc(xs, s.dma_start(out=dsb[tb % 2][:, :].rearrange("p (j o) -> p j o", o=FS),
                                    in_=dsc[1024 * tb:1024 * (tb + 1), :]
                                    .rearrange("(j p) o -> p j o", p=128)), 16)
                inc(xs, s.dma_start(out=tmb_t[:, :],
                                    in_=tmd[:, 1024 * tb:1024 * (tb + 1)]), 16)
            f_x = cnt[xs]

            for j in range(8):
                @block.tensor
                def _(t, tb=tb, f_ld=f_ld, f_x=f_x, sbs=sbs, xs=xs, j=j, tmb_t=tmb_t):
                    if j == 0:
                        wait(t, sbs, f_ld)
                        wait(t, xs, f_x)
                        if tb == 0:
                            wait_banks(t, [0, 1])
                            wait(t, "w", wot_ld)
                    ps = PS[j % 2]
                    if (tb, j - 2) in fin_ev:
                        wait(t, "v", fin_ev[(tb, j - 2)])
                    elif (tb - 1, j + 6) in fin_ev:
                        wait(t, "v", fin_ev[(tb - 1, j + 6)])
                    cur = asb[tb % 2]
                    for i in range(16):
                        t.matmul(ps[:, 0:256],
                                 cur[:, 1024 * i + 128 * j:1024 * i + 128 * (j + 1)],
                                 WQT[:, 256 * i:256 * (i + 1)], start=(i == 0), stop=False)
                    t.matmul(ps[:, 0:256], tmb_t[:, 128 * j:128 * (j + 1)], BM[:, :],
                             start=False, stop=False)
                    fin_pe_j[(tb, j)] = inc("p", t.matmul(ps[:, 0:256], ONES1[:, :], BOR[:, :],
                                                          start=False, stop=True), 1)

                @block.vector
                def _(v, tb=tb, j=j, xcb_t=xcb_t):
                    ps = PS[j % 2]
                    wait(v, "p", fin_pe_j[(tb, j)])
                    ft = FT[:, 256 * (j % 2):256 * (j % 2) + 256]
                    if (tb, j - 2) in ft_done:
                        wait(v, "ft", ft_done[(tb, j - 2)])
                    elif (tb - 1, j + 6) in ft_done:
                        wait(v, "ft", ft_done[(tb - 1, j + 6)])
                    inc("v", v.tensor_tensor(ft, ps[:, 0:256],
                                             dsb[tb % 2][:, 256 * j:256 * (j + 1)], ALU.add), 1)
                    wait(v, "v")
                    fin_ev[(tb, j)] = inc("v", v.tensor_tensor(
                        ft, ft, xcb_t[:, 256 * j:256 * (j + 1)], ALU.add), 1)

                osem = "o" if j % 2 == 0 else "o3"

                @block.sync
                def _(sy, tb=tb, j=j, osem=osem):
                    wait(sy, "v", fin_ev[(tb, j)])
                    out_done[(tb, j)] = (osem, inc(osem, sy.dma_start(
                        out=out_h[1024 * tb + 128 * j:1024 * tb + 128 * (j + 1), :],
                        in_=FT[:, 256 * (j % 2):256 * (j % 2) + 256]), 16))

                @block.gpsimd
                def _(g, tb=tb, j=j):
                    g.wait_ge(sems[out_done[(tb, j)][0]], out_done[(tb, j)][1])
                    ft_done[(tb, j)] = inc("ft", g.nop(), 1)
            fin_pe[tb] = cnt["p"]


        if slvl >= 1:
            emit_phase2_half(0)
            emit_qkv_taps(0)
        if slvl >= 2:
            emit_attention_half(0)
            emit_attn_tap(0)
        if slvl >= 3:
            emit_phase2_half(1)
            emit_qkv_taps(1)

        if slvl >= 3:
            @block.gpsimd
            def _(g):
                g.wait_ge(sems["p"], pe_sb_done[-1])
                inc("w", g.dma_start(out=WQT[:, :].rearrange("p (i f) -> p i f", f=FS),
                                     in_=wot_h[:, :].rearrange("(i p) f -> p i f", p=128)), 16)
            wot_ld = cnt["w"]
        if slvl >= 4:
            mid = None
            if slvl >= 5:
                def mid():
                    for tb in range(4):
                        emit_final_tb(tb, True)
            emit_attention_half(1, mid_hook=mid)
            emit_attn_tap(1)

        if {"lg", "cws", "cwt", "cwx"} & set(tap_h):
            @block.gpsimd
            def _(g):
                g.wait_ge(sems["v"], cnt["v"])
                for nm, buf in (("lg", LG), ("cws", CWSd[1]), ("cwt", CWT), ("cwx", CWXS)):
                    if nm in tap_h:
                        inc("o2", g.dma_start(out=tap_h[nm][:, :], in_=buf[:, :]), 16)

        if {"dense", "tmask"} & set(tap_h):
            @block.gpsimd
            def _(g):
                wait(g, "tm", cnt["tm"])
                wait(g, "ds", cnt["ds"])
                if "dense" in tap_h:
                    inc("o2", g.dma_start(out=tap_h["dense"][:, :], in_=dsc[:, :]), 16)
                if "tmask" in tap_h:
                    inc("o2", g.dma_start(out=tap_h["tmask"][:, :], in_=tmd[:, :]), 16)

        if slvl >= 5:
            for tb in range(4, 8):
                emit_final_tb(tb, False)

        @block.sync
        def _(sy):
            wait(sy, "o", cnt["o"])
            wait(sy, "o3", cnt["o3"])

        if cnt["o2"]:
            @block.gpsimd
            def _(g):
                wait(g, "o2", cnt["o2"])

        block.flush(nc)

    return nc


_PERM = np.concatenate([np.arange(0, 128, 2), np.arange(1, 128, 2)])


def _stage(core, ins):
    c = core
    Fc = slice(256 * c, 256 * (c + 1))
    Tc = slice(1024 * c, 1024 * (c + 1))
    xf = np.asarray(ins["L0_attn_x"], dtype=np.float32).reshape(T, E)

    def permrows(w):
        w2 = w.reshape(2, 128, E)[:, _PERM, :]
        return w2.reshape(256, E)

    wq = permrows(np.asarray(ins["L0_attn_Wq"], np.float32)[Fc])
    wk = permrows(np.asarray(ins["L0_attn_Wk"], np.float32)[Fc])
    wv = np.asarray(ins["L0_attn_Wv"], np.float32)[Fc]
    wo = np.asarray(ins["L0_attn_Wo"], np.float32)[Fc]
    wd = np.asarray(ins["L0_ffn_weight"], np.float32)[Fc]
    bq = np.asarray(ins["L0_attn_bq"], np.float32)[Fc].reshape(2, 128)[:, _PERM].T
    bk = np.asarray(ins["L0_attn_bk"], np.float32)[Fc].reshape(2, 128)[:, _PERM].T
    bv = np.asarray(ins["L0_attn_bv"], np.float32)[Fc]
    bo = np.asarray(ins["L0_attn_bo"], np.float32)[Fc].reshape(1, FS)
    aw = np.asarray(ins["L0_ffn_A_w"], np.float32).reshape(128, E)
    gw = np.asarray(ins["L0_ffn_gate_w"], np.float32)
    bmw = np.asarray(ins["L0_ffn_B_w"], np.float32)[:, Fc, :]
    bm = bmw.transpose(0, 2, 1).reshape(128, FS)

    inv = 10000.0 ** (-np.arange(64, dtype=np.float64) / 64.0)
    ang = inv[:, None] * np.arange(S, dtype=np.float64)[None, :]
    cosv = np.cos(ang).astype(np.float32)
    sinv = np.sin(ang).astype(np.float32)

    return {
        "xtok": np.ascontiguousarray(xf[Tc]),
        "xcol": np.ascontiguousarray(xf[:, Fc]),
        "wqt": np.ascontiguousarray(wq.T),
        "wkt": np.ascontiguousarray(wk.T),
        "wvdt": np.ascontiguousarray(np.concatenate([wv.T, wd.T], axis=1)),
        "wot": np.ascontiguousarray(wo.T),
        "awt": np.ascontiguousarray(aw.T),
        "gwt": np.ascontiguousarray(
            np.concatenate([gw.T, np.zeros((E, 8), np.float32)], axis=1)),
        "bq": np.ascontiguousarray(bq),
        "bk": np.ascontiguousarray(bk),
        "bvd": np.concatenate([bv, np.zeros(256, np.float32)]).reshape(1, 512),
        "bo": np.ascontiguousarray(bo),
        "bm": np.ascontiguousarray(bm),
        "cos2": np.ascontiguousarray(np.concatenate([cosv, cosv], axis=0)),
        "sinpm": np.ascontiguousarray(np.concatenate([-sinv, sinv], axis=0)),
    }


def kernel(**inputs):
    nc = build_nc()
    in_maps = [_stage(c, inputs) for c in range(NCORES)]
    res = run_bass_kernel_spmd(nc, in_maps, core_ids=list(range(NCORES)))
    out = np.concatenate([np.asarray(res.results[c]["out"]) for c in range(NCORES)], axis=1)
    return out.reshape(B, S, E).astype(np.float32)


# version 18 (final)
# speedup vs baseline: 1.2048x; 1.0580x over previous
import numpy as np
import ml_dtypes
from contextlib import ExitStack

import concourse.bass as bass
import concourse.mybir as mybir
from concourse.bass_utils import run_bass_kernel_spmd

F32 = mybir.dt.float32
BF16 = mybir.dt.bfloat16
AF = mybir.ActivationFunctionType
ALU = mybir.AluOpType
AX = mybir.AxisListType

NCORES = 8
E = 2048
HD = 128
B = 2
S = 4096
T = B * S
FS = 256
TS = 1024
SCALE = 1.0 / float(np.sqrt(HD))
LEPS = 1e-5
LB128 = float(-0.5 * np.log(128.0))



class _PInstr:
    __slots__ = ("entry",)

    def __init__(self, entry):
        self.entry = entry

    def then_inc(self, sem, value=1, **kw):
        self.entry[3].append((sem, value))
        return self


class _PEng:

    def __init__(self, ops):
        self._ops = ops

    def wait_ge(self, sem, val):
        self._ops.append(("w", sem, val, None))

    def __getattr__(self, name):
        ops = self._ops

        def call(*a, **kw):
            entry = ["op", name, (a, kw), []]
            ops.append(entry)
            return _PInstr(entry)

        return call


class _DeferredBlock:
    def __init__(self):
        self.ops = {k: [] for k in ("tensor", "scalar", "vector", "gpsimd", "sync")}

    def _mk(self, key):
        def deco(fn):
            fn(_PEng(self.ops[key]))
            return None
        return deco

    @property
    def tensor(self):
        return self._mk("tensor")

    @property
    def scalar(self):
        return self._mk("scalar")

    @property
    def vector(self):
        return self._mk("vector")

    @property
    def gpsimd(self):
        return self._mk("gpsimd")

    @property
    def sync(self):
        return self._mk("sync")

    def flush(self, nc):
        with nc.Block() as blk:
            for key in ("tensor", "scalar", "vector", "gpsimd", "sync"):
                ops = self.ops[key]
                if not ops:
                    continue

                def run(eng, ops=ops):
                    for kind, name, payload, incs in ops:
                        if kind == "w":
                            eng.wait_ge(name, payload)
                        else:
                            a, kw = payload
                            instr = getattr(eng, name)(*a, **kw)
                            for sem, val in incs:
                                instr.then_inc(sem, val)

                getattr(blk, key)(run)


_STAGES = ["ln", "p2b0", "attn0", "p2b1", "attn1", "full"]


def build_nc(taps=(), stage="full"):
    taps = set(taps)
    slvl = _STAGES.index(stage)
    nc = bass.Bass("TRN2", target_bir_lowering=False, debug=False)

    xtok = nc.dram_tensor("xtok", [TS, E], F32, kind="ExternalInput")
    xcol = nc.dram_tensor("xcol", [T, FS], F32, kind="ExternalInput")
    wqt_h = nc.dram_tensor("wqt", [E, FS], F32, kind="ExternalInput")
    wkt_h = nc.dram_tensor("wkt", [E, FS], F32, kind="ExternalInput")
    wvdt_h = nc.dram_tensor("wvdt", [E, 512], F32, kind="ExternalInput")
    wot_h = nc.dram_tensor("wot", [E, FS], F32, kind="ExternalInput")
    awt_h = nc.dram_tensor("awt", [E, 128], F32, kind="ExternalInput")
    gwt_h = nc.dram_tensor("gwt", [E, 16], F32, kind="ExternalInput")
    bq_h = nc.dram_tensor("bq", [128, 2], F32, kind="ExternalInput")
    bk_h = nc.dram_tensor("bk", [128, 2], F32, kind="ExternalInput")
    bvd_h = nc.dram_tensor("bvd", [1, 512], F32, kind="ExternalInput")
    bo_h = nc.dram_tensor("bo", [1, FS], F32, kind="ExternalInput")
    bm_h = nc.dram_tensor("bm", [128, FS], F32, kind="ExternalInput")
    cos2_h = nc.dram_tensor("cos2", [128, S], F32, kind="ExternalInput")
    sinpm_h = nc.dram_tensor("sinpm", [128, S], F32, kind="ExternalInput")
    out_h = nc.dram_tensor("out", [T, FS], F32, kind="ExternalOutput")

    xnt_in = [nc.dram_tensor(f"xnt_in{h}", [E, TS // 2], BF16, kind="Internal") for h in range(2)]
    xnt_out = [nc.dram_tensor(f"xnt_out{h}", [NCORES * E, TS // 2], BF16, kind="Internal",
                              addr_space="Shared") for h in range(2)]
    aot_in = [nc.dram_tensor(f"aot_in{b}", [FS, S], BF16, kind="Internal") for b in range(B)]
    aot_out = [nc.dram_tensor(f"aot_out{b}", [NCORES * FS, S], BF16, kind="Internal",
                              addr_space="Shared") for b in range(B)]
    aot_in1 = [nc.dram_tensor(f"aot_in1{h}", [FS, S // 2], BF16, kind="Internal")
               for h in range(2)]
    aot_out1 = [nc.dram_tensor(f"aot_out1{h}", [NCORES * FS, S // 2], BF16, kind="Internal",
                               addr_space="Shared") for h in range(2)]
    dsc = nc.dram_tensor("dsc", [T, FS], BF16, kind="Internal")
    tmd = nc.dram_tensor("tmd", [128, T], BF16, kind="Internal")

    tap_h = {}
    for nm, shp in (("xn", [E, TS]), ("qk", [128, 2 * 16384]), ("vb", [128, 2 * 8256]),
                    ("dense", [T, FS]), ("tmask", [128, T]), ("attn", [FS, T]),
                    ("lg", [128, 64]), ("cws", [128, 64]), ("cwt", [8, 1024]),
                    ("cwx", [128, 1024])):
        if nm in taps:
            tap_h[nm] = nc.dram_tensor("tap_" + nm, shp, F32, kind="ExternalOutput")

    with ExitStack() as ctx:
        def sb(name, shape, dt, **kw):
            return ctx.enter_context(nc.sbuf_tensor(name, shape, dt, **kw))

        WQT = sb("WQT", [128, 4096], BF16)
        WKT = sb("WKT", [128, 4096], BF16)
        WVD = sb("WVD", [128, 8192], BF16)
        AMT = sb("AMT", [128, 2048], BF16)
        GW8 = sb("GW8", [128, 256], BF16)
        BM = sb("BM", [128, FS], BF16)
        EXM = sb("EXM", [8, 128], BF16)
        BQ = sb("BQ", [128, 2], F32)
        BK = sb("BK", [128, 2], F32)
        BVD = sb("BVD", [1, 512], BF16)
        BOR = sb("BOR", [1, FS], BF16)
        ONES1 = sb("ONES1", [1, 128], BF16)
        ONES128 = sb("ONES128", [128, 128], BF16)
        ID128 = sb("ID128", [128, 128], F32)
        EPS = sb("EPS", [128, 1], F32)
        LBC = sb("LBC", [128, 1], F32)
        COSS = sb("COSS", [128, 2048], F32)
        LN0 = sb("LN0", [128, 2048], F32)
        LN1 = sb("LN1", [128, 2048], F32)
        XNJ = sb("XNJ", [128, 2048], BF16)
        DS2B = sb("DS2B", [128, 2048], BF16)
        DSTG = sb("DSTG", [128, 2048], BF16)
        BIGA = sb("BIGA", [128, 16384], BF16, is_transpose_dest=True)
        XSB0 = sb("XSB0", [128, 16384], BF16)
        Q = sb("Q", [128, 8192], BF16)
        K = sb("K", [128, 8192], BF16)
        V = sb("V", [128, 2 * 32 * 129], BF16)
        LG = sb("LG", [128, 64], F32)
        T2A = sb("T2A", [128, 64], F32)
        T2B = sb("T2B", [128, 64], F32)
        T2C = sb("T2C", [128, 64], F32)
        CWSd = [sb("CWS0", [128, 64], F32), sb("CWS1", [128, 64], F32)]
        M1 = sb("M1", [128, 8], F32)
        M2 = sb("M2", [128, 8], F32)
        DEN = sb("DEN", [128, 8], F32)
        RE = sb("RE", [128, 8], F32)
        CWT = sb("CWT", [8, 1024], BF16)
        CWXS = sb("CWXS", [128, 1024], BF16)
        _xnj_addr = nc.lookup_mloc(XNJ).addr
        RT = nc.alloc_sbuf_tensor_at("RT", [128, 512], F32, offset=_xnj_addr)
        RA = nc.alloc_sbuf_tensor_at("RA", [128, 512], F32, offset=_xnj_addr + 2048)
        SHB = sb("SHB", [128, 1024], BF16)
        RMXQ = sb("RMXQ", [128, 1], F32)
        RMXK = sb("RMXK", [128, 1], F32)
        RTM = sb("RTM", [128, 1], F32)
        NEGSH = sb("NEGSH", [128, 1], F32)
        PTB = sb("PTB", [128, 1024], BF16)
        TMB = sb("TMB", [128, 1024], BF16)
        AOS = sb("AOS", [128, 512], BF16)
        AOTS = sb("AOTS", [128, 1024], BF16, is_transpose_dest=True)
        SSQ = sb("SSQ", [128, 1], F32)
        SUMT = sb("SUMT", [128, 1], F32)
        MEAN = sb("MEAN", [128, 1], F32)
        MSQ = sb("MSQ", [128, 1], F32)
        VAR = sb("VAR", [128, 1], F32)
        LNT = sb("LNT", [128, 1], F32)
        RSTD = sb("RSTD", [128, 1], F32)
        RCP = sb("RCP", [128, 4], F32)
        FT = sb("FT", [128, 512], F32)

        PS = [ctx.enter_context(nc.psum_tensor(f"PS{i}", [128, 512], F32)) for i in range(8)]

        sems = {}
        cnt = {}
        for nm in ("w", "w2", "x", "x2", "v", "a", "p", "g", "t", "c", "sb", "sb2", "sbA", "sb2A",
                   "sc", "sd", "tm", "ds", "o", "o3", "o2", "ft", "s2", "e", "m", "av", "ao", "tb",
                   "u00", "u01", "u10", "u11"):
            sems[nm] = ctx.enter_context(nc.semaphore("s_" + nm))
            cnt[nm] = 0

        def inc(nm, instr, k=1):
            instr.then_inc(sems[nm], k)
            cnt[nm] += k
            return cnt[nm]

        def wait(eng, nm, val=None):
            eng.wait_ge(sems[nm], cnt[nm] if val is None else val)

        bank_free = [("v", 0)] * 8

        def wait_banks(t, idxs):
            need = {}
            for i in idxs:
                nm, c = bank_free[i]
                need[nm] = max(need.get(nm, 0), c)
            for nm, c in need.items():
                if c > 0:
                    t.wait_ge(sems[nm], c)

        block = _DeferredBlock()

        @block.gpsimd
        def _(g):
            inc("w", g.dma_start(out=WQT[:, :].rearrange("p (i f) -> p i f", f=FS),
                                 in_=wqt_h[:, :].rearrange("(i p) f -> p i f", p=128)), 16)
            inc("w", g.dma_start(out=WKT[:, :].rearrange("p (i f) -> p i f", f=FS),
                                 in_=wkt_h[:, :].rearrange("(i p) f -> p i f", p=128)), 16)
            inc("w", g.dma_start(out=WVD[:, :].rearrange("p (i f) -> p i f", f=512),
                                 in_=wvdt_h[:, :].rearrange("(i p) f -> p i f", p=128)), 16)
            inc("w", g.dma_start(out=AMT[:, :].rearrange("p (i f) -> p i f", f=128),
                                 in_=awt_h[:, :].rearrange("(i p) f -> p i f", p=128)), 16)
            inc("w", g.dma_start(out=GW8[:, :].rearrange("p (i f) -> p i f", f=16),
                                 in_=gwt_h[:, :].rearrange("(i p) f -> p i f", p=128)), 16)
            inc("w", g.dma_start(out=BM[:, :], in_=bm_h[:, :]), 16)
            inc("w", g.dma_start(out=BVD[:, :], in_=bvd_h[:, :]), 16)
            inc("w", g.dma_start(out=BOR[:, :], in_=bo_h[:, :]), 16)
            g.memset(ONES1[:, :], 1.0)
            g.memset(ONES128[:, :], 1.0)
            g.memset(V[:, :], 1.0)
            inc("g", g.memset(ID128[:, :], 1.0), 1)
            wait(g, "g")
            inc("g", g.affine_select(ID128[:, :], ID128[:, :], [[-1, 128]], ALU.is_equal, 0.0,
                                     base=0, channel_multiplier=1), 1)
            inc("g", g.memset(EXM[:, :], 1.0), 1)
            wait(g, "g")
            inc("g", g.affine_select(EXM[:, :], EXM[:, :], [[1, 128]], ALU.is_ge, 0.0,
                                     base=0, channel_multiplier=-16), 1)
            wait(g, "g")
            inc("g", g.affine_select(EXM[:, :], EXM[:, :], [[-1, 128]], ALU.is_ge, 0.0,
                                     base=15, channel_multiplier=16), 1)
        g_consts = cnt["g"]

        @block.scalar
        def _(s):
            inc("w2", s.dma_start(out=BQ[:, :], in_=bq_h[:, :]), 16)
            inc("w2", s.dma_start(out=BK[:, :], in_=bk_h[:, :]), 16)
        w_all = cnt["w"]
        w2_all = cnt["w2"]

        @block.vector
        def _(v):
            v.memset(EPS[:, :], LEPS)
            v.memset(LBC[:, :], LB128)
            wait(v, "w", w_all)
            wait(v, "w2", w2_all)
            inc("v", v.tensor_scalar_mul(BM[:, :], BM[:, :], 16.0), 1)

        lnbuf = [LN0, LN1]
        ln_tr = [0] * 8
        for j in range(8):
            xs = "x" if j % 2 == 0 else "x2"

            @block.sync
            def _(sy, j=j, xs=xs):
                if j >= 2:
                    wait(sy, "t", ln_tr[j - 2])
                inc(xs, sy.dma_start(out=lnbuf[j % 2][:, :],
                                     in_=xtok[128 * j:128 * (j + 1), :]), 16)
            x_ld = cnt[xs]

            @block.vector
            def _(v, j=j, x_ld=x_ld, xs=xs):
                wait(v, xs, x_ld)
                if j >= 1:
                    wait(v, "t", ln_tr[j - 1])
                buf = lnbuf[j % 2]
                v.scalar_tensor_tensor(XNJ[:, :], buf[:, :], 1.0, buf[:, :],
                                       ALU.mult, ALU.mult, accum_out=SSQ[:, :])
                inc("v", v.tensor_scalar(DS2B[:, :], buf[:, :], 1.0, 0.0,
                                         ALU.mult, ALU.add, accum_out=SUMT[:, :]), 1)
                wait(v, "v")
                inc("v", v.tensor_scalar(MEAN[:, :], SUMT[:, :], 1.0 / E, None, ALU.mult), 1)
                wait(v, "v")
                inc("v", v.tensor_scalar(MSQ[:, :], MEAN[:, :], MEAN[:, :], None, ALU.mult), 1)
                wait(v, "v")
                inc("v", v.tensor_scalar(VAR[:, :], SSQ[:, :], 1.0 / E, MSQ[:, :],
                                         ALU.mult, ALU.subtract), 1)
            v_var = cnt["v"]

            @block.scalar
            def _(s, v_var=v_var):
                wait(s, "v", v_var)
                inc("a", s.activation(LNT[:, :], VAR[:, :], AF.Ln, bias=EPS[:, :]), 1)
                wait(s, "a")
                inc("a", s.activation(RSTD[:, :], LNT[:, :], AF.Exp, scale=-0.5), 1)
            a_rstd = cnt["a"]

            @block.vector
            def _(v, j=j, a_rstd=a_rstd):
                wait(v, "a", a_rstd)
                inc("v", v.tensor_scalar(XNJ[:, :], lnbuf[j % 2][:, :], MEAN[:, :],
                                         RSTD[:, :], ALU.subtract, ALU.mult), 1)
            v_xn = cnt["v"]

            @block.sync
            def _(sy, j=j, v_xn=v_xn):
                wait(sy, "v", v_xn)
                for i in range(8):
                    inc("t", sy.dma_start_transpose(
                        BIGA[:, 1024 * i + 128 * j:1024 * i + 128 * (j + 1)],
                        XNJ[:, 128 * i:128 * (i + 1)]), 16)

            @block.scalar
            def _(s, j=j, v_xn=v_xn):
                wait(s, "v", v_xn)
                for i in range(8, 16):
                    inc("t", s.dma_start_transpose(
                        BIGA[:, 1024 * i + 128 * j:1024 * i + 128 * (j + 1)],
                        XNJ[:, 128 * i:128 * (i + 1)]), 16)
            ln_tr[j] = cnt["t"]

        I32 = mybir.dt.int32
        for h in range(2):
            hs = "sc" if h == 0 else "sd"

            @block.sync
            def _(sy, h=h, hs=hs):
                wait(sy, "t", ln_tr[4 * h + 3])
                inc(hs, sy.dma_start(
                    out=xnt_in[h][:, :].rearrange("(i p) t -> p i t", p=128),
                    in_=BIGA[:, :].rearrange("p (i t) -> p i t", t=1024)[:, :, 512 * h:512 * (h + 1)]), 16)
            sc_x = cnt[hs]

            @block.gpsimd
            def _(g, h=h, sc_x=sc_x, hs=hs):
                wait(g, hs, sc_x)
                inc("c", g.collective_compute(
                    "AllGather", ALU.bypass, replica_groups=[list(range(NCORES))],
                    ins=[xnt_in[h].ap().opt()],
                    outs=[xnt_out[h].ap().opt()]), 1)
        c_ag1 = cnt["c"]

        if "xn" in tap_h:
            @block.gpsimd
            def _(g):
                wait(g, "c", c_ag1)
                for i in range(16):
                    inc("o2", g.dma_start(out=tap_h["xn"][128 * i:128 * (i + 1), :],
                                          in_=xnt_out[128 * i:128 * (i + 1), :]), 16)

        strm = [BIGA, XSB0]
        pe_sb_done = []
        rope_done = [0]
        rope_half = [0, 0]
        ropemk = {}
        tm_dma = [0]
        ds_dma = [0]
        ag2 = [0, 0]
        ag2h1 = [0, 0]
        g_v1 = [0]
        attn_av_done = [0, 0]
        aots_dma = [("u00", 0), ("u01", 0)]
        aos_free = [0]
        rope_ch_done = [0]

        pending_chain = [None]
        ct_last = {}

        def emit_phase2_half(b):
            for sbi in range(4):
                gsb = 4 * b + sbi
                g0 = 1024 * sbi
                cur = strm[gsb % 2]
                tb_ld = [0]

                def emit_rope(v, X, a, cch, first, sbi=sbi, tb_ld=tb_ld):
                    cols = 4096 * a + 1024 * sbi + 512 * cch
                    wait(v, "v")
                    if first:
                        wait(v, "tb", tb_ld[0])
                    v.tensor_copy(RT[0:64, :], X[64:128, cols:cols + 512])
                    inc("v", v.tensor_copy(RT[64:128, :], X[0:64, cols:cols + 512]), 1)
                    inc("v", v.tensor_tensor(RA[:, :], X[:, cols:cols + 512],
                                             COSS[:, 512 * cch:512 * (cch + 1)],
                                             ALU.mult), 1)
                    wait(v, "v")
                    inc("v", v.tensor_tensor(RT[:, :], RT[:, :],
                                             COSS[:, 1024 + 512 * cch:1024 + 512 * (cch + 1)],
                                             ALU.mult), 1)
                    wait(v, "v")
                    inc("v", v.tensor_tensor(X[:, cols:cols + 512], RA[:, :],
                                             RT[:, :], ALU.add), 1)

                sbh = [("sb", "sbA"), ("sb2", "sb2A")][gsb % 2]
                sb_ldh = [0, 0]

                @block.scalar
                def _(sy, gsb=gsb, cur=cur, g0=g0, sbh=sbh):
                    if pe_sb_done:
                        wait(sy, "p", pe_sb_done[-1])
                    for h in range(2):
                        wait(sy, "c", min(h + 1, c_ag1))
                        sb_ldh[h] = inc(sbh[h], sy.dma_start(
                            out=cur[:, :].rearrange("p (i t) -> p i t", t=1024)[:, :, 512 * h:512 * (h + 1)],
                            in_=xnt_out[h][E * gsb:E * (gsb + 1), :].rearrange("(i p) t -> p i t", p=128)), 16)

                mt_mm = [0] * 8
                mt_ev = [0] * 8
                for j in range(8):
                    @block.tensor
                    def _(t, gsb=gsb, cur=cur, j=j, sbh=sbh):
                        if j == 0:
                            wait(t, sbh[0], sb_ldh[0])
                            if gsb == 0:
                                wait(t, "w", w_all)
                                wait(t, "w2", w2_all)
                                wait(t, "g", g_consts)
                        if j == 4:
                            wait(t, sbh[1], sb_ldh[1])
                        vd, pg = PS[4 + (j % 2)], PS[6 + (j % 2)]
                        wait_banks(t, [4 + (j % 2), 6 + (j % 2)])
                        for i in range(16):
                            lhs = cur[:, 1024 * i + 128 * j:1024 * i + 128 * (j + 1)]
                            t.matmul(vd[:, :], lhs, WVD[:, 512 * i:512 * (i + 1)],
                                     start=(i == 0), stop=False)
                            t.matmul(pg[:, 0:8], lhs, GW8[:, 16 * i:16 * i + 8],
                                     start=(i == 0), stop=(i == 15))
                        mt_mm[j] = inc("p", t.matmul(vd[:, :], ONES1[:, :], BVD[:, :],
                                                     start=False, stop=True), 1)

                    @block.vector
                    def _(v, sbi=sbi, b=b, j=j):
                        vd, pg = PS[4 + (j % 2)], PS[6 + (j % 2)]
                        wait(v, "p", mt_mm[j])
                        if j % 4 == 0:
                            wait(v, "ds", ds_dma[-1])
                            if j == 0 and b == 1 and sbi == 0:
                                wait(v, "g", g_v1[0])
                        for a in range(2):
                            ch = (a * 32 + 8 * sbi + j) * 129
                            v.tensor_copy(V[:, ch:ch + 128], vd[:, 128 * a:128 * (a + 1)])
                        v.tensor_copy(DSTG[:, 256 * j:256 * (j + 1)], vd[:, 256:512])
                        mt_ev[j] = inc("v", v.tensor_copy(LG[:, 8 * j:8 * (j + 1)], pg[:, 0:8]), 1)
                        bank_free[4 + (j % 2)] = ("v", mt_ev[j])
                        bank_free[6 + (j % 2)] = ("v", mt_ev[j])

                @block.sync
                def _(sy, g0=g0, tb_ld=tb_ld):
                    wait(sy, "v", rope_ch_done[0])
                    inc("tb", sy.dma_start(out=COSS[:, 0:1024],
                                           in_=cos2_h[:, g0:g0 + 1024]), 16)
                    tb_ld[0] = inc("tb", sy.dma_start(out=COSS[:, 1024:2048],
                                                      in_=sinpm_h[:, g0:g0 + 1024]), 16)

                q_mm = [0] * 4
                k_mm = [0] * 4

                @block.tensor
                def _(t, cur=cur, sbh=sbh):
                    for a in range(2):
                        for cch in range(2):
                            ps = PS[2 * a + cch]
                            wait_banks(t, [2 * a + cch])
                            wait(t, sbh[cch], sb_ldh[cch])
                            for i in range(16):
                                mm = t.matmul(ps[:, :],
                                              WQT[:, 256 * i + 128 * a:256 * i + 128 * (a + 1)],
                                              cur[:, 1024 * i + 512 * cch:1024 * i + 512 * (cch + 1)],
                                              start=(i == 0), stop=(i == 15))
                            q_mm[2 * a + cch] = inc("p", mm, 1)

                ls_gate = [0]

                @block.vector
                def _(v, sbi=sbi, b=b):
                    for a in range(2):
                        for cch in range(2):
                            ps = PS[2 * a + cch]
                            wait(v, "p", q_mm[2 * a + cch])
                            cc = inc("v", v.tensor_scalar(
                                Q[:, 4096 * a + 1024 * sbi + 512 * cch:
                                  4096 * a + 1024 * sbi + 512 * (cch + 1)],
                                ps[:, :], BQ[:, a:a + 1], None, ALU.add), 1)
                            bank_free[2 * a + cch] = ("v", cc)
                            emit_rope(v, Q, a, cch, first=(a == 0 and cch == 0))
                    lg3 = LG[:, :].rearrange("p (g e) -> p g e", e=8)
                    m1b = M1[:, :].rearrange("p (g o) -> p g o", o=1).broadcast_to([128, 8, 8])
                    a3 = T2A[:, :].rearrange("p (g e) -> p g e", e=8)
                    wait(v, "v")
                    inc("v", v.reduce_max(M1[:, :], lg3, axis=AX.X), 1)
                    wait(v, "v")
                    ls_gate[0] = inc("v", v.tensor_tensor(a3, lg3, m1b, ALU.subtract), 1)

                expl_done = [0]

                @block.scalar
                def _(s):
                    wait(s, "v", ls_gate[0])
                    expl_done[0] = inc("a", s.activation(T2B[:, :], T2A[:, :], AF.Exp), 1)

                cw_ready = [0]
                CWS = CWSd[gsb % 2]

                @block.vector
                def _(v, gsb=gsb, CWS=CWS):
                    a3 = T2A[:, :].rearrange("p (g e) -> p g e", e=8)
                    c3 = T2C[:, :].rearrange("p (g e) -> p g e", e=8)
                    cw3 = CWS[:, :].rearrange("p (g e) -> p g e", e=8)
                    wait(v, "p", ct_last.get(gsb % 2, 0))
                    m2b = M2[:, :].rearrange("p (g o) -> p g o", o=1).broadcast_to([128, 8, 8])
                    reb = RE[:, :].rearrange("p (g o) -> p g o", o=1).broadcast_to([128, 8, 8])
                    wait(v, "v", ls_gate[0])
                    inc("v", v.tensor_scalar(T2C[:, :], T2A[:, :], 0.0, -1e30,
                                             ALU.is_ge, ALU.mult), 1)
                    wait(v, "v")
                    inc("v", v.tensor_add(T2C[:, :], T2C[:, :], T2A[:, :]), 1)
                    wait(v, "v")
                    inc("v", v.reduce_max(M2[:, :], c3, axis=AX.X), 1)
                    wait(v, "v")
                    inc("v", v.tensor_tensor(c3, a3, m2b, ALU.is_ge), 1)
                    wait(v, "a", expl_done[0])
                    wait(v, "v")
                    inc("v", v.tensor_mul(T2A[:, :], T2C[:, :], T2B[:, :]), 1)
                    wait(v, "v")
                    inc("v", v.reduce_sum(DEN[:, :], a3, axis=AX.X), 1)
                    wait(v, "v")
                    inc("v", v.reciprocal(RE[:, :], DEN[:, :]), 1)
                    wait(v, "v")
                    cw_ready[0] = inc("v", v.tensor_tensor(cw3, a3, reb, ALU.mult), 1)

                @block.tensor
                def _(t, cur=cur):
                    for a in range(2):
                        for cch in range(2):
                            ps = PS[2 * a + cch]
                            wait_banks(t, [2 * a + cch])
                            for i in range(16):
                                mm = t.matmul(ps[:, :],
                                              WKT[:, 256 * i + 128 * a:256 * i + 128 * (a + 1)],
                                              cur[:, 1024 * i + 512 * cch:1024 * i + 512 * (cch + 1)],
                                              start=(i == 0), stop=(i == 15))
                            k_mm[2 * a + cch] = inc("p", mm, 1)

                @block.vector
                def _(v, sbi=sbi, b=b):
                    for a in range(2):
                        for cch in range(2):
                            ps = PS[2 * a + cch]
                            wait(v, "p", k_mm[2 * a + cch])
                            cc = inc("v", v.tensor_scalar(
                                K[:, 4096 * a + 1024 * sbi + 512 * cch:
                                  4096 * a + 1024 * sbi + 512 * (cch + 1)],
                                ps[:, :], BK[:, a:a + 1], None, ALU.add), 1)
                            bank_free[2 * a + cch] = ("v", cc)
                            emit_rope(v, K, a, cch, first=False)
                            ropemk[(b, a, 2 * sbi + cch)] = cnt["v"]
                    rope_ch_done[0] = cnt["v"]

                def make_chain(cur=cur, gsb=gsb, cw_ready=cw_ready, CWS=CWS):
                    ct_mm = [0] * 8
                    emm = [0] * 2
                    tmat_mm = [0] * 2
                    cwt_done = [0]
                    cwx_done = [0]
                    tm_ready = [0]

                    ct_mm = [0] * 8

                    @block.tensor
                    def _(t):
                        wait(t, "v", cw_ready[0])
                        for jj in range(8):
                            ps = PS[4 + jj // 4]
                            wait_banks(t, [4 + jj // 4])
                            ct_mm[jj] = inc("p", t.transpose(
                                ps[0:8, 128 * (jj % 4):128 * (jj % 4) + 128],
                                CWS[:, 8 * jj:8 * jj + 8], ID128[:, :]), 1)
                        ct_last[gsb % 2] = ct_mm[7]

                    cwt_done = [0]

                    @block.vector
                    def _(v):
                        for jj in range(8):
                            wait(v, "p", ct_mm[jj])
                            cc = inc("v", v.tensor_copy(CWT[:, 128 * jj:128 * (jj + 1)],
                                                        PS[4 + jj // 4][0:8,
                                                        128 * (jj % 4):128 * (jj % 4) + 128]), 1)
                            if jj % 4 == 3:
                                bank_free[4 + jj // 4] = ("v", cc)
                        cwt_done[0] = cnt["v"]

                    emm = [0] * 2

                    @block.tensor
                    def _(t):
                        wait(t, "v", cwt_done[0])
                        for cch in range(2):
                            ps = PS[6 + cch]
                            wait_banks(t, [6 + cch])
                            emm[cch] = inc("p", t.matmul(ps[:, :], EXM[:, :],
                                                         CWT[:, 512 * cch:512 * (cch + 1)],
                                                         start=True, stop=True), 1)

                    cwx_done = [0]

                    @block.vector
                    def _(v):
                        for cch in range(2):
                            wait(v, "p", emm[cch])
                            cc = inc("v", v.tensor_copy(CWXS[:, 512 * cch:512 * (cch + 1)],
                                                        PS[6 + cch][:, :]), 1)
                            bank_free[6 + cch] = ("v", cc)
                        cwx_done[0] = cnt["v"]

                    tmat_mm = [0] * 2

                    @block.tensor
                    def _(t, cur=cur):
                        for cch in range(2):
                            ps = PS[cch]
                            wait_banks(t, [cch])
                            for i in range(16):
                                mm = t.matmul(ps[:, :], AMT[:, 128 * i:128 * (i + 1)],
                                              cur[:, 1024 * i + 512 * cch:1024 * i + 512 * (cch + 1)],
                                              start=(i == 0), stop=(i == 15))
                            tmat_mm[cch] = inc("p", mm, 1)
                        pe_sb_done.append(cnt["p"])

                    tm_ready = [0]

                    @block.vector
                    def _(v):
                        wait(v, "v", cwx_done[0])
                        wait(v, "tm", tm_dma[-1])
                        for cch in range(2):
                            wait(v, "p", tmat_mm[cch])
                            cc = inc("v", v.tensor_tensor(TMB[:, 512 * cch:512 * (cch + 1)],
                                                          PS[cch][:, :],
                                                          CWXS[:, 512 * cch:512 * (cch + 1)],
                                                          ALU.mult), 1)
                            bank_free[cch] = ("v", cc)
                        tm_ready[0] = cnt["v"]


                    @block.sync
                    def _(sy, gsb=gsb):
                        wait(sy, "v", tm_ready[0])
                        tm_dma.append(inc("tm", sy.dma_start(
                            out=tmd[:, 1024 * gsb:1024 * (gsb + 1)], in_=TMB[:, :]), 16))

                if pending_chain[0] is not None:
                    pending_chain[0]()
                pending_chain[0] = make_chain

                rope_done.append(cnt["v"])

                @block.sync
                def _(sy, gsb=gsb):
                    wait(sy, "v", mt_ev[7])
                    ds_dma.append(inc("ds", sy.dma_start(
                        out=dsc[1024 * gsb:1024 * (gsb + 1), :].rearrange("(j p) o -> p j o", p=128),
                        in_=DSTG[:, :].rearrange("p (j o) -> p j o", o=FS)), 16))

            if pending_chain[0] is not None:
                pending_chain[0]()
                pending_chain[0] = None
            rope_half[b] = rope_done[-1]

        mask_ready = [0]

        def emit_attention_half(b, mid_hook=None):
            aot_last = [0]
            if b == 0:
                @block.gpsimd
                def _(g):
                    g.wait_ge(sems["v"], cnt["v"])
                    inc("g", g.memset(LN0[:, :], 1.0), 1)
                    wait(g, "g")
                    for dl in range(4):
                        mm = g.affine_select(LN0[:, 512 * dl:512 * (dl + 1)],
                                             LN0[:, 512 * dl:512 * (dl + 1)],
                                             [[1, 512]], ALU.is_ge, 0.0,
                                             base=-128 * dl, channel_multiplier=-1)
                    mask_ready[0] = inc("g", mm, 1)
            for a in range(2):
                sq_pe = [0] * 8
                red_dve = [0] * 8
                for idx in range(8):
                    side = idx // 4
                    cc = idx % 4
                    X = (Q, K)[side]
                    RMX = (RMXQ, RMXK)[side]
                    sq_dve = [0]

                    @block.vector
                    def _(v, X=X, cc=cc, idx=idx, a=a, b=b):
                        if idx == 0:
                            wait(v, "av", cnt["av"])
                        if idx >= 1:
                            wait(v, "p", sq_pe[idx - 1])
                        cols = 4096 * a + 1024 * cc
                        sq_dve[0] = inc("v", v.scalar_tensor_tensor(
                            SHB[:, :], X[:, cols:cols + 1024], 1.0,
                            X[:, cols:cols + 1024], ALU.mult, ALU.mult), 1)

                    @block.tensor
                    def _(t, idx=idx):
                        wait(t, "v", sq_dve[0])
                        if idx == 0:
                            wait_banks(t, [4, 5])
                        else:
                            wait(t, "v", red_dve[idx - 1])
                        t.matmul(PS[4][:, :], ONES128[:, :], SHB[:, 0:512],
                                 start=True, stop=True)
                        sq_pe[idx] = inc("p", t.matmul(PS[5][:, :], ONES128[:, :],
                                                       SHB[:, 512:1024],
                                                       start=True, stop=True), 1)

                    @block.vector
                    def _(v, idx=idx, RMX=RMX):
                        wait(v, "p", sq_pe[idx])
                        if idx % 4 == 0:
                            inc("v", v.reduce_max(RMX[:, :], PS[4][:, :], axis=AX.X), 1)
                        else:
                            inc("v", v.reduce_max(RTM[:, :], PS[4][:, :], axis=AX.X), 1)
                            wait(v, "v")
                            inc("v", v.tensor_max(RMX[:, :], RMX[:, :], RTM[:, :]), 1)
                        wait(v, "v")
                        inc("v", v.reduce_max(RTM[:, :], PS[5][:, :], axis=AX.X), 1)
                        wait(v, "v")
                        red_dve[idx] = inc("v", v.tensor_max(RMX[:, :], RMX[:, :], RTM[:, :]), 1)
                bank_free[4] = ("v", red_dve[7])
                bank_free[5] = ("v", red_dve[7])

                shp_v = [0]

                @block.vector
                def _(v):
                    wait(v, "v", red_dve[7])
                    shp_v[0] = inc("v", v.tensor_tensor(RTM[:, :], RMXQ[:, :], RMXK[:, :],
                                                        ALU.mult), 1)

                shp_a = [0]

                @block.scalar
                def _(s):
                    wait(s, "v", shp_v[0])
                    inc("a", s.activation(LNT[:, :], RTM[:, :], AF.Ln), 1)
                    wait(s, "a")
                    shp_a[0] = inc("a", s.activation(NEGSH[:, :], LNT[:, :], AF.Exp,
                                                     scale=0.5, bias=LBC[:, :]), 1)

                shp_neg = [0]

                @block.vector
                def _(v):
                    wait(v, "a", shp_a[0])
                    shp_neg[0] = inc("v", v.tensor_scalar_mul(NEGSH[:, :], NEGSH[:, :], -1.0), 1)

                e_cnt = {}
                m_cnt = {}
                av_last_qb = [0]
                PSS = [PS[0], PS[1], PS[6], PS[7]]
                slot_e = [0, 0, 0, 0]
                slot_av = [0, 0, 0, 0]

                def pbslot(ping):
                    buf = PTB if ping < 2 else SHB
                    off = 512 * (ping % 2)
                    return buf[:, off:off + 512]

                for qb in range(8):
                    nk = 4 * (qb + 1)
                    ao_gate = cnt["ao"]

                    def emit_av(t, kav, qb=qb, nk=nk, a=a, ao_gate=ao_gate):
                        if kav == 0:
                            t.wait_ge(sems["ao"], ao_gate)
                        diag = kav >= 4 * qb
                        if diag:
                            t.wait_ge(sems["m"], m_cnt[(qb, kav)])
                        else:
                            t.wait_ge(sems["e"], e_cnt[(qb, kav)])
                        pb = pbslot(kav % 4)
                        for qs in range(4):
                            psO = PS[2 + qs]
                            mm = t.matmul(psO[:, 0:129],
                                          pb[:, 128 * qs:128 * (qs + 1)],
                                          V[:, (a * 32 + kav) * 129:(a * 32 + kav) * 129 + 129],
                                          start=(kav == 0),
                                          stop=(kav == nk - 1))
                        slot_av[kav % 4] = cnt["av"] + 1
                        return inc("av", mm, 1)

                    for kc in range(nk):
                        @block.tensor
                        def _(t, kc=kc, qb=qb, nk=nk, a=a, b=b):
                            ping = kc % 4
                            if slot_e[ping]:
                                t.wait_ge(sems["e"], slot_e[ping])
                            elif qb == 0 and kc == 0:
                                wait_banks(t, [0, 1, 2, 3, 4, 5, 6, 7])
                            if kc == 0:
                                t.wait_ge(sems["v"], ropemk[(b, a, qb)])
                            mm = t.matmul(PSS[ping][:, :],
                                          K[:, 4096 * a + 128 * kc:4096 * a + 128 * (kc + 1)],
                                          Q[:, 4096 * a + 512 * qb:4096 * a + 512 * (qb + 1)],
                                          start=True, stop=True)
                            inc("s2", mm, 1)
                            if kc >= 1:
                                emit_av(t, kc - 1)
                        s2_now = cnt["s2"]

                        @block.scalar
                        def _(s, kc=kc, qb=qb, s2_now=s2_now):
                            ping = kc % 4
                            if qb == 0 and kc == 0:
                                wait(s, "v", shp_neg[0])
                            wait(s, "s2", s2_now)
                            if slot_av[ping]:
                                s.wait_ge(sems["av"], slot_av[ping])
                            e_cnt[(qb, kc)] = inc("e", s.activation(
                                pbslot(ping), PSS[ping][:, :], AF.Exp,
                                bias=NEGSH[:, :], scale=SCALE), 1)
                            slot_e[ping] = e_cnt[(qb, kc)]

                        if kc >= 4 * qb:
                            @block.vector
                            def _(v, kc=kc, qb=qb):
                                wait(v, "e", e_cnt[(qb, kc)])
                                if qb == 0 and kc == 0:
                                    wait(v, "g", mask_ready[0])
                                dl = kc - 4 * qb
                                m_cnt[(qb, kc)] = inc("m", v.tensor_tensor(
                                    pbslot(kc % 4), pbslot(kc % 4),
                                    LN0[:, 512 * dl:512 * (dl + 1)], ALU.mult), 1)

                    @block.tensor
                    def _(t, qb=qb, nk=nk):
                        av_last_qb[0] = emit_av(t, nk - 1)

                    ao_now = [0]

                    @block.vector
                    def _(v, qb=qb, nk=nk):
                        wait(v, "av", av_last_qb[0])
                        wait(v, "t", aos_free[0])
                        wait(v, "ao", cnt["ao"])
                        for qs in range(4):
                            inc("v", v.reciprocal(RCP[:, qs:qs + 1], PS[2 + qs][:, 128:129]), 1)
                        wait(v, "v")
                        for qs in range(4):
                            last = v.tensor_scalar(AOS[:, 128 * qs:128 * (qs + 1)],
                                                   PS[2 + qs][:, 0:128], RCP[:, qs:qs + 1],
                                                   None, ALU.mult)
                            if qs < 3:
                                inc("v", last, 1)
                        ao_now[0] = inc("ao", last, 1)

                    @block.sync
                    def _(sy, qb=qb, a=a, b=b):
                        wait(sy, "ao", ao_now[0])
                        pp = qb % 2
                        sy.wait_ge(sems[aots_dma[pp][0]], aots_dma[pp][1])
                        for qs in range(4):
                            aos_free[0] = inc("t", sy.dma_start_transpose(
                                AOTS[:, 512 * pp + 128 * qs:512 * pp + 128 * (qs + 1)],
                                AOS[:, 128 * qs:128 * (qs + 1)]), 16)
                        wait(sy, "t", aos_free[0])
                        us = f"u{b}{pp}"
                        if b == 0:
                            dst = aot_in[0][128 * a:128 * (a + 1), 512 * qb:512 * (qb + 1)]
                        else:
                            dst = aot_in1[qb // 4][128 * a:128 * (a + 1),
                                                   512 * (qb % 4):512 * (qb % 4 + 1)]
                        aots_dma[pp] = (us, inc(us, sy.dma_start(
                            out=dst, in_=AOTS[:, 512 * pp:512 * pp + 512]), 16))

                    if b == 1 and a == 1 and qb == 3:
                        @block.gpsimd
                        def _(g):
                            wait(g, "u10", cnt["u10"])
                            wait(g, "u11", cnt["u11"])
                            ag2h1[0] = inc("c", g.collective_compute(
                                "AllGather", ALU.bypass,
                                replica_groups=[list(range(NCORES))],
                                ins=[aot_in1[0].ap().opt()],
                                outs=[aot_out1[0].ap().opt()]), 1)

                for bi in (0, 1, 6, 7):
                    bank_free[bi] = ("e", cnt["e"])
                for bi in (2, 3, 4, 5):
                    bank_free[bi] = ("ao", cnt["ao"])
                if a == 0 and mid_hook is not None:
                    mid_hook()
                    bank_free[0] = ("v", fin_ev[(3, 6)])
                    bank_free[1] = ("v", fin_ev[(3, 7)])

            attn_av_done[b] = cnt["av"]

            @block.gpsimd
            def _(g, b=b):
                wait(g, f"u{b}0", cnt[f"u{b}0"])
                wait(g, f"u{b}1", cnt[f"u{b}1"])
                if b == 0:
                    ag2[b] = inc("c", g.collective_compute(
                        "AllGather", ALU.bypass, replica_groups=[list(range(NCORES))],
                        ins=[aot_in[b].ap().opt()],
                        outs=[aot_out[b].ap().opt()]), 1)
                    g.wait_ge(sems["av"], attn_av_done[0])
                    g_v1[0] = inc("g", g.memset(V[:, :], 1.0), 1)
                else:
                    ag2h1[1] = inc("c", g.collective_compute(
                        "AllGather", ALU.bypass, replica_groups=[list(range(NCORES))],
                        ins=[aot_in1[1].ap().opt()],
                        outs=[aot_out1[1].ap().opt()]), 1)

        av_prev = {}

        def emit_qkv_taps(b):
            if not ({"qk", "vb"} & set(tap_h)):
                return

            @block.gpsimd
            def _(g, b=b):
                g.wait_ge(sems["v"], rope_half[b])
                if "qk" in tap_h:
                    inc("o2", g.dma_start(out=tap_h["qk"][:, 16384 * b:16384 * b + 8192],
                                         in_=Q[:, :]), 16)
                    inc("o2", g.dma_start(out=tap_h["qk"][:, 16384 * b + 8192:16384 * (b + 1)],
                                         in_=K[:, :]), 16)
                if "vb" in tap_h:
                    inc("o2", g.dma_start(out=tap_h["vb"][:, 8256 * b:8256 * (b + 1)],
                                         in_=V[:, :]), 16)

        def emit_attn_tap(b):
            if "attn" not in tap_h:
                return

            @block.gpsimd
            def _(g, b=b):
                if b == 0:
                    wait(g, "c", ag2[0])
                    inc("o2", g.dma_start(out=tap_h["attn"][:, S * b:S * (b + 1)],
                                          in_=aot_in[0][:, :]), 16)
                else:
                    wait(g, "c", ag2h1[1])
                    for h in range(2):
                        inc("o2", g.dma_start(
                            out=tap_h["attn"][:, S * b + 2048 * h:S * b + 2048 * (h + 1)],
                            in_=aot_in1[h][:, :]), 16)

        fin_pe_j = {}
        fin_pe = {}
        fin_ev = {}
        out_done = {}
        ft_done = {}
        asb = [BIGA, XSB0]
        xcb = [LN0, LN1]
        dsb = [XNJ, DS2B]
        tmbb = [TMB, PTB]

        def emit_final_tb(tb, mid):
            bb = tb // 4
            c4 = 1024 * (tb % 4)
            sbs = "sb" if tb % 2 == 0 else "sb2"
            xs = "x" if tb % 2 == 0 else "x2"
            xcb_t = LN1 if mid else xcb[tb % 2]
            tmb_t = TMB if mid else tmbb[tb % 2]

            @block.sync
            def _(sy, tb=tb, bb=bb, c4=c4, sbs=sbs):
                if bb == 0:
                    wait(sy, "c", ag2[0])
                    src = aot_out[0][:, c4:c4 + 1024]
                else:
                    wait(sy, "c", ag2h1[(tb % 4) // 2])
                    src = aot_out1[(tb % 4) // 2][:, 1024 * (tb % 2):1024 * (tb % 2) + 1024]
                if tb == 0:
                    wait(sy, "p", pe_sb_done[-1])
                if tb - 2 in fin_pe:
                    wait(sy, "p", fin_pe[tb - 2])
                inc(sbs, sy.dma_start(
                    out=asb[tb % 2][:, :].rearrange("p (i t) -> p i t", t=1024),
                    in_=src.rearrange("(i p) t -> p i t", p=128)), 16)
            f_ld = cnt[sbs]

            @block.scalar
            def _(s, tb=tb, xs=xs, mid=mid, xcb_t=xcb_t, tmb_t=tmb_t):
                if mid:
                    if tb == 0:
                        wait(s, "v", cnt["v"])
                        wait(s, "tm", tm_dma[-1])
                        wait(s, "ds", ds_dma[-1])
                    else:
                        wait(s, "v", fin_ev[(tb - 1, 7)])
                        wait(s, "p", fin_pe[tb - 1])
                else:
                    if tb in (4, 5):
                        wait(s, "m", cnt["m"])
                        s.wait_ge(sems["av"], attn_av_done[1])
                        wait(s, "e", cnt["e"])
                        wait(s, "tm", tm_dma[-1])
                        wait(s, "ds", ds_dma[-1])
                    if (tb - 2, 7) in fin_ev:
                        wait(s, "v", fin_ev[(tb - 2, 7)])
                    if tb - 2 in fin_pe:
                        wait(s, "p", fin_pe[tb - 2])
                inc(xs, s.dma_start(out=xcb_t[:, :].rearrange("p (j o) -> p j o", o=FS),
                                    in_=xcol[1024 * tb:1024 * (tb + 1), :]
                                    .rearrange("(j p) o -> p j o", p=128)), 16)
                inc(xs, s.dma_start(out=dsb[tb % 2][:, :].rearrange("p (j o) -> p j o", o=FS),
                                    in_=dsc[1024 * tb:1024 * (tb + 1), :]
                                    .rearrange("(j p) o -> p j o", p=128)), 16)
                inc(xs, s.dma_start(out=tmb_t[:, :],
                                    in_=tmd[:, 1024 * tb:1024 * (tb + 1)]), 16)
            f_x = cnt[xs]

            for j in range(8):
                @block.tensor
                def _(t, tb=tb, f_ld=f_ld, f_x=f_x, sbs=sbs, xs=xs, j=j, tmb_t=tmb_t):
                    if j == 0:
                        wait(t, sbs, f_ld)
                        wait(t, xs, f_x)
                        if tb == 0:
                            wait_banks(t, [0, 1])
                            wait(t, "w", wot_ld)
                    ps = PS[j % 2]
                    if (tb, j - 2) in fin_ev:
                        wait(t, "v", fin_ev[(tb, j - 2)])
                    elif (tb - 1, j + 6) in fin_ev:
                        wait(t, "v", fin_ev[(tb - 1, j + 6)])
                    cur = asb[tb % 2]
                    for i in range(16):
                        t.matmul(ps[:, 0:256],
                                 cur[:, 1024 * i + 128 * j:1024 * i + 128 * (j + 1)],
                                 WQT[:, 256 * i:256 * (i + 1)], start=(i == 0), stop=False)
                    t.matmul(ps[:, 0:256], tmb_t[:, 128 * j:128 * (j + 1)], BM[:, :],
                             start=False, stop=False)
                    fin_pe_j[(tb, j)] = inc("p", t.matmul(ps[:, 0:256], ONES1[:, :], BOR[:, :],
                                                          start=False, stop=True), 1)

                @block.vector
                def _(v, tb=tb, j=j, xcb_t=xcb_t):
                    ps = PS[j % 2]
                    wait(v, "p", fin_pe_j[(tb, j)])
                    ft = FT[:, 256 * (j % 2):256 * (j % 2) + 256]
                    if (tb, j - 2) in ft_done:
                        wait(v, "ft", ft_done[(tb, j - 2)])
                    elif (tb - 1, j + 6) in ft_done:
                        wait(v, "ft", ft_done[(tb - 1, j + 6)])
                    inc("v", v.tensor_tensor(ft, ps[:, 0:256],
                                             dsb[tb % 2][:, 256 * j:256 * (j + 1)], ALU.add), 1)
                    wait(v, "v")
                    fin_ev[(tb, j)] = inc("v", v.tensor_tensor(
                        ft, ft, xcb_t[:, 256 * j:256 * (j + 1)], ALU.add), 1)

                osem = "o" if j % 2 == 0 else "o3"

                @block.sync
                def _(sy, tb=tb, j=j, osem=osem):
                    wait(sy, "v", fin_ev[(tb, j)])
                    out_done[(tb, j)] = (osem, inc(osem, sy.dma_start(
                        out=out_h[1024 * tb + 128 * j:1024 * tb + 128 * (j + 1), :],
                        in_=FT[:, 256 * (j % 2):256 * (j % 2) + 256]), 16))

                @block.gpsimd
                def _(g, tb=tb, j=j):
                    g.wait_ge(sems[out_done[(tb, j)][0]], out_done[(tb, j)][1])
                    ft_done[(tb, j)] = inc("ft", g.nop(), 1)
            fin_pe[tb] = cnt["p"]


        if slvl >= 1:
            emit_phase2_half(0)
            emit_qkv_taps(0)
        if slvl >= 2:
            emit_attention_half(0)
            emit_attn_tap(0)
        if slvl >= 3:
            emit_phase2_half(1)
            emit_qkv_taps(1)

        if slvl >= 3:
            @block.gpsimd
            def _(g):
                g.wait_ge(sems["p"], pe_sb_done[-1])
                inc("w", g.dma_start(out=WQT[:, :].rearrange("p (i f) -> p i f", f=FS),
                                     in_=wot_h[:, :].rearrange("(i p) f -> p i f", p=128)), 16)
            wot_ld = cnt["w"]
        if slvl >= 4:
            mid = None
            if slvl >= 5:
                def mid():
                    for tb in range(4):
                        emit_final_tb(tb, True)
            emit_attention_half(1, mid_hook=mid)
            emit_attn_tap(1)

        if {"lg", "cws", "cwt", "cwx"} & set(tap_h):
            @block.gpsimd
            def _(g):
                g.wait_ge(sems["v"], cnt["v"])
                for nm, buf in (("lg", LG), ("cws", CWSd[1]), ("cwt", CWT), ("cwx", CWXS)):
                    if nm in tap_h:
                        inc("o2", g.dma_start(out=tap_h[nm][:, :], in_=buf[:, :]), 16)

        if {"dense", "tmask"} & set(tap_h):
            @block.gpsimd
            def _(g):
                wait(g, "tm", cnt["tm"])
                wait(g, "ds", cnt["ds"])
                if "dense" in tap_h:
                    inc("o2", g.dma_start(out=tap_h["dense"][:, :], in_=dsc[:, :]), 16)
                if "tmask" in tap_h:
                    inc("o2", g.dma_start(out=tap_h["tmask"][:, :], in_=tmd[:, :]), 16)

        if slvl >= 5:
            for tb in range(4, 8):
                emit_final_tb(tb, False)

        @block.sync
        def _(sy):
            wait(sy, "o", cnt["o"])
            wait(sy, "o3", cnt["o3"])

        if cnt["o2"]:
            @block.gpsimd
            def _(g):
                wait(g, "o2", cnt["o2"])

        block.flush(nc)

    return nc


_PERM = np.concatenate([np.arange(0, 128, 2), np.arange(1, 128, 2)])


def _stage(core, ins):
    c = core
    Fc = slice(256 * c, 256 * (c + 1))
    Tc = slice(1024 * c, 1024 * (c + 1))
    xf = np.asarray(ins["L0_attn_x"], dtype=np.float32).reshape(T, E)

    def permrows(w):
        w2 = w.reshape(2, 128, E)[:, _PERM, :]
        return w2.reshape(256, E)

    wq = permrows(np.asarray(ins["L0_attn_Wq"], np.float32)[Fc])
    wk = permrows(np.asarray(ins["L0_attn_Wk"], np.float32)[Fc])
    wv = np.asarray(ins["L0_attn_Wv"], np.float32)[Fc]
    wo = np.asarray(ins["L0_attn_Wo"], np.float32)[Fc]
    wd = np.asarray(ins["L0_ffn_weight"], np.float32)[Fc]
    bq = np.asarray(ins["L0_attn_bq"], np.float32)[Fc].reshape(2, 128)[:, _PERM].T
    bk = np.asarray(ins["L0_attn_bk"], np.float32)[Fc].reshape(2, 128)[:, _PERM].T
    bv = np.asarray(ins["L0_attn_bv"], np.float32)[Fc]
    bo = np.asarray(ins["L0_attn_bo"], np.float32)[Fc].reshape(1, FS)
    aw = np.asarray(ins["L0_ffn_A_w"], np.float32).reshape(128, E)
    gw = np.asarray(ins["L0_ffn_gate_w"], np.float32)
    bmw = np.asarray(ins["L0_ffn_B_w"], np.float32)[:, Fc, :]
    bm = bmw.transpose(0, 2, 1).reshape(128, FS)

    inv = 10000.0 ** (-np.arange(64, dtype=np.float64) / 64.0)
    ang = inv[:, None] * np.arange(S, dtype=np.float64)[None, :]
    cosv = np.cos(ang).astype(np.float32)
    sinv = np.sin(ang).astype(np.float32)

    return {
        "xtok": np.ascontiguousarray(xf[Tc]),
        "xcol": np.ascontiguousarray(xf[:, Fc]),
        "wqt": np.ascontiguousarray(wq.T),
        "wkt": np.ascontiguousarray(wk.T),
        "wvdt": np.ascontiguousarray(np.concatenate([wv.T, wd.T], axis=1)),
        "wot": np.ascontiguousarray(wo.T),
        "awt": np.ascontiguousarray(aw.T),
        "gwt": np.ascontiguousarray(
            np.concatenate([gw.T, np.zeros((E, 8), np.float32)], axis=1)),
        "bq": np.ascontiguousarray(bq),
        "bk": np.ascontiguousarray(bk),
        "bvd": np.concatenate([bv, np.zeros(256, np.float32)]).reshape(1, 512),
        "bo": np.ascontiguousarray(bo),
        "bm": np.ascontiguousarray(bm),
        "cos2": np.ascontiguousarray(np.concatenate([cosv, cosv], axis=0)),
        "sinpm": np.ascontiguousarray(np.concatenate([-sinv, sinv], axis=0)),
    }


def kernel(**inputs):
    nc = build_nc()
    in_maps = [_stage(c, inputs) for c in range(NCORES)]
    res = run_bass_kernel_spmd(nc, in_maps, core_ids=list(range(NCORES)))
    out = np.concatenate([np.asarray(res.results[c]["out"]) for c in range(NCORES)], axis=1)
    return out.reshape(B, S, E).astype(np.float32)
